# version 10
# baseline (speedup 1.0000x reference)
import numpy as np
import ml_dtypes
from contextlib import ExitStack

import concourse.bass as bass
import concourse.mybir as mybir
from concourse.bass_utils import run_bass_kernel_spmd
from concourse.bass import IndirectOffsetOnAxis

F32 = mybir.dt.float32
BF16 = mybir.dt.bfloat16
U32 = mybir.dt.uint32
AF = mybir.ActivationFunctionType
ALU = mybir.AluOpType
AX = mybir.AxisListType.X

P = 128
D = 4096
S = 4096
NT = 32
NJ = 8
TOK = 1024
H = 8
G = 4
QH = 32
DH = 128
CB = 16
NBLK = 256
LBLK = 64
CDIM = 2048
EXP = 64
SCALE = float(DH) ** -0.5
EPS = 1e-5
NEG = -3.0e38
QCOLS = 512

_PERM = np.concatenate([np.arange(0, DH, 2), np.arange(1, DH, 2)])


class Prog:
    QUEUES = ("sync", "gp")
    ENGINES = ("sync", "gp", "pe", "act", "dve")
    NQSEM = 8

    def __init__(self, nc, ctx):
        self.nc = nc
        self.sems = {e: ctx.enter_context(nc.semaphore("sem_" + e))
                     for e in ("pe", "act", "dve")}
        self.qsems = {q: [ctx.enter_context(nc.semaphore(f"sem_{q}{i}"))
                          for i in range(self.NQSEM)] for q in self.QUEUES}
        self.quse = {q: [0] * self.NQSEM for q in self.QUEUES}
        self.qseq = {q: 0 for q in self.QUEUES}
        self.count = {e: 0 for e in ("pe", "act", "dve")}
        self.stream = {e: [] for e in self.ENGINES}
        self.lastw = {}
        self.readers = {}
        self.waited = {e: {} for e in self.ENGINES}

    def _need(self, eng, token):
        dep_eng, sem, val = token
        if eng == dep_eng == "pe":
            return
        if eng == dep_eng and eng not in self.QUEUES:
            pass
        sid = sem.num
        if val <= self.waited[eng].get(sid, 0):
            return
        self.waited[eng][sid] = val
        self.stream[eng].append(lambda e, s=sem, v=val: e.wait_ge(s, v))

    def _deps(self, eng, reads, writes):
        for k in reads:
            if k in self.lastw:
                self._need(eng, self.lastw[k])
        for k in writes:
            if k in self.lastw:
                self._need(eng, self.lastw[k])
            for tok in self.readers.get(k, {}).values():
                self._need(eng, tok)

    def _record(self, token, reads, writes):
        for k in writes:
            self.lastw[k] = token
            self.readers[k] = {}
        for k in reads:
            self.readers.setdefault(k, {})[token[1].num] = token

    def op(self, eng, fn, reads=(), writes=()):
        assert eng not in self.QUEUES
        self._deps(eng, reads, writes)
        self.count[eng] += 1
        cnt = self.count[eng]
        sem = self.sems[eng]
        self.stream[eng].append(lambda e, f=fn, s=sem: f(e).then_inc(s, 1))
        self._record((eng, sem, cnt), reads, writes)
        return cnt

    def qop(self, q, fn, reads=(), writes=()):
        self._deps(q, reads, writes)
        i = self.qseq[q] % self.NQSEM
        self.qseq[q] += 1
        sem = self.qsems[q][i]
        if self.quse[q][i] > 0:
            self._need(q, (q, sem, 16 * self.quse[q][i]))
        self.quse[q][i] += 1
        val = 16 * self.quse[q][i]
        self.stream[q].append(
            lambda e, f=fn, sm=sem: f(e).then_inc(sm, 16))
        self._record((q, sem, val), reads, writes)
        return val

    def dma(self, q, out, in_, reads=(), writes=(), **kw):
        return self.qop(q, lambda e, o=out, s=in_, kw=kw:
                        e.dma_start(out=o, in_=s, **kw), reads, writes)

    def _alltokens(self):
        toks = [(e, self.sems[e], self.count[e]) for e in ("pe", "act", "dve")
                if self.count[e] > 0]
        for q in self.QUEUES:
            for i in range(self.NQSEM):
                if self.quse[q][i] > 0:
                    toks.append((q, self.qsems[q][i], 16 * self.quse[q][i]))
        return toks

    def barrier(self):
        toks = self._alltokens()
        for e in self.ENGINES:
            for tok in toks:
                if tok[0] != e or e in self.QUEUES:
                    self._need(e, tok)

    def finalize(self, block):
        self.barrier()
        for i in range(self.NQSEM):
            for q in self.QUEUES:
                if self.quse[q][i] > 0:
                    self._need("sync", (("x",), self.qsems[q][i],
                                        16 * self.quse[q][i])[0:3]
                               if False else
                               ("_", self.qsems[q][i], 16 * self.quse[q][i]))

        def runner(items):
            def go(e):
                for f in items:
                    f(e)
            return go

        block.sync(runner(self.stream["sync"]))
        block.gpsimd(runner(self.stream["gp"]))
        block.tensor(runner(self.stream["pe"]))
        block.scalar(runner(self.stream["act"]))
        block.vector(runner(self.stream["dve"]))


def _rope_tables():
    inv = 1.0 / (10000.0 ** (np.arange(0, DH, 2, dtype=np.float64) / DH))
    ang = np.arange(S, dtype=np.float64)[:, None] * inv[None, :]
    return (np.cos(ang).T.astype(np.float32), np.sin(ang).T.astype(np.float32))


def _own_positions(r):
    tiles = np.arange(NJ) * 4 + r
    return (tiles[:, None] * P + np.arange(P)[None, :]).reshape(-1)


def _prep_p12(inputs):
    x = np.asarray(inputs["L0_attn_hidden_states"])
    w_qkv = np.asarray(inputs["L0_attn_w_qkv"])
    k_w1 = np.asarray(inputs["L0_attn_k_w1"]); k_b1 = np.asarray(inputs["L0_attn_k_b1"])
    k_w2 = np.asarray(inputs["L0_attn_k_w2"]); k_b2 = np.asarray(inputs["L0_attn_k_b2"])
    v_w1 = np.asarray(inputs["L0_attn_v_w1"]); v_b1 = np.asarray(inputs["L0_attn_v_b1"])
    v_w2 = np.asarray(inputs["L0_attn_v_w2"]); v_b2 = np.asarray(inputs["L0_attn_v_b2"])
    k_pos = np.asarray(inputs["L0_attn_k_pos"]); v_pos = np.asarray(inputs["L0_attn_v_pos"])
    w_gate = np.asarray(inputs["L0_attn_w_gate"]); b_gate = np.asarray(inputs["L0_attn_b_gate"])
    rw = np.asarray(inputs["L0_ffn_router_weight"]); rb = np.asarray(inputs["L0_ffn_router_bias"])
    eb = np.asarray(inputs["L0_ffn_expert_bias"])

    qk_perm = np.arange(6144)
    for qh in range(QH):
        qk_perm[qh * DH:(qh + 1) * DH] = qh * DH + _PERM
    for kh in range(H):
        base = QH * DH + kh * DH
        qk_perm[base:base + DH] = base + _PERM
    wqkvT = np.ascontiguousarray(w_qkv[qk_perm].T)

    in_perm_k = (np.arange(CB)[:, None] * DH + _PERM[None, :]).reshape(-1)
    cos, sin = _rope_tables()
    shared = dict(
        wqkvT=wqkvT,
        kw1T=np.ascontiguousarray(k_w1[:, in_perm_k].T),
        vw1T=np.ascontiguousarray(v_w1.T),
        kw2T=np.ascontiguousarray(k_w2[_PERM].T),
        vw2T=np.ascontiguousarray(v_w2.T),
        kb1=np.ascontiguousarray(k_b1.reshape(16, 128).T),
        vb1=np.ascontiguousarray(v_b1.reshape(16, 128).T),
        kb2=k_b2[_PERM].reshape(DH, 1).copy(), vb2=v_b2.reshape(DH, 1).copy(),
        kposT=np.ascontiguousarray(k_pos[:, :, _PERM].transpose(2, 0, 1).reshape(DH, H * CB)),
        vposT=np.ascontiguousarray(v_pos.transpose(2, 0, 1).reshape(DH, H * CB)),
        wgateT=np.ascontiguousarray(w_gate.T), bgate=b_gate.reshape(96, 1).copy(),
        rwT=np.ascontiguousarray(rw.T), rbias=rb.reshape(EXP, 1).copy(), ebias=eb,
        ident=np.eye(P, dtype=np.float32), identb=np.eye(P, dtype=ml_dtypes.bfloat16),
    )
    maps = []
    for c in range(8):
        b, r = c // 4, c % 4
        pos = _own_positions(r)
        m = dict(shared)
        m.update(xT=np.ascontiguousarray(x[b][pos].T),
                 cosT=np.ascontiguousarray(cos[:, pos]),
                 sinT=np.ascontiguousarray(sin[:, pos]))
        maps.append(m)
    return maps


def _build_p12():
    nc = bass.Bass("TRN2", target_bir_lowering=False)
    d_ = lambda n, sh, dt=F32, k="ExternalInput": nc.dram_tensor(n, sh, dt, kind=k).ap()
    xT = d_("xT", [D, TOK])
    wqkvT = d_("wqkvT", [D, 6144])
    kw1T = d_("kw1T", [CDIM, CDIM]); vw1T = d_("vw1T", [CDIM, CDIM])
    kw2T = d_("kw2T", [CDIM, DH]); vw2T = d_("vw2T", [CDIM, DH])
    kb1 = d_("kb1", [P, 16]); vb1 = d_("vb1", [P, 16])
    kb2 = d_("kb2", [DH, 1]); vb2 = d_("vb2", [DH, 1])
    kposT = d_("kposT", [DH, H * CB]); vposT = d_("vposT", [DH, H * CB])
    wgateT = d_("wgateT", [D, 96]); bgate = d_("bgate", [96, 1])
    rwT = d_("rwT", [D, EXP]); rbias = d_("rbias", [EXP, 1]); ebias = d_("ebias", [EXP, D])
    ident = d_("ident", [P, P]); identb = d_("identb", [P, P], BF16)
    cosT = d_("cosT", [64, TOK]); sinT = d_("sinT", [64, TOK])

    o_x1 = d_("o_x1", [D, TOK], F32, "ExternalOutput")
    o_q = d_("o_q", [H, DH, G * TOK], BF16, "ExternalOutput")
    o_rq = d_("o_rq", [H, DH, G * TOK], BF16, "ExternalOutput")
    o_qsum = d_("o_qsum", [H, DH, TOK], BF16, "ExternalOutput")
    o_gates = d_("o_gates", [96, TOK], F32, "ExternalOutput")
    o_rk = d_("o_rk", [H, DH, TOK], BF16, "ExternalOutput")
    o_ktok = d_("o_ktok", [TOK, H * DH], BF16, "ExternalOutput")
    o_vtok = d_("o_vtok", [TOK, H * DH], BF16, "ExternalOutput")
    o_ck = d_("o_ck", [H, LBLK, DH], BF16, "ExternalOutput")
    o_cv = d_("o_cv", [H, LBLK, DH], BF16, "ExternalOutput")

    with ExitStack() as ctx:
        sb_ = lambda n, sh, dt=F32: ctx.enter_context(nc.sbuf_tensor(n, sh, dt))
        s_y16 = sb_("s_y16", [P, NT * TOK], BF16)
        s_rw = sb_("s_rw", [P, NT * EXP], BF16)
        s_eb = sb_("s_eb", [EXP, D], BF16)
        s_wg = sb_("s_wg", [P, NT * 96], BF16)
        s_id = sb_("s_id", [P, P]); s_idb = sb_("s_idb", [P, P], BF16)
        s_ones1 = sb_("s_ones1", [1, P]); s_onesP = sb_("s_onesP", [P, 1])
        s_rb = sb_("s_rb", [EXP, 1]); s_bg = sb_("s_bg", [96, 1])
        s_kb1 = sb_("s_kb1", [P, 16]); s_vb1 = sb_("s_vb1", [P, 16])
        s_kb2 = sb_("s_kb2", [DH, 1]); s_vb2 = sb_("s_vb2", [DH, 1])
        s_kpos = sb_("s_kpos", [DH, H * CB]); s_vpos = sb_("s_vpos", [DH, H * CB])
        s_cos = sb_("s_cos", [P, TOK]); s_sin = sb_("s_sin", [P, TOK])
        s_eps = sb_("s_eps", [1, 1])

        ps_ = lambda n: ctx.enter_context(nc.psum_tensor(n, [P, 512], F32))
        pb = [ps_(f"pb{i}") for i in range(8)]

        pr = Prog(nc, ctx)
        block = ctx.enter_context(nc.Block())

        for dst, src, key in [
            (s_rb[:], rbias, "rb"), (s_bg[:], bgate, "bg"),
            (s_kb1[:], kb1, "kb1"), (s_vb1[:], vb1, "vb1"),
            (s_kb2[:], kb2, "kb2"), (s_vb2[:], vb2, "vb2"),
            (s_kpos[:], kposT, "kpos"), (s_vpos[:], vposT, "vpos"),
            (s_cos[0:64, :], cosT, "cos"), (s_sin[0:64, :], sinT, "sin"),
            (s_cos[64:128, :], cosT, "cos"), (s_sin[64:128, :], sinT, "sin"),
            (s_id[:], ident, "id"), (s_idb[:], identb, "idb"),
        ]:
            pr.dma("sync", dst, src, writes=(key,))
        pr.dma("gp", s_rw[:].rearrange("p (k e) -> p k e", e=EXP),
               rwT.rearrange("(k p) e -> p k e", p=P), writes=("rw",))
        pr.dma("gp", s_eb[:], ebias, writes=("eb",))
        pr.dma("gp", s_wg[:].rearrange("p (k e) -> p k e", e=96),
               wgateT.rearrange("(k p) e -> p k e", p=P), writes=("wg",))
        pr.op("dve", lambda e: e.memset(s_ones1[:], 1.0), writes=("ones1",))
        pr.op("dve", lambda e: e.memset(s_eps[:], EPS), writes=("eps",))
        pr.op("dve", lambda e: e.memset(s_onesP[:], 1.0), writes=("onesP",))

        xTr = xT.rearrange("(k p) t -> p k t", p=P)
        x1r = o_x1.rearrange("(k p) t -> p k t", p=P)

        with ExitStack() as c1:
            sb1 = lambda n, sh, dt=F32: c1.enter_context(nc.sbuf_tensor(n, sh, dt))
            s_x = sb1("s_x", [P, D])
            s_sq = sb1("s_sq", [P, D])
            s_y32 = sb1("s_y32", [P, D])
            s_y16r = sb1("s_y16r", [P, D], BF16)
            s_x1 = sb1("s_x1", [P, D])
            s_stat = sb1("s_stat", [1, 512])
            s_rep = sb1("s_rep", [P, 1, 2 * P])
            s_lg = sb1("s_lg", [EXP, P])
            s_e = sb1("s_e", [P, EXP])
            s_col = sb1("s_col", [P, 8])
            s_m8 = sb1("s_m8", [P, 8])
            s_zap = sb1("s_zap", [P, EXP])
            s_pm = sb1("s_pm", [P, EXP])
            s_pms = sb1("s_pms", [P, EXP], BF16)
            s_ws = sb1("s_ws", [P, 1])
            s_pmT = sb1("s_pmT", [EXP, P], BF16)
            s_wsrow = sb1("s_wsrow", [1, P])
            s_wsr = sb1("s_wsr", [P, 1, P])

            def layer_norm(src_key, src_sb, out16, out16_key):
                pr.op("act", lambda e: e.activation(s_sq[:], src_sb[:], AF.Square),
                      reads=(src_key,), writes=("sq",))
                def sums(e):
                    for k in range(NT):
                        i = e.matmul(pb[0][0:1, 0:P], s_onesP[:],
                                     src_sb[:, k * P:(k + 1) * P],
                                     start=(k == 0), stop=(k == NT - 1))
                    return i
                pr.op("pe", sums, reads=(src_key, "onesP"), writes=("pb0",))
                def sqsums(e):
                    for k in range(NT):
                        i = e.matmul(pb[0][0:1, P:2 * P], s_onesP[:],
                                     s_sq[:, k * P:(k + 1) * P],
                                     start=(k == 0), stop=(k == NT - 1))
                    return i
                pr.op("pe", sqsums, reads=("sq", "onesP"), writes=("pb0",))
                pr.op("dve", lambda e: e.tensor_scalar(
                    s_stat[0:1, 0:P], pb[0][0:1, 0:P], 1.0 / D, None, ALU.mult),
                    reads=("pb0",), writes=("stat",))
                pr.op("dve", lambda e: e.tensor_tensor(
                    s_stat[0:1, P:2 * P], s_stat[0:1, 0:P], s_stat[0:1, 0:P],
                    ALU.mult), reads=("stat",), writes=("stat",))
                pr.op("dve", lambda e: e.scalar_tensor_tensor(
                    s_stat[0:1, 2 * P:3 * P], pb[0][0:1, P:2 * P], 1.0 / D,
                    s_stat[0:1, P:2 * P], ALU.mult, ALU.subtract),
                    reads=("pb0", "stat"), writes=("stat",))
                pr.op("act", lambda e: e.activation(
                    s_stat[0:1, 2 * P:3 * P], s_stat[0:1, 2 * P:3 * P], AF.Sqrt,
                    bias=s_eps[0:1, 0:1], scale=1.0),
                    reads=("stat", "eps"), writes=("stat",))
                pr.op("dve", lambda e: e.reciprocal(
                    s_stat[0:1, 3 * P:4 * P], s_stat[0:1, 2 * P:3 * P]),
                    reads=("stat",), writes=("stat",))
                def bcast(e):
                    e.matmul(pb[0][:, 0:P], s_ones1[:], s_stat[0:1, 0:P],
                             start=True, stop=True)
                    return e.matmul(pb[0][:, P:2 * P], s_ones1[:],
                                    s_stat[0:1, 3 * P:4 * P], start=True, stop=True)
                pr.op("pe", bcast, reads=("stat", "ones1"), writes=("pb0",))
                pr.op("dve", lambda e: e.tensor_copy(s_rep[:, 0, :], pb[0][:, 0:2 * P]),
                      reads=("pb0",), writes=("rep",))
                pr.op("dve", lambda e: e.tensor_tensor(
                    s_y32[:].rearrange("p (k t) -> p k t", k=NT),
                    src_sb[:].rearrange("p (k t) -> p k t", k=NT),
                    s_rep[:, :, 0:P].to_broadcast([P, NT, P]), ALU.subtract),
                    reads=(src_key, "rep"), writes=("y32",))
                pr.op("dve", lambda e: e.tensor_tensor(
                    s_y32[:].rearrange("p (k t) -> p k t", k=NT),
                    s_y32[:].rearrange("p (k t) -> p k t", k=NT),
                    s_rep[:, :, P:2 * P].to_broadcast([P, NT, P]), ALU.mult),
                    reads=("rep",), writes=("y32",))
                src3 = s_y32[:].rearrange("p (k t) -> p k t", t=P) \
                    if len(out16.shape) == 3 else s_y32[:]
                pr.op("dve", lambda e, src3=src3: e.tensor_copy(out16, src3),
                      reads=("y32",), writes=(out16_key,))

            for j in range(NJ):
                pr.dma("sync", s_x[:].rearrange("p (k t) -> p k t", t=P),
                       xTr[:, :, j * P:(j + 1) * P], writes=("x",))
                layer_norm("x", s_x, s_y16r[:], "y16r")
                def logits(e):
                    for k in range(NT):
                        i = e.matmul(pb[0][0:EXP, 2 * P:3 * P],
                                     s_rw[:, k * EXP:(k + 1) * EXP],
                                     s_y16r[:, k * P:(k + 1) * P],
                                     start=(k == 0), stop=(k == NT - 1))
                    return i
                pr.op("pe", logits, reads=("y16r", "rw"), writes=("pb0",))
                pr.op("act", lambda e: e.activation(
                    s_lg[:], pb[0][0:EXP, 2 * P:3 * P], AF.Identity,
                    bias=s_rb[:, 0:1], scale=1.0),
                    reads=("pb0", "rb"), writes=("lg",))
                pr.op("pe", lambda e: e.matmul(
                    pb[0][0:P, 3 * P:3 * P + EXP], s_lg[:], s_id[0:EXP, 0:EXP],
                    start=True, stop=True), reads=("lg", "id"), writes=("pb0",))
                pr.op("dve", lambda e: e.tensor_reduce(
                    s_col[:, 0:1], pb[0][0:P, 3 * P:3 * P + EXP], AX, ALU.max,
                    negate=True), reads=("pb0",), writes=("col",))
                pr.op("act", lambda e: e.activation(
                    s_e[:], pb[0][0:P, 3 * P:3 * P + EXP], AF.Exp,
                    bias=s_col[:, 0:1], scale=1.0),
                    reads=("pb0", "col"), writes=("e",))
                pr.op("dve", lambda e: e.tensor_reduce(
                    s_col[:, 1:2], s_e[:], AX, ALU.add), reads=("e",), writes=("col",))
                pr.op("dve", lambda e: e.max(out=s_m8[:], in_=s_e[:]),
                      reads=("e",), writes=("m8",))
                pr.op("dve", lambda e: e.match_replace(
                    out=s_zap[:], in_to_replace=s_m8[:], in_values=s_e[:],
                    imm_value=0.0), reads=("m8", "e"), writes=("zap", "m8"))
                pr.op("dve", lambda e: e.tensor_tensor(
                    s_pm[:], s_e[:], s_zap[:], ALU.subtract),
                    reads=("e", "zap"), writes=("pm",))
                pr.op("dve", lambda e: e.tensor_reduce(
                    s_col[:, 2:3], s_pm[:], AX, ALU.add), reads=("pm",), writes=("col",))
                pr.op("dve", lambda e: e.reciprocal(s_col[:, 3:4], s_col[:, 1:2]),
                      reads=("col",), writes=("col",))
                pr.op("dve", lambda e: e.scalar_tensor_tensor(
                    s_ws[:], s_col[:, 2:3], 0.5, s_col[:, 3:4], ALU.mult, ALU.mult),
                    reads=("col",), writes=("ws",))
                pr.op("dve", lambda e: e.tensor_scalar(
                    s_col[:, 4:5], s_col[:, 3:4], 0.5, None, ALU.mult),
                    reads=("col",), writes=("col",))
                pr.op("dve", lambda e: e.tensor_scalar(
                    s_pms[:], s_pm[:], s_col[:, 4:5], None, ALU.mult),
                    reads=("pm", "col"), writes=("pms",))
                pr.op("pe", lambda e: e.matmul(
                    pb[7][0:EXP, 0:P], s_pms[:], s_idb[:], start=True, stop=True),
                    reads=("pms", "idb"), writes=("pb7",))
                pr.op("dve", lambda e: e.tensor_copy(s_pmT[:], pb[7][0:EXP, 0:P]),
                      reads=("pb7",), writes=("pmT",))
                pr.op("pe", lambda e: e.matmul(
                    pb[7][0:1, 0:P], s_ws[:], s_id[:], start=True, stop=True),
                    reads=("ws", "id"), writes=("pb7",))
                pr.op("dve", lambda e: e.tensor_copy(s_wsrow[:], pb[7][0:1, 0:P]),
                      reads=("pb7",), writes=("wsrow",))
                pr.op("pe", lambda e: e.matmul(
                    pb[0][:, 0:P], s_ones1[:], s_wsrow[:], start=True, stop=True),
                    reads=("wsrow", "ones1"), writes=("pb0",))
                pr.op("dve", lambda e: e.tensor_copy(s_wsr[:, 0, :], pb[0][:, 0:P]),
                      reads=("pb0",), writes=("wsr",))
                pr.op("dve", lambda e: e.tensor_tensor(
                    s_sq[:].rearrange("p (k t) -> p k t", k=NT),
                    s_y32[:].rearrange("p (k t) -> p k t", k=NT),
                    s_wsr[:].to_broadcast([P, NT, P]), ALU.mult),
                    reads=("y32", "wsr"), writes=("sq",))
                pr.op("dve", lambda e: e.tensor_tensor(
                    s_x1[:], s_x[:], s_sq[:], ALU.add),
                    reads=("x", "sq"), writes=("x1",))
                for k in range(NT):
                    bank = 1 + (k % 2)
                    pr.op("pe", lambda e, k=k, bank=bank: e.matmul(
                        pb[bank][0:P, 0:P], s_eb[:, k * P:(k + 1) * P], s_pmT[:],
                        start=True, stop=True),
                        reads=("eb", "pmT"), writes=(f"pb{bank}",))
                    pr.op("dve", lambda e, k=k, bank=bank: e.tensor_tensor(
                        s_x1[:, k * P:(k + 1) * P], s_x1[:, k * P:(k + 1) * P],
                        pb[bank][0:P, 0:P], ALU.add),
                        reads=(f"pb{bank}", "x1"), writes=("x1",))
                pr.dma("sync", x1r[:, :, j * P:(j + 1) * P],
                       s_x1[:].rearrange("p (k t) -> p k t", t=P), reads=("x1",))
                y16j = s_y16[:].rearrange("p (k j t) -> p k j t", j=NJ, t=P)[:, :, j, :]
                layer_norm("x1", s_x1, y16j, ("y16", j))

            pr.barrier()

        y16keys = tuple(("y16", j) for j in range(NJ))

        with ExitStack() as c2:
            sb2 = lambda n, sh, dt=F32: c2.enter_context(nc.sbuf_tensor(n, sh, dt))
            s_w = [sb2(f"s_w{i}", [P, NT * P], BF16) for i in range(2)]
            s_qkv = sb2("s_qkv", [P, TOK], BF16)
            s_rope = sb2("s_rope", [P, TOK], BF16)
            s_ksc = sb2("s_ksc", [P, TOK], BF16)
            s_qs1 = sb2("s_qs1", [P, TOK], BF16)
            s_qs2 = sb2("s_qs2", [P, TOK], BF16)
            s_tr = sb2("s_tr", [P, TOK], BF16)
            s_cin = sb2("s_cin", [P, 2 * H * TOK], BF16)
            s_h1 = sb2("s_h1", [P, 16 * 512], BF16)
            s_co = sb2("s_co", [P, 512], BF16)
            s_gt = sb2("s_gt", [96, TOK])

            wqr = wqkvT.rearrange("(k p) c -> p k c", p=P)
            ktokr = o_ktok.rearrange("(j p) c -> p j c", p=P)
            vtokr = o_vtok.rearrange("(j p) c -> p j c", p=P)

            def qkv_coltile(ct):
                wbuf = s_w[ct % 2]
                pr.dma("gp", wbuf[:].rearrange("p (k c) -> p k c", c=P),
                       wqr[:, :, ct * P:(ct + 1) * P], writes=(f"wbuf{ct % 2}",))
                b0, b1 = (3, 4) if ct % 2 == 0 else (5, 6)
                def mms(e, b0=b0, b1=b1, wbuf=wbuf):
                    for k in range(NT):
                        e.matmul(pb[b0][:, 0:512], wbuf[:, k * P:(k + 1) * P],
                                 s_y16[:, k * TOK:k * TOK + 512],
                                 start=(k == 0), stop=(k == NT - 1))
                    for k in range(NT):
                        i = e.matmul(pb[b1][:, 0:512], wbuf[:, k * P:(k + 1) * P],
                                     s_y16[:, k * TOK + 512:(k + 1) * TOK],
                                     start=(k == 0), stop=(k == NT - 1))
                    return i
                pr.op("pe", mms, reads=(f"wbuf{ct % 2}",) + y16keys,
                      writes=(f"pb{b0}", f"pb{b1}"))
                return b0, b1

            def rope(src, src_key, dst, dst_key):
                x0 = src[0:64, :]; x1 = src[64:128, :]
                pr.op("dve", lambda e: e.tensor_tensor(
                    dst[0:64, :], x0, s_cos[0:64, :], ALU.mult),
                    reads=(src_key, "cos"), writes=(dst_key,))
                pr.op("dve", lambda e: e.tensor_tensor(
                    s_tr[0:64, :], x1, s_sin[64:128, :], ALU.mult),
                    reads=(src_key, "sin"), writes=("trs",))
                pr.op("dve", lambda e: e.tensor_tensor(
                    dst[0:64, :], dst[0:64, :], s_tr[0:64, :], ALU.subtract),
                    reads=(dst_key, "trs"), writes=(dst_key,))
                pr.op("dve", lambda e: e.tensor_tensor(
                    dst[64:128, :], x1, s_cos[64:128, :], ALU.mult),
                    reads=(src_key, "cos"), writes=(dst_key,))
                pr.op("dve", lambda e: e.tensor_tensor(
                    s_tr[64:128, :], x0, s_sin[0:64, :], ALU.mult),
                    reads=(src_key, "sin"), writes=("trs",))
                pr.op("dve", lambda e: e.tensor_tensor(
                    dst[64:128, :], dst[64:128, :], s_tr[64:128, :], ALU.add),
                    reads=(dst_key, "trs"), writes=(dst_key,))

            for ct in range(QH):
                h, g = ct // 4, ct % 4
                b0, b1 = qkv_coltile(ct)
                def ev(e, b0=b0, b1=b1):
                    e.activation(s_qkv[:, 0:512], pb[b0][:, 0:512], AF.Identity,
                                 scale=SCALE)
                    return e.activation(s_qkv[:, 512:1024], pb[b1][:, 0:512],
                                        AF.Identity, scale=SCALE)
                pr.op("act", ev, reads=(f"pb{b0}", f"pb{b1}"), writes=("qkv",))
                pr.dma("sync", o_q[h, :, g * TOK:(g + 1) * TOK], s_qkv[:],
                       reads=("qkv",))
                rope(s_qkv, "qkv", s_rope, "rope")
                pr.dma("sync", o_rq[h, :, g * TOK:(g + 1) * TOK], s_rope[:],
                       reads=("rope",))
                if g == 0:
                    pr.op("dve", lambda e: e.tensor_copy(s_qs1[:], s_qkv[:]),
                          reads=("qkv",), writes=("qs1",))
                elif g == 1:
                    pr.op("dve", lambda e: e.tensor_tensor(
                        s_qs1[:], s_qs1[:], s_qkv[:], ALU.add),
                        reads=("qkv", "qs1"), writes=("qs1",))
                elif g == 2:
                    pr.op("dve", lambda e: e.tensor_copy(s_qs2[:], s_qkv[:]),
                          reads=("qkv",), writes=("qs2",))
                else:
                    pr.op("dve", lambda e: e.tensor_tensor(
                        s_qs2[:], s_qs2[:], s_qkv[:], ALU.add),
                        reads=("qkv", "qs2"), writes=("qs2",))
                    pr.op("dve", lambda e: e.tensor_tensor(
                        s_qs1[:], s_qs1[:], s_qs2[:], ALU.add),
                        reads=("qs1", "qs2"), writes=("qs1",))
                    pr.op("dve", lambda e: e.tensor_scalar(
                        s_qs1[:], s_qs1[:], 0.25, None, ALU.mult),
                        reads=("qs1",), writes=("qs1",))
                    pr.dma("sync", o_qsum[h, :, :], s_qs1[:], reads=("qs1",))

            for ct in range(QH, QH + H):
                h = ct - QH
                b0, b1 = qkv_coltile(ct)
                def evk(e, b0=b0, b1=b1):
                    e.activation(s_ksc[:, 0:512], pb[b0][:, 0:512], AF.Identity)
                    return e.activation(s_ksc[:, 512:1024], pb[b1][:, 0:512],
                                        AF.Identity)
                pr.op("act", evk, reads=(f"pb{b0}", f"pb{b1}"), writes=("ksc",))
                rope(s_ksc, "ksc", s_rope, "rope")
                pr.dma("sync", o_rk[h, :, :], s_rope[:], reads=("rope",))
                for jt in range(NJ):
                    pr.op("pe", lambda e, jt=jt: e.matmul(
                        pb[7][:, 0:P], s_rope[:, jt * P:(jt + 1) * P], s_idb[:],
                        start=True, stop=True),
                        reads=("rope", "idb"), writes=("pb7",))
                    pr.op("dve", lambda e, jt=jt: e.tensor_copy(
                        s_tr[:, jt * P:(jt + 1) * P], pb[7][:, 0:P]),
                        reads=("pb7",), writes=("trs",))
                pr.dma("sync", ktokr[:, :, h * DH:(h + 1) * DH],
                       s_tr[:].rearrange("p (j c) -> p j c", j=NJ), reads=("trs",))
                for cb in range(CB):
                    off = h * TOK + cb * LBLK
                    pr.op("dve", lambda e, cb=cb, off=off, h=h: e.tensor_scalar(
                        s_cin[:, off:off + LBLK],
                        s_ksc[:].rearrange("p (blk c) -> p blk c", c=CB)[:, :, cb:cb + 1]
                        .rearrange("p b c -> p (b c)"),
                        s_kpos[:, h * CB + cb:h * CB + cb + 1], None, ALU.add),
                        reads=("ksc", "kpos"), writes=(("cin", 0, h),))

            for ct in range(QH + H, QH + 2 * H):
                h = ct - QH - H
                b0, b1 = qkv_coltile(ct)
                def evv(e, b0=b0, b1=b1):
                    e.activation(s_qkv[:, 0:512], pb[b0][:, 0:512], AF.Identity)
                    return e.activation(s_qkv[:, 512:1024], pb[b1][:, 0:512],
                                        AF.Identity)
                pr.op("act", evv, reads=(f"pb{b0}", f"pb{b1}"), writes=("qkv",))
                for jt in range(NJ):
                    pr.op("pe", lambda e, jt=jt: e.matmul(
                        pb[7][:, 0:P], s_qkv[:, jt * P:(jt + 1) * P], s_idb[:],
                        start=True, stop=True),
                        reads=("qkv", "idb"), writes=("pb7",))
                    pr.op("dve", lambda e, jt=jt: e.tensor_copy(
                        s_tr[:, jt * P:(jt + 1) * P], pb[7][:, 0:P]),
                        reads=("pb7",), writes=("trs",))
                pr.dma("sync", vtokr[:, :, h * DH:(h + 1) * DH],
                       s_tr[:].rearrange("p (j c) -> p j c", j=NJ), reads=("trs",))
                for cb in range(CB):
                    off = (H + h) * TOK + cb * LBLK
                    pr.op("dve", lambda e, cb=cb, off=off, h=h: e.tensor_scalar(
                        s_cin[:, off:off + LBLK],
                        s_qkv[:].rearrange("p (blk c) -> p blk c", c=CB)[:, :, cb:cb + 1]
                        .rearrange("p b c -> p (b c)"),
                        s_vpos[:, h * CB + cb:h * CB + cb + 1], None, ALU.add),
                        reads=("qkv", "vpos"), writes=(("cin", 1, h),))

            ckr = o_ck.rearrange("h b d -> b h d")
            cvr = o_cv.rearrange("h b d -> b h d")
            for side, w1, w2, bias1, bias2, outr in (
                    (0, kw1T, kw2T, s_kb1, s_kb2, ckr),
                    (1, vw1T, vw2T, s_vb1, s_vb2, cvr)):
                cink = tuple(("cin", side, h) for h in range(H))
                w1r = w1.rearrange("(c p) o -> p c o", p=P)
                base = side * H * TOK
                rhs_ap = lambda c, side=side: s_cin[:].rearrange(
                    "p (s h cb blk) -> p s h cb blk", s=2, h=H, cb=CB) \
                    [:, side, :, c, :]
                for o in range(16):
                    wb = s_w[o % 2]
                    pr.dma("gp", wb[:, 0:CDIM].rearrange("p (c o) -> p c o", o=P),
                           w1r[:, :, o * P:(o + 1) * P], writes=(f"wbuf{o % 2}",))
                    bank = 3 + (o % 2)
                    def m1(e, o=o, wb=wb, bank=bank, rhs_ap=rhs_ap):
                        for c in range(16):
                            i = e.matmul(pb[bank][:, 0:512],
                                         wb[:, c * P:(c + 1) * P], rhs_ap(c),
                                         start=(c == 0), stop=(c == 15))
                        return i
                    pr.op("pe", m1, reads=(f"wbuf{o % 2}",) + cink,
                          writes=(f"pb{bank}",))
                    pr.op("act", lambda e, o=o, bank=bank, bias1=bias1: e.activation(
                        s_h1[:, o * 512:(o + 1) * 512], pb[bank][:, 0:512],
                        AF.Relu, bias=bias1[:, o:o + 1], scale=1.0),
                        reads=(f"pb{bank}", "kb1" if side == 0 else "vb1"),
                        writes=("h1",))
                w2r = w2.rearrange("(c p) o -> p c o", p=P)
                pr.dma("gp", s_w[0][:, 0:16 * DH].rearrange("p (c o) -> p c o", o=DH),
                       w2r, writes=("wbuf0",))
                def m2(e):
                    for c in range(16):
                        i = e.matmul(pb[5][:, 0:512],
                                     s_w[0][:, c * DH:(c + 1) * DH],
                                     s_h1[:, c * 512:(c + 1) * 512],
                                     start=(c == 0), stop=(c == 15))
                    return i
                pr.op("pe", m2, reads=("wbuf0", "h1"), writes=("pb5",))
                pr.op("act", lambda e, bias2=bias2: e.activation(
                    s_co[:], pb[5][:, 0:512], AF.Identity,
                    bias=bias2[:, 0:1], scale=1.0),
                    reads=("pb5", "kb2" if side == 0 else "vb2"), writes=("co",))
                for h in range(H):
                    pr.op("pe", lambda e, h=h: e.matmul(
                        pb[7][0:64, 0:P], s_co[:, h * 64:(h + 1) * 64], s_idb[:],
                        start=True, stop=True),
                        reads=("co", "idb"), writes=("pb7",))
                    pr.op("dve", lambda e, h=h: e.tensor_copy(
                        s_tr[0:64, h * DH:(h + 1) * DH], pb[7][0:64, 0:P]),
                        reads=("pb7",), writes=("trs",))
                pr.dma("sync", outr,
                       s_tr[0:64, 0:H * DH].rearrange("p (h d) -> p h d", h=H),
                       reads=("trs",))

            def gmm(e):
                for k in range(NT):
                    e.matmul(pb[1][0:96, 0:512], s_wg[:, k * 96:(k + 1) * 96],
                             s_y16[:, k * TOK:k * TOK + 512],
                             start=(k == 0), stop=(k == NT - 1))
                for k in range(NT):
                    i = e.matmul(pb[2][0:96, 0:512], s_wg[:, k * 96:(k + 1) * 96],
                                 s_y16[:, k * TOK + 512:(k + 1) * TOK],
                                 start=(k == 0), stop=(k == NT - 1))
                return i
            pr.op("pe", gmm, reads=("wg",) + y16keys, writes=("pb1", "pb2"))
            def gev(e):
                e.activation(s_gt[:, 0:512], pb[1][0:96, 0:512], AF.Sigmoid,
                             bias=s_bg[:, 0:1], scale=1.0)
                return e.activation(s_gt[:, 512:1024], pb[2][0:96, 0:512],
                                    AF.Sigmoid, bias=s_bg[:, 0:1], scale=1.0)
            pr.op("act", gev, reads=("pb1", "pb2", "bg"), writes=("gt",))
            pr.dma("sync", o_gates, s_gt[:], reads=("gt",))

            pr.finalize(block)
    return nc


P12_OUTS = ["o_x1", "o_q", "o_rq", "o_qsum", "o_gates", "o_rk", "o_ktok",
            "o_vtok", "o_ck", "o_cv"]




def _prep_p34(inputs, res1, cores=tuple(range(8))):
    mem_kv = np.asarray(inputs["L0_attn_mem_kv"])
    w_o = np.asarray(inputs["L0_attn_w_o"])
    rw = np.asarray(inputs["L0_ffn_router_weight"]); rb = np.asarray(inputs["L0_ffn_router_bias"])
    eb = np.asarray(inputs["L0_ffn_expert_bias"])
    bf = ml_dtypes.bfloat16

    onehot = np.full((512, NBLK), NEG, dtype=bf)
    for jb in range(NBLK):
        onehot[jb + 1, jb] = 0.0
    E9 = np.zeros((9, P), dtype=bf)
    for kk in range(P):
        E9[kk // CB, kk] = 1.0
    E9[8, :] = 1.0
    e3 = np.zeros((P, 9), dtype=bf)
    for bb in range(3):
        e3[:, 3 * bb + bb] = 1.0
    e3w = np.zeros((3, 3 * P), dtype=bf)
    for bb in range(3):
        e3w[bb, bb * P:(bb + 1) * P] = 1.0
    gsel = np.zeros((96, 96), dtype=np.float32)
    for qh in range(QH):
        for bb in range(3):
            gsel[3 * qh + bb, qh * 3 + bb] = 1.0
    qloc = np.arange(P)
    wloc = np.arange(P)
    od = np.where((wloc[:, None] >= (qloc[None, :] // CB) * CB)
                  & (wloc[:, None] <= qloc[None, :]), 0.0, NEG).astype(np.float32)
    owndiag = np.tile(od, (1, 4)).reshape(P, 4 * P)
    owndiag = np.concatenate([od] * 4, axis=1).astype(bf)

    shared = dict(
        onehot=onehot, E9=E9, e3=e3, e3w=e3w, gsel=gsel, owndiag=owndiag,
        woT=np.ascontiguousarray(w_o.T),
        rwT=np.ascontiguousarray(rw.T), rbias=rb.reshape(EXP, 1).copy(), ebias=eb,
        ident=np.eye(P, dtype=np.float32), identb=np.eye(P, dtype=bf),
        memkP=np.ascontiguousarray(mem_kv[0, :, 0, :][:, _PERM].T),
        memv=np.ascontiguousarray(mem_kv[1, :, 0, :].reshape(1, H * DH)),
    )

    maps = {}
    for c in cores:
        b, r = c // 4, c % 4
        rkTf = np.zeros((H * DH, S), dtype=bf)
        vtok = np.zeros((S, H * DH), dtype=bf)
        ktok = np.zeros((S, H * DH), dtype=bf)
        ck_full = np.zeros((H, NBLK, DH), dtype=bf)
        cv_full = np.zeros((H, NBLK, DH), dtype=bf)
        for rr in range(4):
            rsrc = res1[4 * b + rr]
            pos = _own_positions(rr)
            rkTf[:, pos] = rsrc["o_rk"].reshape(H * DH, TOK)
            vtok[pos] = rsrc["o_vtok"]
            ktok[pos] = rsrc["o_ktok"]
            oblk = ((np.arange(NJ)[:, None] * 4 + rr) * NJ + np.arange(NJ)[None, :]).reshape(-1)
            ck_full[:, oblk] = rsrc["o_ck"]
            cv_full[:, oblk] = rsrc["o_cv"]
        ckTin = np.zeros((H * DH, 384), dtype=bf)
        for h in range(H):
            ckTin[h * DH:(h + 1) * DH, 1:NBLK + 1] = ck_full[h].T
        cvin = np.zeros((384, H * DH), dtype=bf)
        for h in range(H):
            cvin[1:NBLK + 1, h * DH:(h + 1) * DH] = cv_full[h]
        wk = np.zeros((NJ, 2 * P, H * DH), dtype=bf)
        wv = np.zeros((NJ, 2 * P, H * DH), dtype=bf)
        for j in range(NJ):
            T = 4 * j + r
            lo = T * P - P
            for w in range(2 * P):
                gw = lo + w
                if gw >= 0:
                    wk[j, w] = ktok[gw]
                    wv[j, w] = vtok[gw]
        cmask = np.zeros((NJ, 2, P, 4 * P), dtype=bf)
        impmask = np.zeros((NJ, P, 384), dtype=bf)
        swmask = np.zeros((NJ, 2, P, 4 * P), dtype=bf)
        for j in range(NJ):
            T = 4 * j + r
            qabs = T * P + np.arange(P)
            col = np.arange(384)
            ck_seq = np.where(col == 0, -1, 16 * col - 1)
            allow = (ck_seq[:, None] < qabs[None, :]) & (col[:, None] <= NBLK)
            cm = np.where(allow, 0.0, NEG).astype(np.float32)
            cmask[j] = np.tile(cm.reshape(3, P, 1, P), (1, 1, 4, 1)).reshape(3, P, 4 * P)[:2]
            col = np.arange(384)
            iallow = (col[None, :] >= 1) & (col[None, :] <= NBLK) \
                & (16 * col[None, :] - 1 < qabs[:, None])
            im = np.where(iallow, 0.0, NEG).astype(np.float32)
            im[:, NBLK + 1] = -1000.0
            impmask[j] = im
            kw = T * P - P + np.arange(2 * P)
            sallow = ((kw[:, None] >= qabs[None, :] - 64)
                      & (kw[:, None] <= qabs[None, :]) & (kw[:, None] >= 0))
            sm = np.where(sallow, 0.0, NEG).astype(np.float32)
            swmask[j] = np.tile(sm.reshape(2, P, 1, P), (1, 1, 4, 1)).reshape(2, P, 4 * P)

        rsrc = res1[c]
        m = dict(shared)
        m.update(
            x1T=rsrc["o_x1"], qT=rsrc["o_q"], rqT=rsrc["o_rq"],
            qsumT=rsrc["o_qsum"], gatesT=rsrc["o_gates"],
            rkTf=rkTf, vtokf=vtok, ckTin=ckTin, cvin=cvin,
            wk=np.ascontiguousarray(wk.reshape(NJ * 2 * P, H * DH)),
            wv=np.ascontiguousarray(wv.reshape(NJ * 2 * P, H * DH)),
            cmask=cmask, impmask=impmask, swmask=swmask,
        )
        maps[c] = m
    return maps


def _build_p34():
    nc = bass.Bass("TRN2", target_bir_lowering=False)
    d_ = lambda n, sh, dt=F32, k="ExternalInput": nc.dram_tensor(n, sh, dt, kind=k).ap()
    x1T = d_("x1T", [D, TOK])
    qT = d_("qT", [H, DH, G * TOK], BF16)
    rqT = d_("rqT", [H, DH, G * TOK], BF16)
    qsumT = d_("qsumT", [H, DH, TOK], BF16)
    gatesT = d_("gatesT", [96, TOK])
    rkTf = d_("rkTf", [H * DH, S], BF16)
    vtokf = d_("vtokf", [S, H * DH], BF16)
    ckTin = d_("ckTin", [H * DH, 384], BF16)
    cvin = d_("cvin", [384, H * DH], BF16)
    wk = d_("wk", [NJ * 2 * P, H * DH], BF16)
    wv = d_("wv", [NJ * 2 * P, H * DH], BF16)
    cmask = d_("cmask", [NJ, 2, P, 4 * P], BF16)
    impmask = d_("impmask", [NJ, P, 384], BF16)
    swmask = d_("swmask", [NJ, 2, P, 4 * P], BF16)
    owndiag = d_("owndiag", [P, 4 * P], BF16)
    onehot = d_("onehot", [512, NBLK], BF16)
    E9in = d_("E9", [9, P], BF16)
    e3in = d_("e3", [P, 9], BF16)
    e3w = d_("e3w", [3, 3 * P], BF16)
    gselin = d_("gsel", [96, 96])
    woT = d_("woT", [D, D])
    rwT = d_("rwT", [D, EXP]); rbias = d_("rbias", [EXP, 1]); ebias = d_("ebias", [EXP, D])
    ident = d_("ident", [P, P]); identb = d_("identb", [P, P], BF16)
    memkP = d_("memkP", [DH, H]); memv = d_("memv", [1, H * DH])

    attd = nc.dram_tensor("attd", [QH, DH, TOK], BF16, kind="Internal").ap()
    m01d = nc.dram_tensor("m01d", [2 * P, P], BF16, kind="Internal").ap()
    x2d = nc.dram_tensor("x2d", [D, TOK], F32, kind="Internal").ap()
    o_out = d_("o_out", [D, TOK], F32, "ExternalOutput")

    with ExitStack() as ctx:
        sb_ = lambda n, sh, dt=F32: ctx.enter_context(nc.sbuf_tensor(n, sh, dt))
        s_id = sb_("s_id", [P, P]); s_idb = sb_("s_idb", [P, P], BF16)
        s_ones1 = sb_("s_ones1", [1, P]); s_onesP = sb_("s_onesP", [P, 1])
        s_onesPb = sb_("s_onesPb", [P, 1], BF16)
        s_eps = sb_("s_eps", [1, 1])
        s_E9 = sb_("s_E9", [9, P], BF16)
        s_e3 = sb_("s_e3", [P, 9], BF16)
        s_e3w = sb_("s_e3w", [35, 3 * P], BF16)
        s_gsel = sb_("s_gsel", [96, 96])
        s_od = sb_("s_od", [P, 4 * P], BF16)
        s_gates = sb_("s_gates", [96, TOK])
        s_maxk = sb_("s_maxk", [1, 16])
        s_ones1b = sb_("s_ones1b", [1, P], BF16)
        s_rb = sb_("s_rb", [EXP, 1])

        pA = ctx.enter_context(nc.psum_tensor("pA", [P, 1024], F32))
        pB = ctx.enter_context(nc.psum_tensor("pB", [P, 1024], F32))
        pc = [ctx.enter_context(nc.psum_tensor(f"pc{i}", [P, 512], F32))
              for i in range(4)]
        pSc = lambda t: pA[:, (t % 2) * 512:(t % 2) * 512 + 512]
        pM = pB[:, 0:512]
        pL = pB[:, 512:1024]
        pLg = pB[32:35, 512:1024]
        pC, pF, pW, pI = pc[0], pc[1], pc[2], pc[3]

        pr = Prog(nc, ctx)
        block = ctx.enter_context(nc.Block())

        for dst, src, key in [
            (s_id[:], ident, "id"), (s_idb[:], identb, "idb"),
            (s_E9[:], E9in, "E9"), (s_e3[:], e3in, "e3"),
            (s_e3w[32:35, :], e3w, "e3w"), (s_gsel[:], gselin, "gsel"),
            (s_od[:], owndiag, "od"), (s_gates[:], gatesT, "gates"),
            (s_rb[:], rbias, "rb"),
        ]:
            pr.dma("sync", dst, src, writes=(key,))
        pr.op("dve", lambda e: e.memset(s_ones1[:], 1.0), writes=("ones1",))
        pr.op("dve", lambda e: e.memset(s_onesP[:], 1.0), writes=("onesP",))
        pr.op("dve", lambda e: e.memset(s_onesPb[:], 1.0), writes=("onesPb",))
        pr.op("dve", lambda e: e.memset(s_ones1b[:], 1.0), writes=("ones1b",))
        pr.op("dve", lambda e: e.memset(s_eps[:], EPS), writes=("eps",))

        with ExitStack() as c1:
            sb1 = lambda n, sh, dt=F32: c1.enter_context(nc.sbuf_tensor(n, sh, dt))
            s_rk = sb1("s_rk", [P, H * S], BF16)
            s_vt = sb1("s_vt", [P, NT * H * DH], BF16)
            s_ck = sb1("s_ck", [P, H * 384], BF16)
            s_cv = sb1("s_cv", [P, 3 * H * DH], BF16)
            s_sq = sb1("s_sq", [P, 512], BF16)
            s_qg = [sb1(f"s_qg{i}", [P, QCOLS], BF16) for i in range(2)]
            s_rqg = [sb1(f"s_rqg{i}", [P, QCOLS], BF16) for i in range(2)]
            s_qsj = sb1("s_qsj", [P, P], BF16)
            s_cm = sb1("s_cm", [P, 2 * 512], BF16)
            s_im = [sb1(f"s_im{i}", [P, 384], BF16) for i in range(2)]
            s_sm = sb1("s_sm", [P, 2 * 512], BF16)
            s_wk = sb1("s_wk", [P, H * DH], BF16)
            s_wv = sb1("s_wv", [P, 2 * H * DH], BF16)
            s_wkT = sb1("s_wkT", [P, H * 2 * DH], BF16)
            s_pc16 = [sb1(f"s_pc16_{i}", [P, 512], BF16) for i in range(2)]
            s_pf16 = [sb1(f"s_pf16_{i}", [P, 512], BF16) for i in range(2)]
            s_ps16 = sb1("s_ps16", [P, 2 * 512], BF16)
            s_po16 = sb1("s_po16", [P, 512], BF16)
            s_ei = sb1("s_ei", [P, 384])
            s_col = sb1("s_col", [P, 8])
            s_m8v = sb1("s_m8v", [P, 8])
            s_m8i = sb1("s_m8i", [P, 8], U32)
            s_sel = sb1("s_sel", [P, 4], U32)
            s_selv = sb1("s_selv", [P, 4], U32)
            s_mg = [sb1(f"s_mg{i}", [P, NBLK], BF16) for i in range(4)]
            s_mS = s_mg[0]
            s_m01T = s_mg[1]
            s_m01r = sb1("s_m01r", [8, NT * P], BF16)
            s_mb = [sb1(f"s_mb{i}", [9, 512], BF16) for i in range(2)]
            s_nrow = sb1("s_nrow", [1, 512])
            s_nmf = sb1("s_nmf", [1, 512])
            s_nmf16 = [sb1(f"s_nmf16_{i}", [1, 512], BF16) for i in range(2)]
            s_nmc16 = [sb1(f"s_nmc16_{i}", [1, 512], BF16) for i in range(2)]
            s_rl = sb1("s_rl", [35, 512])
            s_Wb = sb1("s_Wb", [35, 512], BF16)
            s_wrep = sb1("s_wrep", [P, 512], BF16)
            s_evc = sb1("s_evc", [P, 3 * 512], BF16)
            s_att = sb1("s_att", [P, 512], BF16)
            s_att16 = sb1("s_att16", [P, 512], BF16)
            s_row = sb1("s_row", [1, 8])

            for h in range(H):
                pr.dma("sync", s_rk[:, h * S:(h + 1) * S],
                       rkTf[h * DH:(h + 1) * DH, :], writes=(("rk", h),))
            pr.dma("sync", s_vt[:].rearrange("p (t c) -> p t c", c=H * DH),
                   vtokf.rearrange("(t p) c -> p t c", p=P), writes=("vt",))
            pr.dma("sync", s_ck[:].rearrange("p (h c) -> p h c", c=384),
                   ckTin.rearrange("(h p) c -> p h c", p=P), writes=("ckt",))
            pr.dma("sync", s_cv[:].rearrange("p (u c) -> p u c", c=H * DH),
                   cvin.rearrange("(u p) c -> p u c", p=P), writes=("cvt",))
            pr.dma("gp", s_ck[:].rearrange("p (h c) -> p h c", c=384)[:, :, 0:1],
                   memkP.rearrange("p (h o) -> p h o", o=1),
                   reads=("ckt",), writes=("ckt",),
                   allow_slow_non_contiguous=True)
            pr.dma("gp", s_cv[0:1, 0:H * DH], memv, reads=("cvt",), writes=("cvt",))

            pr.op("dve", lambda e: e.memset(s_maxk[:], 0.0), writes=("maxk",))
            def norm_pass(src_ap, width, colidx, key):
                for base in range(0, width, 512):
                    w = min(512, width - base)
                    pr.op("act", lambda e, base=base, w=w: e.activation(
                        s_sq[:, 0:w], src_ap[:, base:base + w], AF.Square),
                        reads=(key,), writes=("sq",))
                    for cb2 in range(0, w, 512):
                        cw = min(512, w - cb2)
                        pr.op("pe", lambda e, cb2=cb2, cw=cw: e.matmul(
                            pI[0:1, 0:cw], s_onesPb[:],
                            s_sq[:, cb2:cb2 + cw], start=True, stop=True),
                            reads=("sq", "onesPb"), writes=("pI",))
                        pr.op("dve", lambda e, cw=cw: e.tensor_reduce(
                            s_row[0:1, 0:1], pI[0:1, 0:cw], AX, ALU.max),
                            reads=("pI",), writes=("rowx",))
                        pr.op("dve", lambda e, colidx=colidx: e.tensor_tensor(
                            s_maxk[0:1, colidx:colidx + 1],
                            s_maxk[0:1, colidx:colidx + 1],
                            s_row[0:1, 0:1], ALU.max),
                            reads=("rowx", "maxk"), writes=("maxk",))
            for h in range(H):
                norm_pass(s_rk[:, h * S:(h + 1) * S], S, h, ("rk", h))
                norm_pass(s_ck[:, h * 384:(h + 1) * 384], 384, 8 + h, "ckt")

            units = [(j, h) for j in range(NJ) for h in range(H)]

            def j_boundary(j):
                pr.dma("sync", s_cm[:].rearrange("p (t c) -> p t c", c=512),
                       cmask[j].rearrange("t p c -> p t c"), writes=("cmj",))
                pr.dma("sync", s_sm[:].rearrange("p (t c) -> p t c", c=512),
                       swmask[j].rearrange("u p c -> p u c"), writes=("smj",))
                pr.dma("sync", s_wv[:].rearrange("p (u c) -> p u c", c=H * DH),
                       wv.rearrange("(j u p) c -> j p u c", j=NJ, p=P)[j],
                       writes=("wvbuf",))
                for u in range(2):
                    pr.dma("sync", s_wk[:],
                           wk.rearrange("(j u p) c -> j u p c", j=NJ, p=P)[j, u],
                           writes=("wkbuf",))
                    for h in range(H):
                        pr.op("pe", lambda e, h=h: e.matmul(
                            pM[:, 0:P], s_wk[:, h * DH:(h + 1) * DH],
                            s_idb[:], start=True, stop=True),
                            reads=("wkbuf", "idb"), writes=("pM",))
                        pr.op("dve", lambda e, h=h, u=u: e.tensor_copy(
                            s_wkT[:, (h * 2 + u) * DH:(h * 2 + u + 1) * DH],
                            pM[:, 0:P]), reads=("pM",), writes=("wkT",))

            def sel_stage(j, h):
                if h == 0:
                    pr.dma("sync", s_im[j % 2][:], impmask[j],
                           writes=(f"imj{j % 2}",))
                pr.dma("sync", s_qsj[:], qsumT[h, :, j * P:(j + 1) * P],
                       writes=("qsj",))
                pr.op("pe", lambda e, h=h: e.matmul(
                    pW[:, 0:384], s_qsj[:], s_ck[:, h * 384:(h + 1) * 384],
                    start=True, stop=True, skip_group_check=True),
                    reads=("qsj", "ckt"), writes=("pW",))
                pr.op("dve", lambda e, j=j: e.tensor_tensor(
                    pW[:, 0:384], pW[:, 0:384], s_im[j % 2][:], ALU.add),
                    reads=(f"imj{j % 2}", "pW"), writes=("pW",))
                pr.op("dve", lambda e: e.tensor_reduce(
                    s_col[:, 0:1], pW[:, 0:384], AX, ALU.max, negate=True),
                    reads=("pW",), writes=("col",))
                pr.op("act", lambda e: e.activation(
                    s_ei[:], pW[:, 0:384], AF.Exp, bias=s_col[:, 0:1],
                    scale=1.0), reads=("pW", "col"), writes=("ei",))
                pr.op("dve", lambda e: e.tensor_reduce(
                    s_col[:, 1:2], s_ei[:], AX, ALU.add),
                    reads=("ei",), writes=("col",))
                pr.op("dve", lambda e: e.max(out=s_m8v[:], in_=s_ei[:]),
                      reads=("ei",), writes=("m8v",))
                pr.op("dve", lambda e: e.max_index(
                    out=s_m8i[:], in_max=s_m8v[:], in_values=s_ei[:]),
                    reads=("m8v", "ei"), writes=("m8i",))
                pr.op("dve", lambda e: e.tensor_scalar(
                    s_col[:, 2:3], s_col[:, 1:2], 1e-10, None, ALU.mult),
                    reads=("col",), writes=("col",))
                pr.op("dve", lambda e: e.tensor_scalar(
                    s_selv[:], s_m8v[:, 0:4], s_col[:, 2:3], None, ALU.is_gt),
                    reads=("m8v", "col"), writes=("selv",))
                pr.op("dve", lambda e: e.memset(s_sel[:], 0), writes=("sel",))
                pr.op("dve", lambda e: e.copy_predicated(
                    s_sel[:], s_selv[:], s_m8i[:, 0:4]),
                    reads=("selv", "m8i", "sel"), writes=("sel",))
                for ss in range(4):
                    pr.qop("gp", lambda e, ss=ss: e.indirect_dma_start(
                        s_mg[ss][:], None, onehot,
                        IndirectOffsetOnAxis(s_sel[:, ss:ss + 1], 0)),
                        reads=("sel",), writes=(f"mg{ss}",))

            def sel_stage_b():
                pr.op("dve", lambda e: e.tensor_tensor(
                    s_mS[:], s_mg[0][:], s_mg[1][:], ALU.max),
                    reads=("mg0", "mg1"), writes=("mg0",))
                pr.op("dve", lambda e: e.tensor_tensor(
                    s_mS[:], s_mS[:], s_mg[2][:], ALU.max),
                    reads=("mg0", "mg2"), writes=("mg0",))
                pr.op("dve", lambda e: e.tensor_tensor(
                    s_mS[:], s_mS[:], s_mg[3][:], ALU.max),
                    reads=("mg0", "mg3"), writes=("mg0",))
                for u in range(2):
                    pr.op("pe", lambda e, u=u: e.matmul(
                        pI[:, 0:P], s_mS[:, u * P:(u + 1) * P], s_idb[:],
                        start=True, stop=True),
                        reads=("mg0", "idb"), writes=("pI",))
                    pr.op("dve", lambda e, u=u: e.tensor_copy(
                        s_m01T[:, u * P:(u + 1) * P], pI[:, 0:P]),
                        reads=("pI",), writes=("mg1",))
                pr.dma("sync", m01d.rearrange("(u p) q -> p u q", p=P),
                       s_m01T[:].rearrange("p (u q) -> p u q", q=P),
                       reads=("mg1",), writes=("m01d",))

            def m01r_load():
                pr.dma("sync",
                       s_m01r[0:8, :].rearrange("r (t q) -> r t q", q=P),
                       m01d.rearrange("(t r) q -> r t q", r=8),
                       reads=("m01d",), writes=("m01r",))

            def load_negm(ui):
                jn, hn = units[ui]
                sl = ui % 2
                pr.dma("sync", s_qg[sl][:].rearrange("p (g q) -> p g q", q=P),
                       qT[hn].rearrange("p (g t) -> p g t", t=TOK)
                       [:, :, jn * P:(jn + 1) * P], writes=(f"qg{sl}",))
                pr.dma("sync", s_rqg[sl][:].rearrange("p (g q) -> p g q", q=P),
                       rqT[hn].rearrange("p (g t) -> p g t", t=TOK)
                       [:, :, jn * P:(jn + 1) * P], writes=(f"rqg{sl}",))

                def negm_row(src, srckey, col, dst16, dstkey):
                    pr.op("dve", lambda e, src=src: e.tensor_tensor(
                        s_sq[:, 0:512], src, src, ALU.mult),
                        reads=(srckey,), writes=("sq",))
                    pr.op("pe", lambda e: e.matmul(
                        pM[0:1, 0:512], s_onesPb[:], s_sq[:, 0:512],
                        start=True, stop=True, skip_group_check=True),
                        reads=("sq", "onesPb"), writes=("pM",))
                    pr.op("dve", lambda e, col=col: e.tensor_scalar(
                        s_nrow[:], pM[0:1, 0:512],
                        s_maxk[0:1, col:col + 1], None, ALU.mult),
                        reads=("pM", "maxk"), writes=("nrow",))
                    pr.op("act", lambda e: e.activation(
                        s_nrow[:], s_nrow[:], AF.Sqrt),
                        reads=("nrow",), writes=("nrow",))
                    pr.op("dve", lambda e: e.tensor_scalar(
                        s_nmf[:], s_nrow[:], -1.0001, None, ALU.mult),
                        reads=("nrow",), writes=("nmf",))
                    pr.op("dve", lambda e, dst16=dst16: e.tensor_copy(
                        dst16, s_nmf[:]), reads=("nmf",), writes=(dstkey,))
                negm_row(s_rqg[sl][:], f"rqg{sl}", hn,
                         s_nmf16[sl][:], f"nmf16_{sl}")
                negm_row(s_qg[sl][:], f"qg{sl}", 8 + hn,
                         s_nmc16[sl][:], f"nmc16_{sl}")

            def main_stage(idx, j, h):
                NTD = 4 * j + 4
                sl = idx % 2
                for i in range(2):
                    pr.dma("gp", s_mb[i][8:9, :], s_nmf16[sl][:],
                           reads=(f"nmf16_{sl}",), writes=(f"mb{i}",))

                lfirst = [True]
                pend = []

                def flush_pend():
                    while pend:
                        pend.pop(0)()

                def lpv(probkey, prob_ap, e3c, pvb, pvkey, vkey, v_ap,
                        lstop=False, pvstart=False, pvstop=False):
                    def emit(probkey=probkey, prob_ap=prob_ap, e3c=e3c,
                             pvb=pvb, pvkey=pvkey, vkey=vkey, v_ap=v_ap,
                             lstop=lstop, pvstart=pvstart, pvstop=pvstop):
                        st = lfirst[0]
                        lfirst[0] = False
                        def mm(e, st=st):
                            e.matmul(pL[0:3, :], s_e3[:, e3c:e3c + 3],
                                     prob_ap, start=st, stop=lstop,
                                     skip_group_check=True)
                            return e.matmul(pvb, v_ap, prob_ap,
                                            start=pvstart, stop=pvstop)
                        pr.op("pe", mm, reads=(probkey, "e3", vkey),
                              writes=("pL", pvkey))
                    return emit

                CT = 1 if j <= 3 else 2
                for t in range(CT):
                    def cmm(e, t=t, h=h):
                        e.matmul(pSc(t),
                                 s_ck[:, h * 384 + t * P:h * 384 + (t + 1) * P],
                                 s_qg[sl][:], start=True, stop=False)
                        return e.matmul(pSc(t), s_ones1b[:], s_nmc16[sl][:],
                                        start=False, stop=True)
                    pr.op("pe", cmm,
                          reads=("ckt", f"qg{sl}", "ones1b", f"nmc16_{sl}"),
                          writes=(f"pS{t % 2}",))
                    pr.op("dve", lambda e, t=t: e.tensor_tensor(
                        pSc(t), pSc(t), s_cm[:, t * 512:(t + 1) * 512], ALU.add),
                        reads=("cmj", f"pS{t % 2}"), writes=(f"pS{t % 2}",))
                    pr.op("act", lambda e, t=t: e.activation(
                        s_pc16[t % 2][:], pSc(t), AF.Exp),
                        reads=(f"pS{t % 2}",), writes=(f"pc{t % 2}",))
                    flush_pend()
                    pend.append(lpv(
                        f"pc{t % 2}", s_pc16[t % 2][:], 0, pC[:, :], "pC",
                        "cvt", s_cv[:, t * H * DH + h * DH:
                                    t * H * DH + (h + 1) * DH],
                        pvstart=(t == 0), pvstop=(t == CT - 1)))

                if idx + 1 < len(units):
                    sel_stage(*units[idx + 1])

                for t in range(NTD):
                    pr.op("pe", lambda e, t=t, h=h: e.matmul(
                        pSc(t), s_rk[:, h * S + t * P:h * S + (t + 1) * P],
                        s_rqg[sl][:], start=True, stop=False),
                        reads=(("rk", h), f"rqg{sl}"), writes=(f"pS{t % 2}",))
                    pr.op("dve", lambda e, t=t: e.tensor_copy(
                        s_mb[t % 2][0:8, :].rearrange("r (g q) -> r g q", q=P),
                        s_m01r[0:8, :].rearrange("r (t q) -> r t q", q=P)
                        [:, t:t + 1, :].to_broadcast([8, 4, P])),
                        reads=("m01r",), writes=(f"mb{t % 2}",))
                    pr.op("pe", lambda e, t=t: e.matmul(
                        pSc(t), s_E9[:], s_mb[t % 2][:],
                        start=False, stop=True),
                        reads=(f"mb{t % 2}", "E9"), writes=(f"pS{t % 2}",))
                    pr.op("act", lambda e, t=t: e.activation(
                        s_pf16[t % 2][:], pSc(t), AF.Exp),
                        reads=(f"pS{t % 2}",), writes=(f"pf{t % 2}",))
                    flush_pend()
                    pend.append(lpv(
                        f"pf{t % 2}", s_pf16[t % 2][:], 3, pF[:, :], "pF",
                        "vt", s_vt[:, t * H * DH + h * DH:
                                   t * H * DH + (h + 1) * DH],
                        pvstart=(t == 0)))
                if epi1_pend:
                    epi1_pend.pop(0)()
                if idx + 1 < len(units):
                    load_negm(idx + 1)
                def omm(e, h=h):
                    e.matmul(pSc(0),
                             s_wkT[:, (h * 2 + 1) * DH:(h * 2 + 2) * DH],
                             s_rqg[sl][:], start=True, stop=False)
                    return e.matmul(pSc(0), s_ones1b[:], s_nmf16[sl][:],
                                    start=False, stop=True)
                pr.op("pe", omm,
                      reads=("wkT", f"rqg{sl}", "ones1b", f"nmf16_{sl}"),
                      writes=("pS0",))
                pr.op("dve", lambda e: e.tensor_tensor(
                    pSc(0), pSc(0), s_od[:], ALU.add),
                    reads=("od", "pS0"), writes=("pS0",))
                pr.op("act", lambda e: e.activation(
                    s_po16[:], pSc(0), AF.Exp),
                    reads=("pS0",), writes=("po16",))
                flush_pend()
                pend.append(lpv(
                    "po16", s_po16[:], 3, pF[:, :], "pF",
                    "wvbuf", s_wv[:, H * DH + h * DH:H * DH + (h + 1) * DH],
                    pvstop=True))
                for u in range(2):
                    def smm(e, u=u, h=h):
                        e.matmul(pSc(u),
                                 s_wkT[:, (h * 2 + u) * DH:(h * 2 + u + 1) * DH],
                                 s_rqg[sl][:], start=True, stop=False)
                        return e.matmul(pSc(u), s_ones1b[:], s_nmf16[sl][:],
                                        start=False, stop=True)
                    pr.op("pe", smm,
                          reads=("wkT", f"rqg{sl}", "ones1b", f"nmf16_{sl}"),
                          writes=(f"pS{u}",))
                    pr.op("dve", lambda e, u=u: e.tensor_tensor(
                        pSc(u), pSc(u), s_sm[:, u * 512:(u + 1) * 512], ALU.add),
                        reads=("smj", f"pS{u}"), writes=(f"pS{u}",))
                    pr.op("act", lambda e, u=u: e.activation(
                        s_ps16[:, u * 512:(u + 1) * 512], pSc(u), AF.Exp),
                        reads=(f"pS{u}",), writes=("ps16",))
                    flush_pend()
                    pend.append(lpv(
                        "ps16", s_ps16[:, u * 512:(u + 1) * 512], 6,
                        pW[:, :], "pW",
                        "wvbuf", s_wv[:, u * H * DH + h * DH:
                                      u * H * DH + (h + 1) * DH],
                        lstop=(u == 1), pvstart=(u == 0), pvstop=(u == 1)))
                if idx + 1 < len(units):
                    sel_stage_b()
                    m01r_load()
                if epi2_pend:
                    epi2_pend.pop(0)()
                for g in range(G):
                    pr.op("pe", lambda e, g=g, h=h, j=j: e.matmul(
                        pLg[:, g * P:(g + 1) * P],
                        s_gsel[:, (h * G + g) * 3:(h * G + g) * 3 + 3],
                        s_gates[:, j * P:(j + 1) * P],
                        start=(g == 0), stop=(g == 3),
                        skip_group_check=True),
                        reads=("gsel", "gates"), writes=("pLg",))
                flush_pend()
                pr.op("dve", lambda e: e.tensor_scalar(
                    s_rl[32:35, :], pL[0:3, :], 1e-37, None, ALU.add),
                    reads=("pL",), writes=("rl",))
                pr.op("act", lambda e: e.activation(
                    s_evc[:, 0:512], pC[:, :], AF.Identity),
                    reads=("pC",), writes=("evc",))
                pr.op("act", lambda e: e.activation(
                    s_evc[:, 512:1024], pF[:, :], AF.Identity),
                    reads=("pF",), writes=("evc",))
                pr.op("act", lambda e: e.activation(
                    s_evc[:, 1024:1536], pW[:, :], AF.Identity),
                    reads=("pW",), writes=("evc",))

                def epi1():
                    pr.op("dve", lambda e: e.reciprocal(
                        s_rl[32:35, :], s_rl[32:35, :]),
                        reads=("rl",), writes=("rl",))
                    pr.op("dve", lambda e: e.tensor_tensor(
                        s_Wb[32:35, :], pLg[:, 0:512], s_rl[32:35, :],
                        ALU.mult),
                        reads=("pLg", "rl"), writes=("Wb",))
                epi1_pend.append(epi1)

                def epi2(j=j, h=h):
                    for bb, bkey, src_ap in (
                            (0, "evc", s_evc[:, 0:512]),
                            (1, "evc", s_evc[:, 512:1024]),
                            (2, "evc", s_evc[:, 1024:1536])):
                        pr.op("pe", lambda e, bb=bb: e.matmul(
                            pI[:, 0:512], s_e3w[32:35, bb * P:(bb + 1) * P],
                            s_Wb[32:35, :],
                            start=True, stop=True, skip_group_check=True),
                            reads=("Wb", "e3w"), writes=("pI",))
                        pr.op("act", lambda e: e.activation(
                            s_wrep[:], pI[:, 0:512], AF.Identity),
                            reads=("pI",), writes=("wrep",))
                        if bb == 0:
                            pr.op("dve", lambda e, src_ap=src_ap:
                                  e.tensor_tensor(
                                      s_att[:], src_ap, s_wrep[:], ALU.mult),
                                  reads=(bkey, "wrep"), writes=("att",))
                        else:
                            pr.op("dve", lambda e, src_ap=src_ap:
                                  e.scalar_tensor_tensor(
                                      s_sq[:, 0:512], src_ap, 1.0, s_wrep[:],
                                      ALU.mult, ALU.mult),
                                  reads=(bkey, "wrep"), writes=("sq",))
                            pr.op("dve", lambda e: e.tensor_tensor(
                                s_att[:], s_att[:], s_sq[:, 0:512], ALU.add),
                                reads=("att", "sq"), writes=("att",))
                    pr.op("act", lambda e: e.activation(
                        s_att16[:], s_att[:], AF.Identity),
                        reads=("att",), writes=("att16",))
                    pr.dma("sync",
                           attd[4 * h:4 * h + 4, :, j * P:(j + 1) * P]
                           .rearrange("g p q -> p g q"),
                           s_att16[:].rearrange("p (g q) -> p g q", q=P),
                           reads=("att16",), writes=(("attd", h, j),))
                epi2_pend.append(epi2)

            epi1_pend = []
            epi2_pend = []
            load_negm(0)
            sel_stage(0, 0)
            sel_stage_b()
            m01r_load()
            for idx, (j, h) in enumerate(units):
                if h == 0:
                    j_boundary(j)
                main_stage(idx, j, h)
            while epi1_pend:
                epi1_pend.pop(0)()
            while epi2_pend:
                epi2_pend.pop(0)()
            pr.barrier()

        with ExitStack() as c2:
            sb2 = lambda n, sh, dt=F32: c2.enter_context(nc.sbuf_tensor(n, sh, dt))
            s_attf = sb2("s_attf", [P, QH * TOK], BF16)
            s_wo = [sb2(f"s_wo{i}", [P, NT * P], BF16) for i in range(2)]
            s_x1c = [sb2(f"s_x1c{i}", [P, TOK]) for i in range(2)]
            s_x2c = sb2("s_x2c", [P, TOK])
            pr.dma("sync", s_attf[:].rearrange("p (qh t) -> p qh t", t=TOK),
                   attd.rearrange("qh p t -> p qh t"), writes=("attf",),
                   reads=tuple(("attd", h, j) for h in range(H) for j in range(NJ)))
            wor = woT.rearrange("(k p) c -> p k c", p=P)
            x1r2 = x1T.rearrange("(k p) t -> p k t", p=P)
            x2r = x2d.rearrange("(k p) t -> p k t", p=P)
            for cc in range(NT):
                wb = s_wo[cc % 2]
                pr.dma("gp", wb[:].rearrange("p (k c) -> p k c", c=P),
                       wor[:, :, cc * P:(cc + 1) * P], writes=(f"wo{cc % 2}",))
                pr.dma("sync", s_x1c[cc % 2][:], x1r2[:, cc, :],
                       writes=(f"x1c{cc % 2}",))
                pbank = pA if cc % 2 == 0 else pB
                def womm(e, wb=wb, pbank=pbank):
                    for k in range(NT):
                        e.matmul(pbank[:, 0:512], wb[:, k * P:(k + 1) * P],
                                 s_attf[:, k * TOK:k * TOK + 512],
                                 start=(k == 0), stop=(k == NT - 1))
                    for k in range(NT):
                        i = e.matmul(pbank[:, 512:1024], wb[:, k * P:(k + 1) * P],
                                     s_attf[:, k * TOK + 512:(k + 1) * TOK],
                                     start=(k == 0), stop=(k == NT - 1))
                    return i
                pr.op("pe", womm, reads=(f"wo{cc % 2}", "attf"),
                      writes=(f"pAB{cc % 2}",))
                pr.op("dve", lambda e, cc=cc, pbank=pbank: e.tensor_tensor(
                    s_x2c[:], s_x1c[cc % 2][:], pbank[:, :], ALU.add),
                    reads=(f"pAB{cc % 2}", f"x1c{cc % 2}"), writes=("x2c",))
                pr.dma("sync", x2r[:, cc, :], s_x2c[:], reads=("x2c",), writes=(("x2d", cc),))
            pr.barrier()

        with ExitStack() as c3:
            sb3 = lambda n, sh, dt=F32: c3.enter_context(nc.sbuf_tensor(n, sh, dt))
            s_x = sb3("s_x", [P, D])
            s_sq3 = sb3("s_sq3", [P, D])
            s_y32 = sb3("s_y32", [P, D])
            s_y16r = sb3("s_y16r", [P, D], BF16)
            s_x3 = sb3("s_x3", [P, D])
            s_stat = sb3("s_stat", [1, 512])
            s_rep = sb3("s_rep", [P, 1, 2 * P])
            s_lg = sb3("s_lg", [EXP, P])
            s_e = sb3("s_e", [P, EXP])
            s_col3 = sb3("s_col3", [P, 8])
            s_m8 = sb3("s_m8", [P, 8])
            s_zap = sb3("s_zap", [P, EXP])
            s_pm = sb3("s_pm", [P, EXP])
            s_pms = sb3("s_pms", [P, EXP], BF16)
            s_ws = sb3("s_ws", [P, 1])
            s_pmT = sb3("s_pmT", [EXP, P], BF16)
            s_wsrow = sb3("s_wsrow", [1, P])
            s_wsr = sb3("s_wsr", [P, 1, P])
            s_eb = sb3("s_eb", [EXP, D], BF16)
            s_rw = sb3("s_rw", [P, NT * EXP], BF16)
            pr.dma("gp", s_eb[:], ebias, writes=("eb",))
            pr.dma("gp", s_rw[:].rearrange("p (k e) -> p k e", e=EXP),
                   rwT.rearrange("(k p) e -> p k e", p=P), writes=("rw",))
            pb0, pb1, pb2, pb7 = pc[0], pc[1], pc[2], pc[3]
            x2rr = x2d.rearrange("(k p) t -> p k t", p=P)
            outr = o_out.rearrange("(k p) t -> p k t", p=P)

            for j in range(NJ):
                pr.dma("sync", s_x[:].rearrange("p (k t) -> p k t", t=P),
                       x2rr[:, :, j * P:(j + 1) * P], writes=("x",),
                       reads=tuple(("x2d", cc) for cc in range(NT)))
                pr.op("act", lambda e: e.activation(s_sq3[:], s_x[:], AF.Square),
                      reads=("x",), writes=("sq3",))
                def sums(e):
                    for k in range(NT):
                        i = e.matmul(pb0[0:1, 0:P], s_onesP[:],
                                     s_x[:, k * P:(k + 1) * P],
                                     start=(k == 0), stop=(k == NT - 1))
                    return i
                pr.op("pe", sums, reads=("x", "onesP"), writes=("pb0",))
                def sqsums(e):
                    for k in range(NT):
                        i = e.matmul(pb0[0:1, P:2 * P], s_onesP[:],
                                     s_sq3[:, k * P:(k + 1) * P],
                                     start=(k == 0), stop=(k == NT - 1))
                    return i
                pr.op("pe", sqsums, reads=("sq3", "onesP"), writes=("pb0",))
                pr.op("dve", lambda e: e.tensor_scalar(
                    s_stat[0:1, 0:P], pb0[0:1, 0:P], 1.0 / D, None, ALU.mult),
                    reads=("pb0",), writes=("stat",))
                pr.op("dve", lambda e: e.tensor_tensor(
                    s_stat[0:1, P:2 * P], s_stat[0:1, 0:P], s_stat[0:1, 0:P],
                    ALU.mult), reads=("stat",), writes=("stat",))
                pr.op("dve", lambda e: e.scalar_tensor_tensor(
                    s_stat[0:1, 2 * P:3 * P], pb0[0:1, P:2 * P], 1.0 / D,
                    s_stat[0:1, P:2 * P], ALU.mult, ALU.subtract),
                    reads=("pb0", "stat"), writes=("stat",))
                pr.op("act", lambda e: e.activation(
                    s_stat[0:1, 2 * P:3 * P], s_stat[0:1, 2 * P:3 * P], AF.Sqrt,
                    bias=s_eps[0:1, 0:1], scale=1.0),
                    reads=("stat", "eps"), writes=("stat",))
                pr.op("dve", lambda e: e.reciprocal(
                    s_stat[0:1, 3 * P:4 * P], s_stat[0:1, 2 * P:3 * P]),
                    reads=("stat",), writes=("stat",))
                def bcast(e):
                    e.matmul(pb0[:, 0:P], s_ones1[:], s_stat[0:1, 0:P],
                             start=True, stop=True)
                    return e.matmul(pb0[:, P:2 * P], s_ones1[:],
                                    s_stat[0:1, 3 * P:4 * P], start=True, stop=True)
                pr.op("pe", bcast, reads=("stat", "ones1"), writes=("pb0",))
                pr.op("dve", lambda e: e.tensor_copy(s_rep[:, 0, :], pb0[:, 0:2 * P]),
                      reads=("pb0",), writes=("rep",))
                pr.op("dve", lambda e: e.tensor_tensor(
                    s_y32[:].rearrange("p (k t) -> p k t", k=NT),
                    s_x[:].rearrange("p (k t) -> p k t", k=NT),
                    s_rep[:, :, 0:P].to_broadcast([P, NT, P]), ALU.subtract),
                    reads=("x", "rep"), writes=("y32",))
                pr.op("dve", lambda e: e.tensor_tensor(
                    s_y32[:].rearrange("p (k t) -> p k t", k=NT),
                    s_y32[:].rearrange("p (k t) -> p k t", k=NT),
                    s_rep[:, :, P:2 * P].to_broadcast([P, NT, P]), ALU.mult),
                    reads=("rep",), writes=("y32",))
                pr.op("dve", lambda e: e.tensor_copy(s_y16r[:], s_y32[:]),
                      reads=("y32",), writes=("y16r",))
                def logits(e):
                    for k in range(NT):
                        i = e.matmul(pb0[0:EXP, 2 * P:3 * P],
                                     s_rw[:, k * EXP:(k + 1) * EXP],
                                     s_y16r[:, k * P:(k + 1) * P],
                                     start=(k == 0), stop=(k == NT - 1))
                    return i
                pr.op("pe", logits, reads=("y16r", "rw"), writes=("pb0",))
                pr.op("act", lambda e: e.activation(
                    s_lg[:], pb0[0:EXP, 2 * P:3 * P], AF.Identity,
                    bias=s_rb[:, 0:1], scale=1.0),
                    reads=("pb0", "rb"), writes=("lg",))
                pr.op("pe", lambda e: e.matmul(
                    pb0[0:P, 3 * P:3 * P + EXP], s_lg[:], s_id[0:EXP, 0:EXP],
                    start=True, stop=True), reads=("lg", "id"), writes=("pb0",))
                pr.op("dve", lambda e: e.tensor_reduce(
                    s_col3[:, 0:1], pb0[0:P, 3 * P:3 * P + EXP], AX, ALU.max,
                    negate=True), reads=("pb0",), writes=("col3",))
                pr.op("act", lambda e: e.activation(
                    s_e[:], pb0[0:P, 3 * P:3 * P + EXP], AF.Exp,
                    bias=s_col3[:, 0:1], scale=1.0),
                    reads=("pb0", "col3"), writes=("e",))
                pr.op("dve", lambda e: e.tensor_reduce(
                    s_col3[:, 1:2], s_e[:], AX, ALU.add), reads=("e",), writes=("col3",))
                pr.op("dve", lambda e: e.max(out=s_m8[:], in_=s_e[:]),
                      reads=("e",), writes=("m8",))
                pr.op("dve", lambda e: e.match_replace(
                    out=s_zap[:], in_to_replace=s_m8[:], in_values=s_e[:],
                    imm_value=0.0), reads=("m8", "e"), writes=("zap", "m8"))
                pr.op("dve", lambda e: e.tensor_tensor(
                    s_pm[:], s_e[:], s_zap[:], ALU.subtract),
                    reads=("e", "zap"), writes=("pm",))
                pr.op("dve", lambda e: e.tensor_reduce(
                    s_col3[:, 2:3], s_pm[:], AX, ALU.add), reads=("pm",), writes=("col3",))
                pr.op("dve", lambda e: e.reciprocal(s_col3[:, 3:4], s_col3[:, 1:2]),
                      reads=("col3",), writes=("col3",))
                pr.op("dve", lambda e: e.scalar_tensor_tensor(
                    s_ws[:], s_col3[:, 2:3], 0.5, s_col3[:, 3:4], ALU.mult, ALU.mult),
                    reads=("col3",), writes=("ws",))
                pr.op("dve", lambda e: e.tensor_scalar(
                    s_col3[:, 4:5], s_col3[:, 3:4], 0.5, None, ALU.mult),
                    reads=("col3",), writes=("col3",))
                pr.op("dve", lambda e: e.tensor_scalar(
                    s_pms[:], s_pm[:], s_col3[:, 4:5], None, ALU.mult),
                    reads=("pm", "col3"), writes=("pms",))
                pr.op("pe", lambda e: e.matmul(
                    pb7[0:EXP, 0:P], s_pms[:], s_idb[:], start=True, stop=True),
                    reads=("pms", "idb"), writes=("pb7",))
                pr.op("dve", lambda e: e.tensor_copy(s_pmT[:], pb7[0:EXP, 0:P]),
                      reads=("pb7",), writes=("pmT",))
                pr.op("pe", lambda e: e.matmul(
                    pb7[0:1, 0:P], s_ws[:], s_id[:], start=True, stop=True),
                    reads=("ws", "id"), writes=("pb7",))
                pr.op("dve", lambda e: e.tensor_copy(s_wsrow[:], pb7[0:1, 0:P]),
                      reads=("pb7",), writes=("wsrow",))
                pr.op("pe", lambda e: e.matmul(
                    pb0[:, 0:P], s_ones1[:], s_wsrow[:], start=True, stop=True),
                    reads=("wsrow", "ones1"), writes=("pb0",))
                pr.op("dve", lambda e: e.tensor_copy(s_wsr[:, 0, :], pb0[:, 0:P]),
                      reads=("pb0",), writes=("wsr",))
                pr.op("dve", lambda e: e.tensor_tensor(
                    s_sq3[:].rearrange("p (k t) -> p k t", k=NT),
                    s_y32[:].rearrange("p (k t) -> p k t", k=NT),
                    s_wsr[:].to_broadcast([P, NT, P]), ALU.mult),
                    reads=("y32", "wsr"), writes=("sq3",))
                pr.op("dve", lambda e: e.tensor_tensor(
                    s_x3[:], s_x[:], s_sq3[:], ALU.add),
                    reads=("x", "sq3"), writes=("x3",))
                for k in range(NT):
                    bank = pb1 if k % 2 == 0 else pb2
                    bkey = "pb1" if k % 2 == 0 else "pb2"
                    pr.op("pe", lambda e, k=k, bank=bank: e.matmul(
                        bank[0:P, 0:P], s_eb[:, k * P:(k + 1) * P], s_pmT[:],
                        start=True, stop=True),
                        reads=("eb", "pmT"), writes=(bkey,))
                    pr.op("dve", lambda e, k=k, bank=bank: e.tensor_tensor(
                        s_x3[:, k * P:(k + 1) * P], s_x3[:, k * P:(k + 1) * P],
                        bank[0:P, 0:P], ALU.add),
                        reads=(bkey, "x3"), writes=("x3",))
                pr.dma("sync", outr[:, :, j * P:(j + 1) * P],
                       s_x3[:].rearrange("p (k t) -> p k t", t=P), reads=("x3",))
            pr.finalize(block)
    return nc


P34_OUTS = ["o_out"]


def kernel(**inputs):
    maps1 = _prep_p12(inputs)
    nc1 = _build_p12()
    res1 = run_bass_kernel_spmd(nc1, maps1, core_ids=list(range(8))).results
    maps2 = _prep_p34(inputs, res1)
    nc2 = _build_p34()
    res2 = run_bass_kernel_spmd(nc2, [maps2[c] for c in range(8)],
                                core_ids=list(range(8))).results
    out = np.zeros((2, S, D), dtype=np.float32)
    for c in range(8):
        b, r = c // 4, c % 4
        out[b, _own_positions(r)] = res2[c]["o_out"].T
    return out


# version 11
# speedup vs baseline: 1.0787x; 1.0787x over previous
import numpy as np
import ml_dtypes
from contextlib import ExitStack

import concourse.bass as bass
import concourse.mybir as mybir
from concourse.bass_utils import run_bass_kernel_spmd
from concourse.bass import IndirectOffsetOnAxis

F32 = mybir.dt.float32
BF16 = mybir.dt.bfloat16
U32 = mybir.dt.uint32
AF = mybir.ActivationFunctionType
ALU = mybir.AluOpType
AX = mybir.AxisListType.X

P = 128
D = 4096
S = 4096
NT = 32
NJ = 8
TOK = 1024
H = 8
G = 4
QH = 32
DH = 128
CB = 16
NBLK = 256
LBLK = 64
CDIM = 2048
EXP = 64
SCALE = float(DH) ** -0.5
EPS = 1e-5
NEG = -3.0e38
QCOLS = 512

_PERM = np.concatenate([np.arange(0, DH, 2), np.arange(1, DH, 2)])


class Prog:
    QUEUES = ("sync", "gp")
    ENGINES = ("sync", "gp", "pe", "act", "dve")
    NQSEM = 8

    def __init__(self, nc, ctx):
        self.nc = nc
        self.sems = {e: ctx.enter_context(nc.semaphore("sem_" + e))
                     for e in ("pe", "act", "dve")}
        self.qsems = {q: [ctx.enter_context(nc.semaphore(f"sem_{q}{i}"))
                          for i in range(self.NQSEM)] for q in self.QUEUES}
        self.quse = {q: [0] * self.NQSEM for q in self.QUEUES}
        self.qseq = {q: 0 for q in self.QUEUES}
        self.count = {e: 0 for e in ("pe", "act", "dve")}
        self.stream = {e: [] for e in self.ENGINES}
        self.lastw = {}
        self.readers = {}
        self.waited = {e: {} for e in self.ENGINES}

    def _need(self, eng, token):
        dep_eng, sem, val = token
        if eng == dep_eng == "pe":
            return
        if eng == dep_eng and eng not in self.QUEUES:
            pass
        sid = sem.num
        if val <= self.waited[eng].get(sid, 0):
            return
        self.waited[eng][sid] = val
        self.stream[eng].append(lambda e, s=sem, v=val: e.wait_ge(s, v))

    def _deps(self, eng, reads, writes):
        for k in reads:
            if k in self.lastw:
                self._need(eng, self.lastw[k])
        for k in writes:
            if k in self.lastw:
                self._need(eng, self.lastw[k])
            for tok in self.readers.get(k, {}).values():
                self._need(eng, tok)

    def _record(self, token, reads, writes):
        for k in writes:
            self.lastw[k] = token
            self.readers[k] = {}
        for k in reads:
            self.readers.setdefault(k, {})[token[1].num] = token

    def op(self, eng, fn, reads=(), writes=()):
        assert eng not in self.QUEUES
        self._deps(eng, reads, writes)
        self.count[eng] += 1
        cnt = self.count[eng]
        sem = self.sems[eng]
        self.stream[eng].append(lambda e, f=fn, s=sem: f(e).then_inc(s, 1))
        self._record((eng, sem, cnt), reads, writes)
        return cnt

    def qop(self, q, fn, reads=(), writes=()):
        self._deps(q, reads, writes)
        i = self.qseq[q] % self.NQSEM
        self.qseq[q] += 1
        sem = self.qsems[q][i]
        if self.quse[q][i] > 0:
            self._need(q, (q, sem, 16 * self.quse[q][i]))
        self.quse[q][i] += 1
        val = 16 * self.quse[q][i]
        self.stream[q].append(
            lambda e, f=fn, sm=sem: f(e).then_inc(sm, 16))
        self._record((q, sem, val), reads, writes)
        return val

    def dma(self, q, out, in_, reads=(), writes=(), **kw):
        return self.qop(q, lambda e, o=out, s=in_, kw=kw:
                        e.dma_start(out=o, in_=s, **kw), reads, writes)

    def _alltokens(self):
        toks = [(e, self.sems[e], self.count[e]) for e in ("pe", "act", "dve")
                if self.count[e] > 0]
        for q in self.QUEUES:
            for i in range(self.NQSEM):
                if self.quse[q][i] > 0:
                    toks.append((q, self.qsems[q][i], 16 * self.quse[q][i]))
        return toks

    def barrier(self):
        toks = self._alltokens()
        for e in self.ENGINES:
            for tok in toks:
                if tok[0] != e or e in self.QUEUES:
                    self._need(e, tok)

    def finalize(self, block):
        self.barrier()
        for i in range(self.NQSEM):
            for q in self.QUEUES:
                if self.quse[q][i] > 0:
                    self._need("sync", (("x",), self.qsems[q][i],
                                        16 * self.quse[q][i])[0:3]
                               if False else
                               ("_", self.qsems[q][i], 16 * self.quse[q][i]))

        def runner(items):
            def go(e):
                for f in items:
                    f(e)
            return go

        block.sync(runner(self.stream["sync"]))
        block.gpsimd(runner(self.stream["gp"]))
        block.tensor(runner(self.stream["pe"]))
        block.scalar(runner(self.stream["act"]))
        block.vector(runner(self.stream["dve"]))


def _rope_tables():
    inv = 1.0 / (10000.0 ** (np.arange(0, DH, 2, dtype=np.float64) / DH))
    ang = np.arange(S, dtype=np.float64)[:, None] * inv[None, :]
    return (np.cos(ang).T.astype(np.float32), np.sin(ang).T.astype(np.float32))


def _own_positions(r):
    tiles = np.arange(NJ) * 4 + r
    return (tiles[:, None] * P + np.arange(P)[None, :]).reshape(-1)


def _prep_p12(inputs):
    x = np.asarray(inputs["L0_attn_hidden_states"])
    w_qkv = np.asarray(inputs["L0_attn_w_qkv"])
    k_w1 = np.asarray(inputs["L0_attn_k_w1"]); k_b1 = np.asarray(inputs["L0_attn_k_b1"])
    k_w2 = np.asarray(inputs["L0_attn_k_w2"]); k_b2 = np.asarray(inputs["L0_attn_k_b2"])
    v_w1 = np.asarray(inputs["L0_attn_v_w1"]); v_b1 = np.asarray(inputs["L0_attn_v_b1"])
    v_w2 = np.asarray(inputs["L0_attn_v_w2"]); v_b2 = np.asarray(inputs["L0_attn_v_b2"])
    k_pos = np.asarray(inputs["L0_attn_k_pos"]); v_pos = np.asarray(inputs["L0_attn_v_pos"])
    w_gate = np.asarray(inputs["L0_attn_w_gate"]); b_gate = np.asarray(inputs["L0_attn_b_gate"])
    rw = np.asarray(inputs["L0_ffn_router_weight"]); rb = np.asarray(inputs["L0_ffn_router_bias"])
    eb = np.asarray(inputs["L0_ffn_expert_bias"])

    qk_perm = np.arange(6144)
    for qh in range(QH):
        qk_perm[qh * DH:(qh + 1) * DH] = qh * DH + _PERM
    for kh in range(H):
        base = QH * DH + kh * DH
        qk_perm[base:base + DH] = base + _PERM
    wqkvT = np.ascontiguousarray(w_qkv[qk_perm].T)

    in_perm_k = (np.arange(CB)[:, None] * DH + _PERM[None, :]).reshape(-1)
    cos, sin = _rope_tables()
    shared = dict(
        wqkvT=wqkvT,
        kw1T=np.ascontiguousarray(k_w1[:, in_perm_k].T),
        vw1T=np.ascontiguousarray(v_w1.T),
        kw2T=np.ascontiguousarray(k_w2[_PERM].T),
        vw2T=np.ascontiguousarray(v_w2.T),
        kb1=np.ascontiguousarray(k_b1.reshape(16, 128).T),
        vb1=np.ascontiguousarray(v_b1.reshape(16, 128).T),
        kb2=k_b2[_PERM].reshape(DH, 1).copy(), vb2=v_b2.reshape(DH, 1).copy(),
        kposT=np.ascontiguousarray(k_pos[:, :, _PERM].transpose(2, 0, 1).reshape(DH, H * CB)),
        vposT=np.ascontiguousarray(v_pos.transpose(2, 0, 1).reshape(DH, H * CB)),
        wgateT=np.ascontiguousarray(w_gate.T), bgate=b_gate.reshape(96, 1).copy(),
        rwT=np.ascontiguousarray(rw.T), rbias=rb.reshape(EXP, 1).copy(), ebias=eb,
        ident=np.eye(P, dtype=np.float32), identb=np.eye(P, dtype=ml_dtypes.bfloat16),
    )
    maps = []
    for c in range(8):
        b, r = c // 4, c % 4
        pos = _own_positions(r)
        m = dict(shared)
        m.update(xT=np.ascontiguousarray(x[b][pos].T),
                 cosT=np.ascontiguousarray(cos[:, pos]),
                 sinT=np.ascontiguousarray(sin[:, pos]))
        maps.append(m)
    return maps


def _build_p12():
    nc = bass.Bass("TRN2", target_bir_lowering=False)
    d_ = lambda n, sh, dt=F32, k="ExternalInput": nc.dram_tensor(n, sh, dt, kind=k).ap()
    xT = d_("xT", [D, TOK])
    wqkvT = d_("wqkvT", [D, 6144])
    kw1T = d_("kw1T", [CDIM, CDIM]); vw1T = d_("vw1T", [CDIM, CDIM])
    kw2T = d_("kw2T", [CDIM, DH]); vw2T = d_("vw2T", [CDIM, DH])
    kb1 = d_("kb1", [P, 16]); vb1 = d_("vb1", [P, 16])
    kb2 = d_("kb2", [DH, 1]); vb2 = d_("vb2", [DH, 1])
    kposT = d_("kposT", [DH, H * CB]); vposT = d_("vposT", [DH, H * CB])
    wgateT = d_("wgateT", [D, 96]); bgate = d_("bgate", [96, 1])
    rwT = d_("rwT", [D, EXP]); rbias = d_("rbias", [EXP, 1]); ebias = d_("ebias", [EXP, D])
    ident = d_("ident", [P, P]); identb = d_("identb", [P, P], BF16)
    cosT = d_("cosT", [64, TOK]); sinT = d_("sinT", [64, TOK])

    o_x1 = d_("o_x1", [D, TOK], F32, "ExternalOutput")
    o_q = d_("o_q", [H, DH, G * TOK], BF16, "ExternalOutput")
    o_rq = d_("o_rq", [H, DH, G * TOK], BF16, "ExternalOutput")
    o_qsum = d_("o_qsum", [H, DH, TOK], BF16, "ExternalOutput")
    o_gates = d_("o_gates", [96, TOK], F32, "ExternalOutput")
    o_rk = d_("o_rk", [H, DH, TOK], BF16, "ExternalOutput")
    o_ktok = d_("o_ktok", [TOK, H * DH], BF16, "ExternalOutput")
    o_vtok = d_("o_vtok", [TOK, H * DH], BF16, "ExternalOutput")
    o_ck = d_("o_ck", [H, LBLK, DH], BF16, "ExternalOutput")
    o_cv = d_("o_cv", [H, LBLK, DH], BF16, "ExternalOutput")

    with ExitStack() as ctx:
        sb_ = lambda n, sh, dt=F32: ctx.enter_context(nc.sbuf_tensor(n, sh, dt))
        s_y16 = sb_("s_y16", [P, NT * TOK], BF16)
        s_rw = sb_("s_rw", [P, NT * EXP], BF16)
        s_eb = sb_("s_eb", [EXP, D], BF16)
        s_wg = sb_("s_wg", [P, NT * 96], BF16)
        s_id = sb_("s_id", [P, P]); s_idb = sb_("s_idb", [P, P], BF16)
        s_ones1 = sb_("s_ones1", [1, P]); s_onesP = sb_("s_onesP", [P, 1])
        s_rb = sb_("s_rb", [EXP, 1]); s_bg = sb_("s_bg", [96, 1])
        s_kb1 = sb_("s_kb1", [P, 16]); s_vb1 = sb_("s_vb1", [P, 16])
        s_kb2 = sb_("s_kb2", [DH, 1]); s_vb2 = sb_("s_vb2", [DH, 1])
        s_kpos = sb_("s_kpos", [DH, H * CB]); s_vpos = sb_("s_vpos", [DH, H * CB])
        s_cos = sb_("s_cos", [P, TOK]); s_sin = sb_("s_sin", [P, TOK])
        s_eps = sb_("s_eps", [1, 1])

        ps_ = lambda n: ctx.enter_context(nc.psum_tensor(n, [P, 512], F32))
        pb = [ps_(f"pb{i}") for i in range(8)]

        pr = Prog(nc, ctx)
        block = ctx.enter_context(nc.Block())

        for dst, src, key in [
            (s_rb[:], rbias, "rb"), (s_bg[:], bgate, "bg"),
            (s_kb1[:], kb1, "kb1"), (s_vb1[:], vb1, "vb1"),
            (s_kb2[:], kb2, "kb2"), (s_vb2[:], vb2, "vb2"),
            (s_kpos[:], kposT, "kpos"), (s_vpos[:], vposT, "vpos"),
            (s_cos[0:64, :], cosT, "cos"), (s_sin[0:64, :], sinT, "sin"),
            (s_cos[64:128, :], cosT, "cos"), (s_sin[64:128, :], sinT, "sin"),
            (s_id[:], ident, "id"), (s_idb[:], identb, "idb"),
        ]:
            pr.dma("sync", dst, src, writes=(key,))
        pr.dma("gp", s_rw[:].rearrange("p (k e) -> p k e", e=EXP),
               rwT.rearrange("(k p) e -> p k e", p=P), writes=("rw",))
        pr.dma("gp", s_eb[:], ebias, writes=("eb",))
        pr.dma("gp", s_wg[:].rearrange("p (k e) -> p k e", e=96),
               wgateT.rearrange("(k p) e -> p k e", p=P), writes=("wg",))
        pr.op("dve", lambda e: e.memset(s_ones1[:], 1.0), writes=("ones1",))
        pr.op("dve", lambda e: e.memset(s_eps[:], EPS), writes=("eps",))
        pr.op("dve", lambda e: e.memset(s_onesP[:], 1.0), writes=("onesP",))

        xTr = xT.rearrange("(k p) t -> p k t", p=P)
        x1r = o_x1.rearrange("(k p) t -> p k t", p=P)

        with ExitStack() as c1:
            sb1 = lambda n, sh, dt=F32: c1.enter_context(nc.sbuf_tensor(n, sh, dt))
            s_x = sb1("s_x", [P, D])
            s_sq = sb1("s_sq", [P, D])
            s_y32 = sb1("s_y32", [P, D])
            s_y16r = sb1("s_y16r", [P, D], BF16)
            s_x1 = sb1("s_x1", [P, D])
            s_stat = sb1("s_stat", [1, 512])
            s_rep = sb1("s_rep", [P, 1, 2 * P])
            s_lg = sb1("s_lg", [EXP, P])
            s_e = sb1("s_e", [P, EXP])
            s_col = sb1("s_col", [P, 8])
            s_m8 = sb1("s_m8", [P, 8])
            s_zap = sb1("s_zap", [P, EXP])
            s_pm = sb1("s_pm", [P, EXP])
            s_pms = sb1("s_pms", [P, EXP], BF16)
            s_ws = sb1("s_ws", [P, 1])
            s_pmT = sb1("s_pmT", [EXP, P], BF16)
            s_wsrow = sb1("s_wsrow", [1, P])
            s_wsr = sb1("s_wsr", [P, 1, P])

            def layer_norm(src_key, src_sb, out16, out16_key):
                pr.op("act", lambda e: e.activation(s_sq[:], src_sb[:], AF.Square),
                      reads=(src_key,), writes=("sq",))
                def sums(e):
                    for k in range(NT):
                        i = e.matmul(pb[0][0:1, 0:P], s_onesP[:],
                                     src_sb[:, k * P:(k + 1) * P],
                                     start=(k == 0), stop=(k == NT - 1))
                    return i
                pr.op("pe", sums, reads=(src_key, "onesP"), writes=("pb0",))
                def sqsums(e):
                    for k in range(NT):
                        i = e.matmul(pb[0][0:1, P:2 * P], s_onesP[:],
                                     s_sq[:, k * P:(k + 1) * P],
                                     start=(k == 0), stop=(k == NT - 1))
                    return i
                pr.op("pe", sqsums, reads=("sq", "onesP"), writes=("pb0",))
                pr.op("dve", lambda e: e.tensor_scalar(
                    s_stat[0:1, 0:P], pb[0][0:1, 0:P], 1.0 / D, None, ALU.mult),
                    reads=("pb0",), writes=("stat",))
                pr.op("dve", lambda e: e.tensor_tensor(
                    s_stat[0:1, P:2 * P], s_stat[0:1, 0:P], s_stat[0:1, 0:P],
                    ALU.mult), reads=("stat",), writes=("stat",))
                pr.op("dve", lambda e: e.scalar_tensor_tensor(
                    s_stat[0:1, 2 * P:3 * P], pb[0][0:1, P:2 * P], 1.0 / D,
                    s_stat[0:1, P:2 * P], ALU.mult, ALU.subtract),
                    reads=("pb0", "stat"), writes=("stat",))
                pr.op("act", lambda e: e.activation(
                    s_stat[0:1, 2 * P:3 * P], s_stat[0:1, 2 * P:3 * P], AF.Sqrt,
                    bias=s_eps[0:1, 0:1], scale=1.0),
                    reads=("stat", "eps"), writes=("stat",))
                pr.op("dve", lambda e: e.reciprocal(
                    s_stat[0:1, 3 * P:4 * P], s_stat[0:1, 2 * P:3 * P]),
                    reads=("stat",), writes=("stat",))
                def bcast(e):
                    e.matmul(pb[0][:, 0:P], s_ones1[:], s_stat[0:1, 0:P],
                             start=True, stop=True)
                    return e.matmul(pb[0][:, P:2 * P], s_ones1[:],
                                    s_stat[0:1, 3 * P:4 * P], start=True, stop=True)
                pr.op("pe", bcast, reads=("stat", "ones1"), writes=("pb0",))
                pr.op("dve", lambda e: e.tensor_copy(s_rep[:, 0, :], pb[0][:, 0:2 * P]),
                      reads=("pb0",), writes=("rep",))
                pr.op("dve", lambda e: e.tensor_tensor(
                    s_y32[:].rearrange("p (k t) -> p k t", k=NT),
                    src_sb[:].rearrange("p (k t) -> p k t", k=NT),
                    s_rep[:, :, 0:P].to_broadcast([P, NT, P]), ALU.subtract),
                    reads=(src_key, "rep"), writes=("y32",))
                pr.op("dve", lambda e: e.tensor_tensor(
                    s_y32[:].rearrange("p (k t) -> p k t", k=NT),
                    s_y32[:].rearrange("p (k t) -> p k t", k=NT),
                    s_rep[:, :, P:2 * P].to_broadcast([P, NT, P]), ALU.mult),
                    reads=("rep",), writes=("y32",))
                src3 = s_y32[:].rearrange("p (k t) -> p k t", t=P) \
                    if len(out16.shape) == 3 else s_y32[:]
                pr.op("dve", lambda e, src3=src3: e.tensor_copy(out16, src3),
                      reads=("y32",), writes=(out16_key,))

            for j in range(NJ):
                pr.dma("sync", s_x[:].rearrange("p (k t) -> p k t", t=P),
                       xTr[:, :, j * P:(j + 1) * P], writes=("x",))
                layer_norm("x", s_x, s_y16r[:], "y16r")
                def logits(e):
                    for k in range(NT):
                        i = e.matmul(pb[0][0:EXP, 2 * P:3 * P],
                                     s_rw[:, k * EXP:(k + 1) * EXP],
                                     s_y16r[:, k * P:(k + 1) * P],
                                     start=(k == 0), stop=(k == NT - 1))
                    return i
                pr.op("pe", logits, reads=("y16r", "rw"), writes=("pb0",))
                pr.op("act", lambda e: e.activation(
                    s_lg[:], pb[0][0:EXP, 2 * P:3 * P], AF.Identity,
                    bias=s_rb[:, 0:1], scale=1.0),
                    reads=("pb0", "rb"), writes=("lg",))
                pr.op("pe", lambda e: e.matmul(
                    pb[0][0:P, 3 * P:3 * P + EXP], s_lg[:], s_id[0:EXP, 0:EXP],
                    start=True, stop=True), reads=("lg", "id"), writes=("pb0",))
                pr.op("dve", lambda e: e.tensor_reduce(
                    s_col[:, 0:1], pb[0][0:P, 3 * P:3 * P + EXP], AX, ALU.max,
                    negate=True), reads=("pb0",), writes=("col",))
                pr.op("act", lambda e: e.activation(
                    s_e[:], pb[0][0:P, 3 * P:3 * P + EXP], AF.Exp,
                    bias=s_col[:, 0:1], scale=1.0),
                    reads=("pb0", "col"), writes=("e",))
                pr.op("dve", lambda e: e.tensor_reduce(
                    s_col[:, 1:2], s_e[:], AX, ALU.add), reads=("e",), writes=("col",))
                pr.op("dve", lambda e: e.max(out=s_m8[:], in_=s_e[:]),
                      reads=("e",), writes=("m8",))
                pr.op("dve", lambda e: e.match_replace(
                    out=s_zap[:], in_to_replace=s_m8[:], in_values=s_e[:],
                    imm_value=0.0), reads=("m8", "e"), writes=("zap", "m8"))
                pr.op("dve", lambda e: e.tensor_tensor(
                    s_pm[:], s_e[:], s_zap[:], ALU.subtract),
                    reads=("e", "zap"), writes=("pm",))
                pr.op("dve", lambda e: e.tensor_reduce(
                    s_col[:, 2:3], s_pm[:], AX, ALU.add), reads=("pm",), writes=("col",))
                pr.op("dve", lambda e: e.reciprocal(s_col[:, 3:4], s_col[:, 1:2]),
                      reads=("col",), writes=("col",))
                pr.op("dve", lambda e: e.scalar_tensor_tensor(
                    s_ws[:], s_col[:, 2:3], 0.5, s_col[:, 3:4], ALU.mult, ALU.mult),
                    reads=("col",), writes=("ws",))
                pr.op("dve", lambda e: e.tensor_scalar(
                    s_col[:, 4:5], s_col[:, 3:4], 0.5, None, ALU.mult),
                    reads=("col",), writes=("col",))
                pr.op("dve", lambda e: e.tensor_scalar(
                    s_pms[:], s_pm[:], s_col[:, 4:5], None, ALU.mult),
                    reads=("pm", "col"), writes=("pms",))
                pr.op("pe", lambda e: e.matmul(
                    pb[7][0:EXP, 0:P], s_pms[:], s_idb[:], start=True, stop=True),
                    reads=("pms", "idb"), writes=("pb7",))
                pr.op("dve", lambda e: e.tensor_copy(s_pmT[:], pb[7][0:EXP, 0:P]),
                      reads=("pb7",), writes=("pmT",))
                pr.op("pe", lambda e: e.matmul(
                    pb[7][0:1, 0:P], s_ws[:], s_id[:], start=True, stop=True),
                    reads=("ws", "id"), writes=("pb7",))
                pr.op("dve", lambda e: e.tensor_copy(s_wsrow[:], pb[7][0:1, 0:P]),
                      reads=("pb7",), writes=("wsrow",))
                pr.op("pe", lambda e: e.matmul(
                    pb[0][:, 0:P], s_ones1[:], s_wsrow[:], start=True, stop=True),
                    reads=("wsrow", "ones1"), writes=("pb0",))
                pr.op("dve", lambda e: e.tensor_copy(s_wsr[:, 0, :], pb[0][:, 0:P]),
                      reads=("pb0",), writes=("wsr",))
                pr.op("dve", lambda e: e.tensor_tensor(
                    s_sq[:].rearrange("p (k t) -> p k t", k=NT),
                    s_y32[:].rearrange("p (k t) -> p k t", k=NT),
                    s_wsr[:].to_broadcast([P, NT, P]), ALU.mult),
                    reads=("y32", "wsr"), writes=("sq",))
                pr.op("dve", lambda e: e.tensor_tensor(
                    s_x1[:], s_x[:], s_sq[:], ALU.add),
                    reads=("x", "sq"), writes=("x1",))
                for k in range(NT):
                    bank = 1 + (k % 2)
                    pr.op("pe", lambda e, k=k, bank=bank: e.matmul(
                        pb[bank][0:P, 0:P], s_eb[:, k * P:(k + 1) * P], s_pmT[:],
                        start=True, stop=True),
                        reads=("eb", "pmT"), writes=(f"pb{bank}",))
                    pr.op("dve", lambda e, k=k, bank=bank: e.tensor_tensor(
                        s_x1[:, k * P:(k + 1) * P], s_x1[:, k * P:(k + 1) * P],
                        pb[bank][0:P, 0:P], ALU.add),
                        reads=(f"pb{bank}", "x1"), writes=("x1",))
                pr.dma("sync", x1r[:, :, j * P:(j + 1) * P],
                       s_x1[:].rearrange("p (k t) -> p k t", t=P), reads=("x1",))
                y16j = s_y16[:].rearrange("p (k j t) -> p k j t", j=NJ, t=P)[:, :, j, :]
                layer_norm("x1", s_x1, y16j, ("y16", j))

            pr.barrier()

        y16keys = tuple(("y16", j) for j in range(NJ))

        with ExitStack() as c2:
            sb2 = lambda n, sh, dt=F32: c2.enter_context(nc.sbuf_tensor(n, sh, dt))
            s_w = [sb2(f"s_w{i}", [P, NT * P], BF16) for i in range(2)]
            s_qkv = sb2("s_qkv", [P, TOK], BF16)
            s_rope = sb2("s_rope", [P, TOK], BF16)
            s_ksc = sb2("s_ksc", [P, TOK], BF16)
            s_qs1 = sb2("s_qs1", [P, TOK], BF16)
            s_qs2 = sb2("s_qs2", [P, TOK], BF16)
            s_tr = sb2("s_tr", [P, TOK], BF16)
            s_cin = sb2("s_cin", [P, 2 * H * TOK], BF16)
            s_h1 = sb2("s_h1", [P, 16 * 512], BF16)
            s_co = sb2("s_co", [P, 512], BF16)
            s_gt = sb2("s_gt", [96, TOK])

            wqr = wqkvT.rearrange("(k p) c -> p k c", p=P)
            ktokr = o_ktok.rearrange("(j p) c -> p j c", p=P)
            vtokr = o_vtok.rearrange("(j p) c -> p j c", p=P)

            def qkv_coltile(ct):
                wbuf = s_w[ct % 2]
                pr.dma("gp", wbuf[:].rearrange("p (k c) -> p k c", c=P),
                       wqr[:, :, ct * P:(ct + 1) * P], writes=(f"wbuf{ct % 2}",))
                b0, b1 = (3, 4) if ct % 2 == 0 else (5, 6)
                def mms(e, b0=b0, b1=b1, wbuf=wbuf):
                    for k in range(NT):
                        e.matmul(pb[b0][:, 0:512], wbuf[:, k * P:(k + 1) * P],
                                 s_y16[:, k * TOK:k * TOK + 512],
                                 start=(k == 0), stop=(k == NT - 1))
                    for k in range(NT):
                        i = e.matmul(pb[b1][:, 0:512], wbuf[:, k * P:(k + 1) * P],
                                     s_y16[:, k * TOK + 512:(k + 1) * TOK],
                                     start=(k == 0), stop=(k == NT - 1))
                    return i
                pr.op("pe", mms, reads=(f"wbuf{ct % 2}",) + y16keys,
                      writes=(f"pb{b0}", f"pb{b1}"))
                return b0, b1

            def rope(src, src_key, dst, dst_key):
                x0 = src[0:64, :]; x1 = src[64:128, :]
                pr.op("dve", lambda e: e.tensor_tensor(
                    dst[0:64, :], x0, s_cos[0:64, :], ALU.mult),
                    reads=(src_key, "cos"), writes=(dst_key,))
                pr.op("dve", lambda e: e.tensor_tensor(
                    s_tr[0:64, :], x1, s_sin[64:128, :], ALU.mult),
                    reads=(src_key, "sin"), writes=("trs",))
                pr.op("dve", lambda e: e.tensor_tensor(
                    dst[0:64, :], dst[0:64, :], s_tr[0:64, :], ALU.subtract),
                    reads=(dst_key, "trs"), writes=(dst_key,))
                pr.op("dve", lambda e: e.tensor_tensor(
                    dst[64:128, :], x1, s_cos[64:128, :], ALU.mult),
                    reads=(src_key, "cos"), writes=(dst_key,))
                pr.op("dve", lambda e: e.tensor_tensor(
                    s_tr[64:128, :], x0, s_sin[0:64, :], ALU.mult),
                    reads=(src_key, "sin"), writes=("trs",))
                pr.op("dve", lambda e: e.tensor_tensor(
                    dst[64:128, :], dst[64:128, :], s_tr[64:128, :], ALU.add),
                    reads=(dst_key, "trs"), writes=(dst_key,))

            for ct in range(QH):
                h, g = ct // 4, ct % 4
                b0, b1 = qkv_coltile(ct)
                def ev(e, b0=b0, b1=b1):
                    e.activation(s_qkv[:, 0:512], pb[b0][:, 0:512], AF.Identity,
                                 scale=SCALE)
                    return e.activation(s_qkv[:, 512:1024], pb[b1][:, 0:512],
                                        AF.Identity, scale=SCALE)
                pr.op("act", ev, reads=(f"pb{b0}", f"pb{b1}"), writes=("qkv",))
                pr.dma("sync", o_q[h, :, g * TOK:(g + 1) * TOK], s_qkv[:],
                       reads=("qkv",))
                rope(s_qkv, "qkv", s_rope, "rope")
                pr.dma("sync", o_rq[h, :, g * TOK:(g + 1) * TOK], s_rope[:],
                       reads=("rope",))
                if g == 0:
                    pr.op("dve", lambda e: e.tensor_copy(s_qs1[:], s_qkv[:]),
                          reads=("qkv",), writes=("qs1",))
                elif g == 1:
                    pr.op("dve", lambda e: e.tensor_tensor(
                        s_qs1[:], s_qs1[:], s_qkv[:], ALU.add),
                        reads=("qkv", "qs1"), writes=("qs1",))
                elif g == 2:
                    pr.op("dve", lambda e: e.tensor_copy(s_qs2[:], s_qkv[:]),
                          reads=("qkv",), writes=("qs2",))
                else:
                    pr.op("dve", lambda e: e.tensor_tensor(
                        s_qs2[:], s_qs2[:], s_qkv[:], ALU.add),
                        reads=("qkv", "qs2"), writes=("qs2",))
                    pr.op("dve", lambda e: e.tensor_tensor(
                        s_qs1[:], s_qs1[:], s_qs2[:], ALU.add),
                        reads=("qs1", "qs2"), writes=("qs1",))
                    pr.op("dve", lambda e: e.tensor_scalar(
                        s_qs1[:], s_qs1[:], 0.25, None, ALU.mult),
                        reads=("qs1",), writes=("qs1",))
                    pr.dma("sync", o_qsum[h, :, :], s_qs1[:], reads=("qs1",))

            for ct in range(QH, QH + H):
                h = ct - QH
                b0, b1 = qkv_coltile(ct)
                def evk(e, b0=b0, b1=b1):
                    e.activation(s_ksc[:, 0:512], pb[b0][:, 0:512], AF.Identity)
                    return e.activation(s_ksc[:, 512:1024], pb[b1][:, 0:512],
                                        AF.Identity)
                pr.op("act", evk, reads=(f"pb{b0}", f"pb{b1}"), writes=("ksc",))
                rope(s_ksc, "ksc", s_rope, "rope")
                pr.dma("sync", o_rk[h, :, :], s_rope[:], reads=("rope",))
                for jt in range(NJ):
                    pr.op("pe", lambda e, jt=jt: e.matmul(
                        pb[7][:, 0:P], s_rope[:, jt * P:(jt + 1) * P], s_idb[:],
                        start=True, stop=True),
                        reads=("rope", "idb"), writes=("pb7",))
                    pr.op("dve", lambda e, jt=jt: e.tensor_copy(
                        s_tr[:, jt * P:(jt + 1) * P], pb[7][:, 0:P]),
                        reads=("pb7",), writes=("trs",))
                pr.dma("sync", ktokr[:, :, h * DH:(h + 1) * DH],
                       s_tr[:].rearrange("p (j c) -> p j c", j=NJ), reads=("trs",))
                for cb in range(CB):
                    off = h * TOK + cb * LBLK
                    pr.op("dve", lambda e, cb=cb, off=off, h=h: e.tensor_scalar(
                        s_cin[:, off:off + LBLK],
                        s_ksc[:].rearrange("p (blk c) -> p blk c", c=CB)[:, :, cb:cb + 1]
                        .rearrange("p b c -> p (b c)"),
                        s_kpos[:, h * CB + cb:h * CB + cb + 1], None, ALU.add),
                        reads=("ksc", "kpos"), writes=(("cin", 0, h),))

            for ct in range(QH + H, QH + 2 * H):
                h = ct - QH - H
                b0, b1 = qkv_coltile(ct)
                def evv(e, b0=b0, b1=b1):
                    e.activation(s_qkv[:, 0:512], pb[b0][:, 0:512], AF.Identity)
                    return e.activation(s_qkv[:, 512:1024], pb[b1][:, 0:512],
                                        AF.Identity)
                pr.op("act", evv, reads=(f"pb{b0}", f"pb{b1}"), writes=("qkv",))
                for jt in range(NJ):
                    pr.op("pe", lambda e, jt=jt: e.matmul(
                        pb[7][:, 0:P], s_qkv[:, jt * P:(jt + 1) * P], s_idb[:],
                        start=True, stop=True),
                        reads=("qkv", "idb"), writes=("pb7",))
                    pr.op("dve", lambda e, jt=jt: e.tensor_copy(
                        s_tr[:, jt * P:(jt + 1) * P], pb[7][:, 0:P]),
                        reads=("pb7",), writes=("trs",))
                pr.dma("sync", vtokr[:, :, h * DH:(h + 1) * DH],
                       s_tr[:].rearrange("p (j c) -> p j c", j=NJ), reads=("trs",))
                for cb in range(CB):
                    off = (H + h) * TOK + cb * LBLK
                    pr.op("dve", lambda e, cb=cb, off=off, h=h: e.tensor_scalar(
                        s_cin[:, off:off + LBLK],
                        s_qkv[:].rearrange("p (blk c) -> p blk c", c=CB)[:, :, cb:cb + 1]
                        .rearrange("p b c -> p (b c)"),
                        s_vpos[:, h * CB + cb:h * CB + cb + 1], None, ALU.add),
                        reads=("qkv", "vpos"), writes=(("cin", 1, h),))

            ckr = o_ck.rearrange("h b d -> b h d")
            cvr = o_cv.rearrange("h b d -> b h d")
            for side, w1, w2, bias1, bias2, outr in (
                    (0, kw1T, kw2T, s_kb1, s_kb2, ckr),
                    (1, vw1T, vw2T, s_vb1, s_vb2, cvr)):
                cink = tuple(("cin", side, h) for h in range(H))
                w1r = w1.rearrange("(c p) o -> p c o", p=P)
                base = side * H * TOK
                rhs_ap = lambda c, side=side: s_cin[:].rearrange(
                    "p (s h cb blk) -> p s h cb blk", s=2, h=H, cb=CB) \
                    [:, side, :, c, :]
                for o in range(16):
                    wb = s_w[o % 2]
                    pr.dma("gp", wb[:, 0:CDIM].rearrange("p (c o) -> p c o", o=P),
                           w1r[:, :, o * P:(o + 1) * P], writes=(f"wbuf{o % 2}",))
                    bank = 3 + (o % 2)
                    def m1(e, o=o, wb=wb, bank=bank, rhs_ap=rhs_ap):
                        for c in range(16):
                            i = e.matmul(pb[bank][:, 0:512],
                                         wb[:, c * P:(c + 1) * P], rhs_ap(c),
                                         start=(c == 0), stop=(c == 15))
                        return i
                    pr.op("pe", m1, reads=(f"wbuf{o % 2}",) + cink,
                          writes=(f"pb{bank}",))
                    pr.op("act", lambda e, o=o, bank=bank, bias1=bias1: e.activation(
                        s_h1[:, o * 512:(o + 1) * 512], pb[bank][:, 0:512],
                        AF.Relu, bias=bias1[:, o:o + 1], scale=1.0),
                        reads=(f"pb{bank}", "kb1" if side == 0 else "vb1"),
                        writes=("h1",))
                w2r = w2.rearrange("(c p) o -> p c o", p=P)
                pr.dma("gp", s_w[0][:, 0:16 * DH].rearrange("p (c o) -> p c o", o=DH),
                       w2r, writes=("wbuf0",))
                def m2(e):
                    for c in range(16):
                        i = e.matmul(pb[5][:, 0:512],
                                     s_w[0][:, c * DH:(c + 1) * DH],
                                     s_h1[:, c * 512:(c + 1) * 512],
                                     start=(c == 0), stop=(c == 15))
                    return i
                pr.op("pe", m2, reads=("wbuf0", "h1"), writes=("pb5",))
                pr.op("act", lambda e, bias2=bias2: e.activation(
                    s_co[:], pb[5][:, 0:512], AF.Identity,
                    bias=bias2[:, 0:1], scale=1.0),
                    reads=("pb5", "kb2" if side == 0 else "vb2"), writes=("co",))
                for h in range(H):
                    pr.op("pe", lambda e, h=h: e.matmul(
                        pb[7][0:64, 0:P], s_co[:, h * 64:(h + 1) * 64], s_idb[:],
                        start=True, stop=True),
                        reads=("co", "idb"), writes=("pb7",))
                    pr.op("dve", lambda e, h=h: e.tensor_copy(
                        s_tr[0:64, h * DH:(h + 1) * DH], pb[7][0:64, 0:P]),
                        reads=("pb7",), writes=("trs",))
                pr.dma("sync", outr,
                       s_tr[0:64, 0:H * DH].rearrange("p (h d) -> p h d", h=H),
                       reads=("trs",))

            def gmm(e):
                for k in range(NT):
                    e.matmul(pb[1][0:96, 0:512], s_wg[:, k * 96:(k + 1) * 96],
                             s_y16[:, k * TOK:k * TOK + 512],
                             start=(k == 0), stop=(k == NT - 1))
                for k in range(NT):
                    i = e.matmul(pb[2][0:96, 0:512], s_wg[:, k * 96:(k + 1) * 96],
                                 s_y16[:, k * TOK + 512:(k + 1) * TOK],
                                 start=(k == 0), stop=(k == NT - 1))
                return i
            pr.op("pe", gmm, reads=("wg",) + y16keys, writes=("pb1", "pb2"))
            def gev(e):
                e.activation(s_gt[:, 0:512], pb[1][0:96, 0:512], AF.Sigmoid,
                             bias=s_bg[:, 0:1], scale=1.0)
                return e.activation(s_gt[:, 512:1024], pb[2][0:96, 0:512],
                                    AF.Sigmoid, bias=s_bg[:, 0:1], scale=1.0)
            pr.op("act", gev, reads=("pb1", "pb2", "bg"), writes=("gt",))
            pr.dma("sync", o_gates, s_gt[:], reads=("gt",))

            pr.finalize(block)
    return nc


P12_OUTS = ["o_x1", "o_q", "o_rq", "o_qsum", "o_gates", "o_rk", "o_ktok",
            "o_vtok", "o_ck", "o_cv"]




def _prep_p34(inputs, res1, cores=tuple(range(8))):
    mem_kv = np.asarray(inputs["L0_attn_mem_kv"])
    w_o = np.asarray(inputs["L0_attn_w_o"])
    rw = np.asarray(inputs["L0_ffn_router_weight"]); rb = np.asarray(inputs["L0_ffn_router_bias"])
    eb = np.asarray(inputs["L0_ffn_expert_bias"])
    bf = ml_dtypes.bfloat16

    onehot = np.full((512, NBLK), NEG, dtype=bf)
    for jb in range(NBLK):
        onehot[jb + 1, jb] = 0.0
    E9 = np.zeros((9, P), dtype=bf)
    for kk in range(P):
        E9[kk // CB, kk] = 1.0
    E9[8, :] = 1.0
    e3 = np.zeros((P, 9), dtype=bf)
    for bb in range(3):
        e3[:, 3 * bb + bb] = 1.0
    e3w = np.zeros((3, 3 * P), dtype=bf)
    for bb in range(3):
        e3w[bb, bb * P:(bb + 1) * P] = 1.0
    gsel = np.zeros((96, 96), dtype=np.float32)
    for qh in range(QH):
        for bb in range(3):
            gsel[3 * qh + bb, qh * 3 + bb] = 1.0
    qloc = np.arange(P)
    wloc = np.arange(P)
    od = np.where((wloc[:, None] >= (qloc[None, :] // CB) * CB)
                  & (wloc[:, None] <= qloc[None, :]), 0.0, NEG).astype(np.float32)
    owndiag = np.tile(od, (1, 4)).reshape(P, 4 * P)
    owndiag = np.concatenate([od] * 4, axis=1).astype(bf)

    shared = dict(
        onehot=onehot, E9=E9, e3=e3, e3w=e3w, gsel=gsel, owndiag=owndiag,
        woT=np.ascontiguousarray(w_o.T),
        rwT=np.ascontiguousarray(rw.T), rbias=rb.reshape(EXP, 1).copy(), ebias=eb,
        ident=np.eye(P, dtype=np.float32), identb=np.eye(P, dtype=bf),
        memkP=np.ascontiguousarray(mem_kv[0, :, 0, :][:, _PERM].T),
        memv=np.ascontiguousarray(mem_kv[1, :, 0, :].reshape(1, H * DH)),
    )

    maps = {}
    for c in cores:
        b, r = c // 4, c % 4
        rkTf = np.zeros((H * DH, S), dtype=bf)
        vtok = np.zeros((S, H * DH), dtype=bf)
        ktok = np.zeros((S, H * DH), dtype=bf)
        ck_full = np.zeros((H, NBLK, DH), dtype=bf)
        cv_full = np.zeros((H, NBLK, DH), dtype=bf)
        for rr in range(4):
            rsrc = res1[4 * b + rr]
            pos = _own_positions(rr)
            rkTf[:, pos] = rsrc["o_rk"].reshape(H * DH, TOK)
            vtok[pos] = rsrc["o_vtok"]
            ktok[pos] = rsrc["o_ktok"]
            oblk = ((np.arange(NJ)[:, None] * 4 + rr) * NJ + np.arange(NJ)[None, :]).reshape(-1)
            ck_full[:, oblk] = rsrc["o_ck"]
            cv_full[:, oblk] = rsrc["o_cv"]
        ckTin = np.zeros((H * DH, 384), dtype=bf)
        for h in range(H):
            ckTin[h * DH:(h + 1) * DH, 1:NBLK + 1] = ck_full[h].T
        cvin = np.zeros((384, H * DH), dtype=bf)
        for h in range(H):
            cvin[1:NBLK + 1, h * DH:(h + 1) * DH] = cv_full[h]
        wk = np.zeros((NJ, 2 * P, H * DH), dtype=bf)
        wv = np.zeros((NJ, 2 * P, H * DH), dtype=bf)
        for j in range(NJ):
            T = 4 * j + r
            lo = T * P - P
            for w in range(2 * P):
                gw = lo + w
                if gw >= 0:
                    wk[j, w] = ktok[gw]
                    wv[j, w] = vtok[gw]
        cmask = np.zeros((NJ, 2, P, 4 * P), dtype=bf)
        impmask = np.zeros((NJ, P, 384), dtype=bf)
        swmask = np.zeros((NJ, 2, P, 4 * P), dtype=bf)
        for j in range(NJ):
            T = 4 * j + r
            qabs = T * P + np.arange(P)
            col = np.arange(384)
            ck_seq = np.where(col == 0, -1, 16 * col - 1)
            allow = (ck_seq[:, None] < qabs[None, :]) & (col[:, None] <= NBLK)
            cm = np.where(allow, 0.0, NEG).astype(np.float32)
            cmask[j] = np.tile(cm.reshape(3, P, 1, P), (1, 1, 4, 1)).reshape(3, P, 4 * P)[:2]
            col = np.arange(384)
            iallow = (col[None, :] >= 1) & (col[None, :] <= NBLK) \
                & (16 * col[None, :] - 1 < qabs[:, None])
            im = np.where(iallow, 0.0, NEG).astype(np.float32)
            im[:, NBLK + 1] = -1000.0
            impmask[j] = im
            kw = T * P - P + np.arange(2 * P)
            sallow = ((kw[:, None] >= qabs[None, :] - 64)
                      & (kw[:, None] <= qabs[None, :]) & (kw[:, None] >= 0))
            sm = np.where(sallow, 0.0, NEG).astype(np.float32)
            swmask[j] = np.tile(sm.reshape(2, P, 1, P), (1, 1, 4, 1)).reshape(2, P, 4 * P)

        rsrc = res1[c]
        m = dict(shared)
        m.update(
            x1T=rsrc["o_x1"], qT=rsrc["o_q"], rqT=rsrc["o_rq"],
            qsumT=rsrc["o_qsum"], gatesT=rsrc["o_gates"],
            rkTf=rkTf, vtokf=vtok, ckTin=ckTin, cvin=cvin,
            wk=np.ascontiguousarray(wk.reshape(NJ * 2 * P, H * DH)),
            wv=np.ascontiguousarray(wv.reshape(NJ * 2 * P, H * DH)),
            cmask=cmask, impmask=impmask, swmask=swmask,
        )
        maps[c] = m
    return maps


def _build_p34():
    nc = bass.Bass("TRN2", target_bir_lowering=False)
    d_ = lambda n, sh, dt=F32, k="ExternalInput": nc.dram_tensor(n, sh, dt, kind=k).ap()
    x1T = d_("x1T", [D, TOK])
    qT = d_("qT", [H, DH, G * TOK], BF16)
    rqT = d_("rqT", [H, DH, G * TOK], BF16)
    qsumT = d_("qsumT", [H, DH, TOK], BF16)
    gatesT = d_("gatesT", [96, TOK])
    rkTf = d_("rkTf", [H * DH, S], BF16)
    vtokf = d_("vtokf", [S, H * DH], BF16)
    ckTin = d_("ckTin", [H * DH, 384], BF16)
    cvin = d_("cvin", [384, H * DH], BF16)
    wk = d_("wk", [NJ * 2 * P, H * DH], BF16)
    wv = d_("wv", [NJ * 2 * P, H * DH], BF16)
    cmask = d_("cmask", [NJ, 2, P, 4 * P], BF16)
    impmask = d_("impmask", [NJ, P, 384], BF16)
    swmask = d_("swmask", [NJ, 2, P, 4 * P], BF16)
    owndiag = d_("owndiag", [P, 4 * P], BF16)
    onehot = d_("onehot", [512, NBLK], BF16)
    E9in = d_("E9", [9, P], BF16)
    e3in = d_("e3", [P, 9], BF16)
    e3w = d_("e3w", [3, 3 * P], BF16)
    gselin = d_("gsel", [96, 96])
    woT = d_("woT", [D, D])
    rwT = d_("rwT", [D, EXP]); rbias = d_("rbias", [EXP, 1]); ebias = d_("ebias", [EXP, D])
    ident = d_("ident", [P, P]); identb = d_("identb", [P, P], BF16)
    memkP = d_("memkP", [DH, H]); memv = d_("memv", [1, H * DH])

    attd = nc.dram_tensor("attd", [QH, DH, TOK], BF16, kind="Internal").ap()
    m01d = nc.dram_tensor("m01d", [2 * P, P], BF16, kind="Internal").ap()
    x2d = nc.dram_tensor("x2d", [D, TOK], F32, kind="Internal").ap()
    o_out = d_("o_out", [D, TOK], F32, "ExternalOutput")

    with ExitStack() as ctx:
        sb_ = lambda n, sh, dt=F32: ctx.enter_context(nc.sbuf_tensor(n, sh, dt))
        s_id = sb_("s_id", [P, P]); s_idb = sb_("s_idb", [P, P], BF16)
        s_ones1 = sb_("s_ones1", [1, P]); s_onesP = sb_("s_onesP", [P, 1])
        s_onesPb = sb_("s_onesPb", [P, 1], BF16)
        s_eps = sb_("s_eps", [1, 1])
        s_E9 = sb_("s_E9", [9, P], BF16)
        s_e3 = sb_("s_e3", [P, 9], BF16)
        s_e3w = sb_("s_e3w", [35, 3 * P], BF16)
        s_gsel = sb_("s_gsel", [96, 96])
        s_od = sb_("s_od", [P, 4 * P], BF16)
        s_gates = sb_("s_gates", [96, TOK])
        s_maxk = sb_("s_maxk", [1, 16])
        s_ones1b = sb_("s_ones1b", [1, P], BF16)
        s_rb = sb_("s_rb", [EXP, 1])

        pA = ctx.enter_context(nc.psum_tensor("pA", [P, 1024], F32))
        pB = ctx.enter_context(nc.psum_tensor("pB", [P, 1024], F32))
        pc = [ctx.enter_context(nc.psum_tensor(f"pc{i}", [P, 512], F32))
              for i in range(4)]
        pSc = lambda t: pA[:, (t % 2) * 512:(t % 2) * 512 + 512]
        pM = pB[:, 0:512]
        pL = pB[:, 512:1024]
        pLg = pB[32:35, 512:1024]
        pC, pF, pW, pI = pc[0], pc[1], pc[2], pc[3]

        pr = Prog(nc, ctx)
        block = ctx.enter_context(nc.Block())

        for dst, src, key in [
            (s_id[:], ident, "id"), (s_idb[:], identb, "idb"),
            (s_E9[:], E9in, "E9"), (s_e3[:], e3in, "e3"),
            (s_e3w[32:35, :], e3w, "e3w"), (s_gsel[:], gselin, "gsel"),
            (s_od[:], owndiag, "od"), (s_gates[:], gatesT, "gates"),
            (s_rb[:], rbias, "rb"),
        ]:
            pr.dma("sync", dst, src, writes=(key,))
        pr.op("dve", lambda e: e.memset(s_ones1[:], 1.0), writes=("ones1",))
        pr.op("dve", lambda e: e.memset(s_onesP[:], 1.0), writes=("onesP",))
        pr.op("dve", lambda e: e.memset(s_onesPb[:], 1.0), writes=("onesPb",))
        pr.op("dve", lambda e: e.memset(s_ones1b[:], 1.0), writes=("ones1b",))
        pr.op("dve", lambda e: e.memset(s_eps[:], EPS), writes=("eps",))

        with ExitStack() as c1:
            sb1 = lambda n, sh, dt=F32: c1.enter_context(nc.sbuf_tensor(n, sh, dt))
            s_rk = sb1("s_rk", [P, H * S], BF16)
            s_vt = sb1("s_vt", [P, NT * H * DH], BF16)
            s_ck = sb1("s_ck", [P, H * 384], BF16)
            s_cv = sb1("s_cv", [P, 3 * H * DH], BF16)
            s_sq = sb1("s_sq", [P, 512], BF16)
            s_qg = [sb1(f"s_qg{i}", [P, QCOLS], BF16) for i in range(2)]
            s_rqg = [sb1(f"s_rqg{i}", [P, QCOLS], BF16) for i in range(2)]
            s_qsj = sb1("s_qsj", [P, P], BF16)
            s_cm = sb1("s_cm", [P, 2 * 512], BF16)
            s_im = [sb1(f"s_im{i}", [P, 384], BF16) for i in range(2)]
            s_sm = sb1("s_sm", [P, 2 * 512], BF16)
            s_wk = sb1("s_wk", [P, H * DH], BF16)
            s_wv = sb1("s_wv", [P, 2 * H * DH], BF16)
            s_wkT = sb1("s_wkT", [P, H * 2 * DH], BF16)
            s_pcd = sb1("s_pcd", [P, 1024], BF16)
            s_pfd = sb1("s_pfd", [P, 1024], BF16)
            s_ps16 = sb1("s_ps16", [P, 2 * 512], BF16)
            s_po16 = sb1("s_po16", [P, 512], BF16)
            s_ei = sb1("s_ei", [P, 384])
            s_col = sb1("s_col", [P, 8])
            s_m8v = sb1("s_m8v", [P, 8])
            s_m8i = sb1("s_m8i", [P, 8], U32)
            s_sel = sb1("s_sel", [P, 4], U32)
            s_selv = sb1("s_selv", [P, 4], U32)
            s_mg = [sb1(f"s_mg{i}", [P, NBLK], BF16) for i in range(4)]
            s_mS = s_mg[0]
            s_m01T = s_mg[1]
            s_m01r = sb1("s_m01r", [8, NT * P], BF16)
            s_mb = [sb1(f"s_mb{i}", [9, 512], BF16) for i in range(2)]
            s_nrow = sb1("s_nrow", [1, 512])
            s_nmf = sb1("s_nmf", [1, 512])
            s_nmf16 = [sb1(f"s_nmf16_{i}", [1, 512], BF16) for i in range(2)]
            s_nmc16 = [sb1(f"s_nmc16_{i}", [1, 512], BF16) for i in range(2)]
            s_rl = sb1("s_rl", [35, 512])
            s_Wb = sb1("s_Wb", [35, 512], BF16)
            s_wrep = sb1("s_wrep", [P, 512], BF16)
            s_evc = sb1("s_evc", [P, 3 * 512], BF16)
            s_att = sb1("s_att", [P, 512], BF16)
            s_att16 = sb1("s_att16", [P, 512], BF16)
            s_row = sb1("s_row", [1, 8])

            for h in range(H):
                pr.dma("sync", s_rk[:, h * S:(h + 1) * S],
                       rkTf[h * DH:(h + 1) * DH, :], writes=(("rk", h),))
            pr.dma("sync", s_vt[:].rearrange("p (t c) -> p t c", c=H * DH),
                   vtokf.rearrange("(t p) c -> p t c", p=P), writes=("vt",))
            pr.dma("sync", s_ck[:].rearrange("p (h c) -> p h c", c=384),
                   ckTin.rearrange("(h p) c -> p h c", p=P), writes=("ckt",))
            pr.dma("sync", s_cv[:].rearrange("p (u c) -> p u c", c=H * DH),
                   cvin.rearrange("(u p) c -> p u c", p=P), writes=("cvt",))
            pr.dma("gp", s_ck[:].rearrange("p (h c) -> p h c", c=384)[:, :, 0:1],
                   memkP.rearrange("p (h o) -> p h o", o=1),
                   reads=("ckt",), writes=("ckt",),
                   allow_slow_non_contiguous=True)
            pr.dma("gp", s_cv[0:1, 0:H * DH], memv, reads=("cvt",), writes=("cvt",))

            pr.op("dve", lambda e: e.memset(s_maxk[:], 0.0), writes=("maxk",))
            def norm_pass(src_ap, width, colidx, key):
                for base in range(0, width, 512):
                    w = min(512, width - base)
                    pr.op("act", lambda e, base=base, w=w: e.activation(
                        s_sq[:, 0:w], src_ap[:, base:base + w], AF.Square),
                        reads=(key,), writes=("sq",))
                    for cb2 in range(0, w, 512):
                        cw = min(512, w - cb2)
                        pr.op("pe", lambda e, cb2=cb2, cw=cw: e.matmul(
                            pI[0:1, 0:cw], s_onesPb[:],
                            s_sq[:, cb2:cb2 + cw], start=True, stop=True),
                            reads=("sq", "onesPb"), writes=("pI",))
                        pr.op("dve", lambda e, cw=cw: e.tensor_reduce(
                            s_row[0:1, 0:1], pI[0:1, 0:cw], AX, ALU.max),
                            reads=("pI",), writes=("rowx",))
                        pr.op("dve", lambda e, colidx=colidx: e.tensor_tensor(
                            s_maxk[0:1, colidx:colidx + 1],
                            s_maxk[0:1, colidx:colidx + 1],
                            s_row[0:1, 0:1], ALU.max),
                            reads=("rowx", "maxk"), writes=("maxk",))
            for h in range(H):
                norm_pass(s_rk[:, h * S:(h + 1) * S], S, h, ("rk", h))
                norm_pass(s_ck[:, h * 384:(h + 1) * 384], 384, 8 + h, "ckt")

            units = [(j, h) for j in range(NJ) for h in range(H)]

            def j_boundary(j):
                pr.dma("sync", s_cm[:].rearrange("p (t c) -> p t c", c=512),
                       cmask[j].rearrange("t p c -> p t c"), writes=("cmj",))
                pr.dma("sync", s_sm[:].rearrange("p (t c) -> p t c", c=512),
                       swmask[j].rearrange("u p c -> p u c"), writes=("smj",))
                pr.dma("sync", s_wv[:].rearrange("p (u c) -> p u c", c=H * DH),
                       wv.rearrange("(j u p) c -> j p u c", j=NJ, p=P)[j],
                       writes=("wvbuf",))
                for u in range(2):
                    pr.dma("sync", s_wk[:],
                           wk.rearrange("(j u p) c -> j u p c", j=NJ, p=P)[j, u],
                           writes=("wkbuf",))
                    for h in range(H):
                        pr.op("pe", lambda e, h=h: e.matmul(
                            pM[:, 0:P], s_wk[:, h * DH:(h + 1) * DH],
                            s_idb[:], start=True, stop=True),
                            reads=("wkbuf", "idb"), writes=("pM",))
                        pr.op("dve", lambda e, h=h, u=u: e.tensor_copy(
                            s_wkT[:, (h * 2 + u) * DH:(h * 2 + u + 1) * DH],
                            pM[:, 0:P]), reads=("pM",), writes=("wkT",))

            def sel_stage(j, h):
                if h == 0:
                    pr.dma("sync", s_im[j % 2][:], impmask[j],
                           writes=(f"imj{j % 2}",))
                pr.dma("sync", s_qsj[:], qsumT[h, :, j * P:(j + 1) * P],
                       writes=("qsj",))
                pr.op("pe", lambda e, h=h: e.matmul(
                    pW[:, 0:384], s_qsj[:], s_ck[:, h * 384:(h + 1) * 384],
                    start=True, stop=True, skip_group_check=True),
                    reads=("qsj", "ckt"), writes=("pW",))
                pr.op("dve", lambda e, j=j: e.tensor_tensor(
                    pW[:, 0:384], pW[:, 0:384], s_im[j % 2][:], ALU.add),
                    reads=(f"imj{j % 2}", "pW"), writes=("pW",))
                pr.op("dve", lambda e: e.tensor_reduce(
                    s_col[:, 0:1], pW[:, 0:384], AX, ALU.max, negate=True),
                    reads=("pW",), writes=("col",))
                pr.op("act", lambda e: e.activation(
                    s_ei[:], pW[:, 0:384], AF.Exp, bias=s_col[:, 0:1],
                    scale=1.0), reads=("pW", "col"), writes=("ei",))
                pr.op("dve", lambda e: e.tensor_reduce(
                    s_col[:, 1:2], s_ei[:], AX, ALU.add),
                    reads=("ei",), writes=("col",))
                pr.op("dve", lambda e: e.max(out=s_m8v[:], in_=s_ei[:]),
                      reads=("ei",), writes=("m8v",))
                pr.op("dve", lambda e: e.max_index(
                    out=s_m8i[:], in_max=s_m8v[:], in_values=s_ei[:]),
                    reads=("m8v", "ei"), writes=("m8i",))
                pr.op("dve", lambda e: e.tensor_scalar(
                    s_col[:, 2:3], s_col[:, 1:2], 1e-10, None, ALU.mult),
                    reads=("col",), writes=("col",))
                pr.op("dve", lambda e: e.tensor_scalar(
                    s_selv[:], s_m8v[:, 0:4], s_col[:, 2:3], None, ALU.is_gt),
                    reads=("m8v", "col"), writes=("selv",))
                pr.op("dve", lambda e: e.memset(s_sel[:], 0), writes=("sel",))
                pr.op("dve", lambda e: e.copy_predicated(
                    s_sel[:], s_selv[:], s_m8i[:, 0:4]),
                    reads=("selv", "m8i", "sel"), writes=("sel",))
                for ss in range(4):
                    pr.qop("gp", lambda e, ss=ss: e.indirect_dma_start(
                        s_mg[ss][:], None, onehot,
                        IndirectOffsetOnAxis(s_sel[:, ss:ss + 1], 0)),
                        reads=("sel",), writes=(f"mg{ss}",))

            def sel_stage_b():
                pr.op("dve", lambda e: e.tensor_tensor(
                    s_mS[:], s_mg[0][:], s_mg[1][:], ALU.max),
                    reads=("mg0", "mg1"), writes=("mg0",))
                pr.op("dve", lambda e: e.tensor_tensor(
                    s_mS[:], s_mS[:], s_mg[2][:], ALU.max),
                    reads=("mg0", "mg2"), writes=("mg0",))
                pr.op("dve", lambda e: e.tensor_tensor(
                    s_mS[:], s_mS[:], s_mg[3][:], ALU.max),
                    reads=("mg0", "mg3"), writes=("mg0",))
                for u in range(2):
                    pr.op("pe", lambda e, u=u: e.matmul(
                        pI[:, 0:P], s_mS[:, u * P:(u + 1) * P], s_idb[:],
                        start=True, stop=True),
                        reads=("mg0", "idb"), writes=("pI",))
                    pr.op("dve", lambda e, u=u: e.tensor_copy(
                        s_m01T[:, u * P:(u + 1) * P], pI[:, 0:P]),
                        reads=("pI",), writes=("mg1",))
                pr.dma("sync", m01d.rearrange("(u p) q -> p u q", p=P),
                       s_m01T[:].rearrange("p (u q) -> p u q", q=P),
                       reads=("mg1",), writes=("m01d",))

            def m01r_load():
                pr.dma("sync",
                       s_m01r[0:8, :].rearrange("r (t q) -> r t q", q=P),
                       m01d.rearrange("(t r) q -> r t q", r=8),
                       reads=("m01d",), writes=("m01r",))

            def load_negm(ui):
                jn, hn = units[ui]
                sl = ui % 2
                pr.dma("sync", s_qg[sl][:].rearrange("p (g q) -> p g q", q=P),
                       qT[hn].rearrange("p (g t) -> p g t", t=TOK)
                       [:, :, jn * P:(jn + 1) * P], writes=(f"qg{sl}",))
                pr.dma("sync", s_rqg[sl][:].rearrange("p (g q) -> p g q", q=P),
                       rqT[hn].rearrange("p (g t) -> p g t", t=TOK)
                       [:, :, jn * P:(jn + 1) * P], writes=(f"rqg{sl}",))

                def negm_row(src, srckey, col, dst16, dstkey):
                    pr.op("dve", lambda e, src=src: e.tensor_tensor(
                        s_sq[:, 0:512], src, src, ALU.mult),
                        reads=(srckey,), writes=("sq",))
                    pr.op("pe", lambda e: e.matmul(
                        pM[0:1, 0:512], s_onesPb[:], s_sq[:, 0:512],
                        start=True, stop=True, skip_group_check=True),
                        reads=("sq", "onesPb"), writes=("pM",))
                    pr.op("dve", lambda e, col=col: e.tensor_scalar(
                        s_nrow[:], pM[0:1, 0:512],
                        s_maxk[0:1, col:col + 1], None, ALU.mult),
                        reads=("pM", "maxk"), writes=("nrow",))
                    pr.op("act", lambda e: e.activation(
                        s_nrow[:], s_nrow[:], AF.Sqrt),
                        reads=("nrow",), writes=("nrow",))
                    pr.op("dve", lambda e: e.tensor_scalar(
                        s_nmf[:], s_nrow[:], -1.0001, None, ALU.mult),
                        reads=("nrow",), writes=("nmf",))
                    pr.op("dve", lambda e, dst16=dst16: e.tensor_copy(
                        dst16, s_nmf[:]), reads=("nmf",), writes=(dstkey,))
                negm_row(s_rqg[sl][:], f"rqg{sl}", hn,
                         s_nmf16[sl][:], f"nmf16_{sl}")
                negm_row(s_qg[sl][:], f"qg{sl}", 8 + hn,
                         s_nmc16[sl][:], f"nmc16_{sl}")

            def main_stage(idx, j, h):
                NTD = 4 * j + 4
                sl = idx % 2
                for i in range(2):
                    pr.dma("gp", s_mb[i][8:9, :], s_nmf16[sl][:],
                           reads=(f"nmf16_{sl}",), writes=(f"mb{i}",))

                lfirst = [True]
                pend = []

                def flush_pend():
                    while pend:
                        pend.pop(0)()

                def lpv(probkey, prob_ap, e3c, pvb, pvkey, vkey, v_ap,
                        lstop=False, pvstart=False, pvstop=False):
                    def emit(probkey=probkey, prob_ap=prob_ap, e3c=e3c,
                             pvb=pvb, pvkey=pvkey, vkey=vkey, v_ap=v_ap,
                             lstop=lstop, pvstart=pvstart, pvstop=pvstop):
                        st = lfirst[0]
                        lfirst[0] = False
                        def mm(e, st=st):
                            e.matmul(pL[0:3, :], s_e3[:, e3c:e3c + 3],
                                     prob_ap, start=st, stop=lstop,
                                     skip_group_check=True)
                            return e.matmul(pvb, v_ap, prob_ap,
                                            start=pvstart, stop=pvstop)
                        pr.op("pe", mm, reads=(probkey, "e3", vkey),
                              writes=("pL", pvkey))
                    return emit

                CT = 1 if j <= 3 else 2
                for t in range(CT):
                    def cmm(e, t=t, h=h):
                        e.matmul(pSc(t),
                                 s_ck[:, h * 384 + t * P:h * 384 + (t + 1) * P],
                                 s_qg[sl][:], start=True, stop=False)
                        return e.matmul(pSc(t), s_ones1b[:], s_nmc16[sl][:],
                                        start=False, stop=True)
                    pr.op("pe", cmm,
                          reads=("ckt", f"qg{sl}", "ones1b", f"nmc16_{sl}"),
                          writes=(f"pS{t % 2}",))
                    pr.op("dve", lambda e, t=t: e.tensor_tensor(
                        pSc(t), pSc(t), s_cm[:, t * 512:(t + 1) * 512], ALU.add),
                        reads=("cmj", f"pS{t % 2}"), writes=(f"pS{t % 2}",))
                if CT == 1:
                    pr.op("act", lambda e: e.activation(
                        s_pcd[:, 0:512], pSc(0), AF.Exp),
                        reads=("pS0",), writes=("pcd",))
                else:
                    pr.op("act", lambda e: e.activation(
                        s_pcd[:], pA[:, 0:1024], AF.Exp),
                        reads=("pS0", "pS1"), writes=("pcd",))
                for t in range(CT):
                    pend.append(lpv(
                        "pcd", s_pcd[:, (t % 2) * 512:(t % 2) * 512 + 512],
                        0, pC[:, :], "pC",
                        "cvt", s_cv[:, t * H * DH + h * DH:
                                    t * H * DH + (h + 1) * DH],
                        pvstart=(t == 0), pvstop=(t == CT - 1)))

                if idx + 1 < len(units):
                    sel_stage(*units[idx + 1])

                for t in range(NTD):
                    pr.op("dve", lambda e, t=t: e.tensor_copy(
                        s_mb[t % 2][0:8, :].rearrange("r (g q) -> r g q", q=P),
                        s_m01r[0:8, :].rearrange("r (t q) -> r t q", q=P)
                        [:, t:t + 1, :].to_broadcast([8, 4, P])),
                        reads=("m01r",), writes=(f"mb{t % 2}",))
                    def fmm(e, t=t, h=h):
                        e.matmul(pSc(t),
                                 s_rk[:, h * S + t * P:h * S + (t + 1) * P],
                                 s_rqg[sl][:], start=True, stop=False)
                        return e.matmul(pSc(t), s_E9[:], s_mb[t % 2][:],
                                        start=False, stop=True)
                    pr.op("pe", fmm,
                          reads=(("rk", h), f"rqg{sl}", f"mb{t % 2}", "E9"),
                          writes=(f"pS{t % 2}",))
                    if t % 2 == 1:
                        flush_pend()
                        pr.op("act", lambda e: e.activation(
                            s_pfd[:], pA[:, 0:1024], AF.Exp),
                            reads=("pS0", "pS1"), writes=("pfd",))
                        for tt in (t - 1, t):
                            pend.append(lpv(
                                "pfd",
                                s_pfd[:, (tt % 2) * 512:(tt % 2) * 512 + 512],
                                3, pF[:, :], "pF",
                                "vt", s_vt[:, tt * H * DH + h * DH:
                                           tt * H * DH + (h + 1) * DH],
                                pvstart=(tt == 0)))
                if epi1_pend:
                    epi1_pend.pop(0)()
                if idx + 1 < len(units):
                    load_negm(idx + 1)
                def omm(e, h=h):
                    e.matmul(pSc(0),
                             s_wkT[:, (h * 2 + 1) * DH:(h * 2 + 2) * DH],
                             s_rqg[sl][:], start=True, stop=False)
                    return e.matmul(pSc(0), s_ones1b[:], s_nmf16[sl][:],
                                    start=False, stop=True)
                pr.op("pe", omm,
                      reads=("wkT", f"rqg{sl}", "ones1b", f"nmf16_{sl}"),
                      writes=("pS0",))
                pr.op("dve", lambda e: e.tensor_tensor(
                    pSc(0), pSc(0), s_od[:], ALU.add),
                    reads=("od", "pS0"), writes=("pS0",))
                def smm0(e, h=h):
                    e.matmul(pSc(1),
                             s_wkT[:, (h * 2) * DH:(h * 2 + 1) * DH],
                             s_rqg[sl][:], start=True, stop=False)
                    return e.matmul(pSc(1), s_ones1b[:], s_nmf16[sl][:],
                                    start=False, stop=True)
                pr.op("pe", smm0,
                      reads=("wkT", f"rqg{sl}", "ones1b", f"nmf16_{sl}"),
                      writes=("pS1",))
                pr.op("dve", lambda e: e.tensor_tensor(
                    pSc(1), pSc(1), s_sm[:, 0:512], ALU.add),
                    reads=("smj", "pS1"), writes=("pS1",))
                pr.op("act", lambda e: e.activation(
                    s_po16[:], pSc(0), AF.Exp),
                    reads=("pS0",), writes=("po16",))
                pr.op("act", lambda e: e.activation(
                    s_ps16[:, 0:512], pSc(1), AF.Exp),
                    reads=("pS1",), writes=("ps16",))
                flush_pend()
                pend.append(lpv(
                    "po16", s_po16[:], 3, pF[:, :], "pF",
                    "wvbuf", s_wv[:, H * DH + h * DH:H * DH + (h + 1) * DH],
                    pvstop=True))
                def smm1(e, h=h):
                    e.matmul(pSc(0),
                             s_wkT[:, (h * 2 + 1) * DH:(h * 2 + 2) * DH],
                             s_rqg[sl][:], start=True, stop=False)
                    return e.matmul(pSc(0), s_ones1b[:], s_nmf16[sl][:],
                                    start=False, stop=True)
                pr.op("pe", smm1,
                      reads=("wkT", f"rqg{sl}", "ones1b", f"nmf16_{sl}"),
                      writes=("pS0",))
                pr.op("dve", lambda e: e.tensor_tensor(
                    pSc(0), pSc(0), s_sm[:, 512:1024], ALU.add),
                    reads=("smj", "pS0"), writes=("pS0",))
                pr.op("act", lambda e: e.activation(
                    s_ps16[:, 512:1024], pSc(0), AF.Exp),
                    reads=("pS0",), writes=("ps16",))
                flush_pend()
                pend.append(lpv(
                    "ps16", s_ps16[:, 0:512], 6, pW[:, :], "pW",
                    "wvbuf", s_wv[:, h * DH:h * DH + DH],
                    pvstart=True))
                flush_pend()
                pend.append(lpv(
                    "ps16", s_ps16[:, 512:1024], 6, pW[:, :], "pW",
                    "wvbuf", s_wv[:, H * DH + h * DH:H * DH + (h + 1) * DH],
                    lstop=True, pvstop=True))
                if idx + 1 < len(units):
                    sel_stage_b()
                    m01r_load()
                if epi2_pend:
                    epi2_pend.pop(0)()
                for g in range(G):
                    pr.op("pe", lambda e, g=g, h=h, j=j: e.matmul(
                        pLg[:, g * P:(g + 1) * P],
                        s_gsel[:, (h * G + g) * 3:(h * G + g) * 3 + 3],
                        s_gates[:, j * P:(j + 1) * P],
                        start=(g == 0), stop=(g == 3),
                        skip_group_check=True),
                        reads=("gsel", "gates"), writes=("pLg",))
                flush_pend()
                pr.op("dve", lambda e: e.tensor_scalar(
                    s_rl[32:35, :], pL[0:3, :], 1e-37, None, ALU.add),
                    reads=("pL",), writes=("rl",))
                pr.op("act", lambda e: e.activation(
                    s_evc[:, 0:512], pC[:, :], AF.Identity),
                    reads=("pC",), writes=("evc",))
                pr.op("act", lambda e: e.activation(
                    s_evc[:, 512:1024], pF[:, :], AF.Identity),
                    reads=("pF",), writes=("evc",))
                pr.op("act", lambda e: e.activation(
                    s_evc[:, 1024:1536], pW[:, :], AF.Identity),
                    reads=("pW",), writes=("evc",))

                def epi1():
                    pr.op("dve", lambda e: e.reciprocal(
                        s_rl[32:35, :], s_rl[32:35, :]),
                        reads=("rl",), writes=("rl",))
                    pr.op("dve", lambda e: e.tensor_tensor(
                        s_Wb[32:35, :], pLg[:, 0:512], s_rl[32:35, :],
                        ALU.mult),
                        reads=("pLg", "rl"), writes=("Wb",))
                epi1_pend.append(epi1)

                def epi2(j=j, h=h):
                    for bb, bkey, src_ap in (
                            (0, "evc", s_evc[:, 0:512]),
                            (1, "evc", s_evc[:, 512:1024]),
                            (2, "evc", s_evc[:, 1024:1536])):
                        pr.op("pe", lambda e, bb=bb: e.matmul(
                            pI[:, 0:512], s_e3w[32:35, bb * P:(bb + 1) * P],
                            s_Wb[32:35, :],
                            start=True, stop=True, skip_group_check=True),
                            reads=("Wb", "e3w"), writes=("pI",))
                        pr.op("act", lambda e: e.activation(
                            s_wrep[:], pI[:, 0:512], AF.Identity),
                            reads=("pI",), writes=("wrep",))
                        if bb == 0:
                            pr.op("dve", lambda e, src_ap=src_ap:
                                  e.tensor_tensor(
                                      s_att[:], src_ap, s_wrep[:], ALU.mult),
                                  reads=(bkey, "wrep"), writes=("att",))
                        else:
                            pr.op("dve", lambda e, src_ap=src_ap:
                                  e.scalar_tensor_tensor(
                                      s_sq[:, 0:512], src_ap, 1.0, s_wrep[:],
                                      ALU.mult, ALU.mult),
                                  reads=(bkey, "wrep"), writes=("sq",))
                            pr.op("dve", lambda e: e.tensor_tensor(
                                s_att[:], s_att[:], s_sq[:, 0:512], ALU.add),
                                reads=("att", "sq"), writes=("att",))
                    pr.op("act", lambda e: e.activation(
                        s_att16[:], s_att[:], AF.Identity),
                        reads=("att",), writes=("att16",))
                    pr.dma("sync",
                           attd[4 * h:4 * h + 4, :, j * P:(j + 1) * P]
                           .rearrange("g p q -> p g q"),
                           s_att16[:].rearrange("p (g q) -> p g q", q=P),
                           reads=("att16",), writes=(("attd", h, j),))
                epi2_pend.append(epi2)

            epi1_pend = []
            epi2_pend = []
            load_negm(0)
            sel_stage(0, 0)
            sel_stage_b()
            m01r_load()
            for idx, (j, h) in enumerate(units):
                if h == 0:
                    j_boundary(j)
                main_stage(idx, j, h)
            while epi1_pend:
                epi1_pend.pop(0)()
            while epi2_pend:
                epi2_pend.pop(0)()
            pr.barrier()

        with ExitStack() as c2:
            sb2 = lambda n, sh, dt=F32: c2.enter_context(nc.sbuf_tensor(n, sh, dt))
            s_attf = sb2("s_attf", [P, QH * TOK], BF16)
            s_wo = [sb2(f"s_wo{i}", [P, NT * P], BF16) for i in range(2)]
            s_x1c = [sb2(f"s_x1c{i}", [P, TOK]) for i in range(2)]
            s_x2c = sb2("s_x2c", [P, TOK])
            pr.dma("sync", s_attf[:].rearrange("p (qh t) -> p qh t", t=TOK),
                   attd.rearrange("qh p t -> p qh t"), writes=("attf",),
                   reads=tuple(("attd", h, j) for h in range(H) for j in range(NJ)))
            wor = woT.rearrange("(k p) c -> p k c", p=P)
            x1r2 = x1T.rearrange("(k p) t -> p k t", p=P)
            x2r = x2d.rearrange("(k p) t -> p k t", p=P)
            for cc in range(NT):
                wb = s_wo[cc % 2]
                pr.dma("gp", wb[:].rearrange("p (k c) -> p k c", c=P),
                       wor[:, :, cc * P:(cc + 1) * P], writes=(f"wo{cc % 2}",))
                pr.dma("sync", s_x1c[cc % 2][:], x1r2[:, cc, :],
                       writes=(f"x1c{cc % 2}",))
                pbank = pA if cc % 2 == 0 else pB
                def womm(e, wb=wb, pbank=pbank):
                    for k in range(NT):
                        e.matmul(pbank[:, 0:512], wb[:, k * P:(k + 1) * P],
                                 s_attf[:, k * TOK:k * TOK + 512],
                                 start=(k == 0), stop=(k == NT - 1))
                    for k in range(NT):
                        i = e.matmul(pbank[:, 512:1024], wb[:, k * P:(k + 1) * P],
                                     s_attf[:, k * TOK + 512:(k + 1) * TOK],
                                     start=(k == 0), stop=(k == NT - 1))
                    return i
                pr.op("pe", womm, reads=(f"wo{cc % 2}", "attf"),
                      writes=(f"pAB{cc % 2}",))
                pr.op("dve", lambda e, cc=cc, pbank=pbank: e.tensor_tensor(
                    s_x2c[:], s_x1c[cc % 2][:], pbank[:, :], ALU.add),
                    reads=(f"pAB{cc % 2}", f"x1c{cc % 2}"), writes=("x2c",))
                pr.dma("sync", x2r[:, cc, :], s_x2c[:], reads=("x2c",), writes=(("x2d", cc),))
            pr.barrier()

        with ExitStack() as c3:
            sb3 = lambda n, sh, dt=F32: c3.enter_context(nc.sbuf_tensor(n, sh, dt))
            s_x = sb3("s_x", [P, D])
            s_sq3 = sb3("s_sq3", [P, D])
            s_y32 = sb3("s_y32", [P, D])
            s_y16r = sb3("s_y16r", [P, D], BF16)
            s_x3 = sb3("s_x3", [P, D])
            s_stat = sb3("s_stat", [1, 512])
            s_rep = sb3("s_rep", [P, 1, 2 * P])
            s_lg = sb3("s_lg", [EXP, P])
            s_e = sb3("s_e", [P, EXP])
            s_col3 = sb3("s_col3", [P, 8])
            s_m8 = sb3("s_m8", [P, 8])
            s_zap = sb3("s_zap", [P, EXP])
            s_pm = sb3("s_pm", [P, EXP])
            s_pms = sb3("s_pms", [P, EXP], BF16)
            s_ws = sb3("s_ws", [P, 1])
            s_pmT = sb3("s_pmT", [EXP, P], BF16)
            s_wsrow = sb3("s_wsrow", [1, P])
            s_wsr = sb3("s_wsr", [P, 1, P])
            s_eb = sb3("s_eb", [EXP, D], BF16)
            s_rw = sb3("s_rw", [P, NT * EXP], BF16)
            pr.dma("gp", s_eb[:], ebias, writes=("eb",))
            pr.dma("gp", s_rw[:].rearrange("p (k e) -> p k e", e=EXP),
                   rwT.rearrange("(k p) e -> p k e", p=P), writes=("rw",))
            pb0, pb1, pb2, pb7 = pc[0], pc[1], pc[2], pc[3]
            x2rr = x2d.rearrange("(k p) t -> p k t", p=P)
            outr = o_out.rearrange("(k p) t -> p k t", p=P)

            for j in range(NJ):
                pr.dma("sync", s_x[:].rearrange("p (k t) -> p k t", t=P),
                       x2rr[:, :, j * P:(j + 1) * P], writes=("x",),
                       reads=tuple(("x2d", cc) for cc in range(NT)))
                pr.op("act", lambda e: e.activation(s_sq3[:], s_x[:], AF.Square),
                      reads=("x",), writes=("sq3",))
                def sums(e):
                    for k in range(NT):
                        i = e.matmul(pb0[0:1, 0:P], s_onesP[:],
                                     s_x[:, k * P:(k + 1) * P],
                                     start=(k == 0), stop=(k == NT - 1))
                    return i
                pr.op("pe", sums, reads=("x", "onesP"), writes=("pb0",))
                def sqsums(e):
                    for k in range(NT):
                        i = e.matmul(pb0[0:1, P:2 * P], s_onesP[:],
                                     s_sq3[:, k * P:(k + 1) * P],
                                     start=(k == 0), stop=(k == NT - 1))
                    return i
                pr.op("pe", sqsums, reads=("sq3", "onesP"), writes=("pb0",))
                pr.op("dve", lambda e: e.tensor_scalar(
                    s_stat[0:1, 0:P], pb0[0:1, 0:P], 1.0 / D, None, ALU.mult),
                    reads=("pb0",), writes=("stat",))
                pr.op("dve", lambda e: e.tensor_tensor(
                    s_stat[0:1, P:2 * P], s_stat[0:1, 0:P], s_stat[0:1, 0:P],
                    ALU.mult), reads=("stat",), writes=("stat",))
                pr.op("dve", lambda e: e.scalar_tensor_tensor(
                    s_stat[0:1, 2 * P:3 * P], pb0[0:1, P:2 * P], 1.0 / D,
                    s_stat[0:1, P:2 * P], ALU.mult, ALU.subtract),
                    reads=("pb0", "stat"), writes=("stat",))
                pr.op("act", lambda e: e.activation(
                    s_stat[0:1, 2 * P:3 * P], s_stat[0:1, 2 * P:3 * P], AF.Sqrt,
                    bias=s_eps[0:1, 0:1], scale=1.0),
                    reads=("stat", "eps"), writes=("stat",))
                pr.op("dve", lambda e: e.reciprocal(
                    s_stat[0:1, 3 * P:4 * P], s_stat[0:1, 2 * P:3 * P]),
                    reads=("stat",), writes=("stat",))
                def bcast(e):
                    e.matmul(pb0[:, 0:P], s_ones1[:], s_stat[0:1, 0:P],
                             start=True, stop=True)
                    return e.matmul(pb0[:, P:2 * P], s_ones1[:],
                                    s_stat[0:1, 3 * P:4 * P], start=True, stop=True)
                pr.op("pe", bcast, reads=("stat", "ones1"), writes=("pb0",))
                pr.op("dve", lambda e: e.tensor_copy(s_rep[:, 0, :], pb0[:, 0:2 * P]),
                      reads=("pb0",), writes=("rep",))
                pr.op("dve", lambda e: e.tensor_tensor(
                    s_y32[:].rearrange("p (k t) -> p k t", k=NT),
                    s_x[:].rearrange("p (k t) -> p k t", k=NT),
                    s_rep[:, :, 0:P].to_broadcast([P, NT, P]), ALU.subtract),
                    reads=("x", "rep"), writes=("y32",))
                pr.op("dve", lambda e: e.tensor_tensor(
                    s_y32[:].rearrange("p (k t) -> p k t", k=NT),
                    s_y32[:].rearrange("p (k t) -> p k t", k=NT),
                    s_rep[:, :, P:2 * P].to_broadcast([P, NT, P]), ALU.mult),
                    reads=("rep",), writes=("y32",))
                pr.op("dve", lambda e: e.tensor_copy(s_y16r[:], s_y32[:]),
                      reads=("y32",), writes=("y16r",))
                def logits(e):
                    for k in range(NT):
                        i = e.matmul(pb0[0:EXP, 2 * P:3 * P],
                                     s_rw[:, k * EXP:(k + 1) * EXP],
                                     s_y16r[:, k * P:(k + 1) * P],
                                     start=(k == 0), stop=(k == NT - 1))
                    return i
                pr.op("pe", logits, reads=("y16r", "rw"), writes=("pb0",))
                pr.op("act", lambda e: e.activation(
                    s_lg[:], pb0[0:EXP, 2 * P:3 * P], AF.Identity,
                    bias=s_rb[:, 0:1], scale=1.0),
                    reads=("pb0", "rb"), writes=("lg",))
                pr.op("pe", lambda e: e.matmul(
                    pb0[0:P, 3 * P:3 * P + EXP], s_lg[:], s_id[0:EXP, 0:EXP],
                    start=True, stop=True), reads=("lg", "id"), writes=("pb0",))
                pr.op("dve", lambda e: e.tensor_reduce(
                    s_col3[:, 0:1], pb0[0:P, 3 * P:3 * P + EXP], AX, ALU.max,
                    negate=True), reads=("pb0",), writes=("col3",))
                pr.op("act", lambda e: e.activation(
                    s_e[:], pb0[0:P, 3 * P:3 * P + EXP], AF.Exp,
                    bias=s_col3[:, 0:1], scale=1.0),
                    reads=("pb0", "col3"), writes=("e",))
                pr.op("dve", lambda e: e.tensor_reduce(
                    s_col3[:, 1:2], s_e[:], AX, ALU.add), reads=("e",), writes=("col3",))
                pr.op("dve", lambda e: e.max(out=s_m8[:], in_=s_e[:]),
                      reads=("e",), writes=("m8",))
                pr.op("dve", lambda e: e.match_replace(
                    out=s_zap[:], in_to_replace=s_m8[:], in_values=s_e[:],
                    imm_value=0.0), reads=("m8", "e"), writes=("zap", "m8"))
                pr.op("dve", lambda e: e.tensor_tensor(
                    s_pm[:], s_e[:], s_zap[:], ALU.subtract),
                    reads=("e", "zap"), writes=("pm",))
                pr.op("dve", lambda e: e.tensor_reduce(
                    s_col3[:, 2:3], s_pm[:], AX, ALU.add), reads=("pm",), writes=("col3",))
                pr.op("dve", lambda e: e.reciprocal(s_col3[:, 3:4], s_col3[:, 1:2]),
                      reads=("col3",), writes=("col3",))
                pr.op("dve", lambda e: e.scalar_tensor_tensor(
                    s_ws[:], s_col3[:, 2:3], 0.5, s_col3[:, 3:4], ALU.mult, ALU.mult),
                    reads=("col3",), writes=("ws",))
                pr.op("dve", lambda e: e.tensor_scalar(
                    s_col3[:, 4:5], s_col3[:, 3:4], 0.5, None, ALU.mult),
                    reads=("col3",), writes=("col3",))
                pr.op("dve", lambda e: e.tensor_scalar(
                    s_pms[:], s_pm[:], s_col3[:, 4:5], None, ALU.mult),
                    reads=("pm", "col3"), writes=("pms",))
                pr.op("pe", lambda e: e.matmul(
                    pb7[0:EXP, 0:P], s_pms[:], s_idb[:], start=True, stop=True),
                    reads=("pms", "idb"), writes=("pb7",))
                pr.op("dve", lambda e: e.tensor_copy(s_pmT[:], pb7[0:EXP, 0:P]),
                      reads=("pb7",), writes=("pmT",))
                pr.op("pe", lambda e: e.matmul(
                    pb7[0:1, 0:P], s_ws[:], s_id[:], start=True, stop=True),
                    reads=("ws", "id"), writes=("pb7",))
                pr.op("dve", lambda e: e.tensor_copy(s_wsrow[:], pb7[0:1, 0:P]),
                      reads=("pb7",), writes=("wsrow",))
                pr.op("pe", lambda e: e.matmul(
                    pb0[:, 0:P], s_ones1[:], s_wsrow[:], start=True, stop=True),
                    reads=("wsrow", "ones1"), writes=("pb0",))
                pr.op("dve", lambda e: e.tensor_copy(s_wsr[:, 0, :], pb0[:, 0:P]),
                      reads=("pb0",), writes=("wsr",))
                pr.op("dve", lambda e: e.tensor_tensor(
                    s_sq3[:].rearrange("p (k t) -> p k t", k=NT),
                    s_y32[:].rearrange("p (k t) -> p k t", k=NT),
                    s_wsr[:].to_broadcast([P, NT, P]), ALU.mult),
                    reads=("y32", "wsr"), writes=("sq3",))
                pr.op("dve", lambda e: e.tensor_tensor(
                    s_x3[:], s_x[:], s_sq3[:], ALU.add),
                    reads=("x", "sq3"), writes=("x3",))
                for k in range(NT):
                    bank = pb1 if k % 2 == 0 else pb2
                    bkey = "pb1" if k % 2 == 0 else "pb2"
                    pr.op("pe", lambda e, k=k, bank=bank: e.matmul(
                        bank[0:P, 0:P], s_eb[:, k * P:(k + 1) * P], s_pmT[:],
                        start=True, stop=True),
                        reads=("eb", "pmT"), writes=(bkey,))
                    pr.op("dve", lambda e, k=k, bank=bank: e.tensor_tensor(
                        s_x3[:, k * P:(k + 1) * P], s_x3[:, k * P:(k + 1) * P],
                        bank[0:P, 0:P], ALU.add),
                        reads=(bkey, "x3"), writes=("x3",))
                pr.dma("sync", outr[:, :, j * P:(j + 1) * P],
                       s_x3[:].rearrange("p (k t) -> p k t", t=P), reads=("x3",))
            pr.finalize(block)
    return nc


P34_OUTS = ["o_out"]


def kernel(**inputs):
    maps1 = _prep_p12(inputs)
    nc1 = _build_p12()
    res1 = run_bass_kernel_spmd(nc1, maps1, core_ids=list(range(8))).results
    maps2 = _prep_p34(inputs, res1)
    nc2 = _build_p34()
    res2 = run_bass_kernel_spmd(nc2, [maps2[c] for c in range(8)],
                                core_ids=list(range(8))).results
    out = np.zeros((2, S, D), dtype=np.float32)
    for c in range(8):
        b, r = c // 4, c % 4
        out[b, _own_positions(r)] = res2[c]["o_out"].T
    return out


# version 12
# speedup vs baseline: 1.1668x; 1.0816x over previous
import numpy as np
import ml_dtypes
from contextlib import ExitStack

import concourse.bass as bass
import concourse.mybir as mybir
from concourse.bass_utils import run_bass_kernel_spmd
from concourse.bass import IndirectOffsetOnAxis

F32 = mybir.dt.float32
BF16 = mybir.dt.bfloat16
U32 = mybir.dt.uint32
AF = mybir.ActivationFunctionType
ALU = mybir.AluOpType
AX = mybir.AxisListType.X

P = 128
D = 4096
S = 4096
NT = 32
NJ = 8
TOK = 1024
H = 8
G = 4
QH = 32
DH = 128
CB = 16
NBLK = 256
LBLK = 64
CDIM = 2048
EXP = 64
SCALE = float(DH) ** -0.5
EPS = 1e-5
NEG = -3.0e38
QCOLS = 512

_PERM = np.concatenate([np.arange(0, DH, 2), np.arange(1, DH, 2)])


class Prog:
    QUEUES = ("sync", "gp")
    ENGINES = ("sync", "gp", "pe", "act", "dve")
    NQSEM = 8

    def __init__(self, nc, ctx):
        self.nc = nc
        self.sems = {e: ctx.enter_context(nc.semaphore("sem_" + e))
                     for e in ("pe", "act", "dve")}
        self.qsems = {q: [ctx.enter_context(nc.semaphore(f"sem_{q}{i}"))
                          for i in range(self.NQSEM)] for q in self.QUEUES}
        self.quse = {q: [0] * self.NQSEM for q in self.QUEUES}
        self.qseq = {q: 0 for q in self.QUEUES}
        self.count = {e: 0 for e in ("pe", "act", "dve")}
        self.stream = {e: [] for e in self.ENGINES}
        self.lastw = {}
        self.readers = {}
        self.waited = {e: {} for e in self.ENGINES}

    def _need(self, eng, token):
        dep_eng, sem, val = token
        if eng == dep_eng == "pe":
            return
        if eng == dep_eng and eng not in self.QUEUES:
            pass
        sid = sem.num
        if val <= self.waited[eng].get(sid, 0):
            return
        self.waited[eng][sid] = val
        self.stream[eng].append(lambda e, s=sem, v=val: e.wait_ge(s, v))

    def _deps(self, eng, reads, writes):
        for k in reads:
            if k in self.lastw:
                self._need(eng, self.lastw[k])
        for k in writes:
            if k in self.lastw:
                self._need(eng, self.lastw[k])
            for tok in self.readers.get(k, {}).values():
                self._need(eng, tok)

    def _record(self, token, reads, writes):
        for k in writes:
            self.lastw[k] = token
            self.readers[k] = {}
        for k in reads:
            self.readers.setdefault(k, {})[token[1].num] = token

    def op(self, eng, fn, reads=(), writes=()):
        assert eng not in self.QUEUES
        self._deps(eng, reads, writes)
        self.count[eng] += 1
        cnt = self.count[eng]
        sem = self.sems[eng]
        self.stream[eng].append(lambda e, f=fn, s=sem: f(e).then_inc(s, 1))
        self._record((eng, sem, cnt), reads, writes)
        return cnt

    def qop(self, q, fn, reads=(), writes=()):
        self._deps(q, reads, writes)
        i = self.qseq[q] % self.NQSEM
        self.qseq[q] += 1
        sem = self.qsems[q][i]
        if self.quse[q][i] > 0:
            self._need(q, (q, sem, 16 * self.quse[q][i]))
        self.quse[q][i] += 1
        val = 16 * self.quse[q][i]
        self.stream[q].append(
            lambda e, f=fn, sm=sem: f(e).then_inc(sm, 16))
        self._record((q, sem, val), reads, writes)
        return val

    def dma(self, q, out, in_, reads=(), writes=(), **kw):
        return self.qop(q, lambda e, o=out, s=in_, kw=kw:
                        e.dma_start(out=o, in_=s, **kw), reads, writes)

    def _alltokens(self):
        toks = [(e, self.sems[e], self.count[e]) for e in ("pe", "act", "dve")
                if self.count[e] > 0]
        for q in self.QUEUES:
            for i in range(self.NQSEM):
                if self.quse[q][i] > 0:
                    toks.append((q, self.qsems[q][i], 16 * self.quse[q][i]))
        return toks

    def barrier(self):
        toks = self._alltokens()
        for e in self.ENGINES:
            for tok in toks:
                if tok[0] != e or e in self.QUEUES:
                    self._need(e, tok)

    def finalize(self, block):
        self.barrier()
        for i in range(self.NQSEM):
            for q in self.QUEUES:
                if self.quse[q][i] > 0:
                    self._need("sync", (("x",), self.qsems[q][i],
                                        16 * self.quse[q][i])[0:3]
                               if False else
                               ("_", self.qsems[q][i], 16 * self.quse[q][i]))

        def runner(items):
            def go(e):
                for f in items:
                    f(e)
            return go

        block.sync(runner(self.stream["sync"]))
        block.gpsimd(runner(self.stream["gp"]))
        block.tensor(runner(self.stream["pe"]))
        block.scalar(runner(self.stream["act"]))
        block.vector(runner(self.stream["dve"]))


def _rope_tables():
    inv = 1.0 / (10000.0 ** (np.arange(0, DH, 2, dtype=np.float64) / DH))
    ang = np.arange(S, dtype=np.float64)[:, None] * inv[None, :]
    return (np.cos(ang).T.astype(np.float32), np.sin(ang).T.astype(np.float32))


def _own_positions(r):
    tiles = np.arange(NJ) * 4 + r
    return (tiles[:, None] * P + np.arange(P)[None, :]).reshape(-1)


def _prep_p12(inputs):
    x = np.asarray(inputs["L0_attn_hidden_states"])
    w_qkv = np.asarray(inputs["L0_attn_w_qkv"])
    k_w1 = np.asarray(inputs["L0_attn_k_w1"]); k_b1 = np.asarray(inputs["L0_attn_k_b1"])
    k_w2 = np.asarray(inputs["L0_attn_k_w2"]); k_b2 = np.asarray(inputs["L0_attn_k_b2"])
    v_w1 = np.asarray(inputs["L0_attn_v_w1"]); v_b1 = np.asarray(inputs["L0_attn_v_b1"])
    v_w2 = np.asarray(inputs["L0_attn_v_w2"]); v_b2 = np.asarray(inputs["L0_attn_v_b2"])
    k_pos = np.asarray(inputs["L0_attn_k_pos"]); v_pos = np.asarray(inputs["L0_attn_v_pos"])
    w_gate = np.asarray(inputs["L0_attn_w_gate"]); b_gate = np.asarray(inputs["L0_attn_b_gate"])
    rw = np.asarray(inputs["L0_ffn_router_weight"]); rb = np.asarray(inputs["L0_ffn_router_bias"])
    eb = np.asarray(inputs["L0_ffn_expert_bias"])

    qk_perm = np.arange(6144)
    for qh in range(QH):
        qk_perm[qh * DH:(qh + 1) * DH] = qh * DH + _PERM
    for kh in range(H):
        base = QH * DH + kh * DH
        qk_perm[base:base + DH] = base + _PERM
    wqkvT = np.ascontiguousarray(w_qkv[qk_perm].T)

    in_perm_k = (np.arange(CB)[:, None] * DH + _PERM[None, :]).reshape(-1)
    cos, sin = _rope_tables()
    shared = dict(
        wqkvT=wqkvT,
        kw1T=np.ascontiguousarray(k_w1[:, in_perm_k].T),
        vw1T=np.ascontiguousarray(v_w1.T),
        kw2T=np.ascontiguousarray(k_w2[_PERM].T),
        vw2T=np.ascontiguousarray(v_w2.T),
        kb1=np.ascontiguousarray(k_b1.reshape(16, 128).T),
        vb1=np.ascontiguousarray(v_b1.reshape(16, 128).T),
        kb2=k_b2[_PERM].reshape(DH, 1).copy(), vb2=v_b2.reshape(DH, 1).copy(),
        kposT=np.ascontiguousarray(k_pos[:, :, _PERM].transpose(2, 0, 1).reshape(DH, H * CB)),
        vposT=np.ascontiguousarray(v_pos.transpose(2, 0, 1).reshape(DH, H * CB)),
        wgateT=np.ascontiguousarray(w_gate.T), bgate=b_gate.reshape(96, 1).copy(),
        rwT=np.ascontiguousarray(rw.T), rbias=rb.reshape(EXP, 1).copy(), ebias=eb,
        ident=np.eye(P, dtype=np.float32), identb=np.eye(P, dtype=ml_dtypes.bfloat16),
    )
    maps = []
    for c in range(8):
        b, r = c // 4, c % 4
        pos = _own_positions(r)
        m = dict(shared)
        m.update(xT=np.ascontiguousarray(x[b][pos].T),
                 cosT=np.ascontiguousarray(cos[:, pos]),
                 sinT=np.ascontiguousarray(sin[:, pos]))
        maps.append(m)
    return maps


def _build_p12():
    nc = bass.Bass("TRN2", target_bir_lowering=False)
    d_ = lambda n, sh, dt=F32, k="ExternalInput": nc.dram_tensor(n, sh, dt, kind=k).ap()
    xT = d_("xT", [D, TOK])
    wqkvT = d_("wqkvT", [D, 6144])
    kw1T = d_("kw1T", [CDIM, CDIM]); vw1T = d_("vw1T", [CDIM, CDIM])
    kw2T = d_("kw2T", [CDIM, DH]); vw2T = d_("vw2T", [CDIM, DH])
    kb1 = d_("kb1", [P, 16]); vb1 = d_("vb1", [P, 16])
    kb2 = d_("kb2", [DH, 1]); vb2 = d_("vb2", [DH, 1])
    kposT = d_("kposT", [DH, H * CB]); vposT = d_("vposT", [DH, H * CB])
    wgateT = d_("wgateT", [D, 96]); bgate = d_("bgate", [96, 1])
    rwT = d_("rwT", [D, EXP]); rbias = d_("rbias", [EXP, 1]); ebias = d_("ebias", [EXP, D])
    ident = d_("ident", [P, P]); identb = d_("identb", [P, P], BF16)
    cosT = d_("cosT", [64, TOK]); sinT = d_("sinT", [64, TOK])

    o_x1 = d_("o_x1", [D, TOK], F32, "ExternalOutput")
    o_q = d_("o_q", [H, DH, G * TOK], BF16, "ExternalOutput")
    o_rq = d_("o_rq", [H, DH, G * TOK], BF16, "ExternalOutput")
    o_qsum = d_("o_qsum", [H, DH, TOK], BF16, "ExternalOutput")
    o_gates = d_("o_gates", [96, TOK], F32, "ExternalOutput")
    o_rk = d_("o_rk", [H, DH, TOK], BF16, "ExternalOutput")
    o_ktok = d_("o_ktok", [TOK, H * DH], BF16, "ExternalOutput")
    o_vtok = d_("o_vtok", [TOK, H * DH], BF16, "ExternalOutput")
    o_ck = d_("o_ck", [H, LBLK, DH], BF16, "ExternalOutput")
    o_cv = d_("o_cv", [H, LBLK, DH], BF16, "ExternalOutput")

    with ExitStack() as ctx:
        sb_ = lambda n, sh, dt=F32: ctx.enter_context(nc.sbuf_tensor(n, sh, dt))
        s_y16 = sb_("s_y16", [P, NT * TOK], BF16)
        s_rw = sb_("s_rw", [P, NT * EXP], BF16)
        s_eb = sb_("s_eb", [EXP, D], BF16)
        s_wg = sb_("s_wg", [P, NT * 96], BF16)
        s_id = sb_("s_id", [P, P]); s_idb = sb_("s_idb", [P, P], BF16)
        s_ones1 = sb_("s_ones1", [1, P]); s_onesP = sb_("s_onesP", [P, 1])
        s_rb = sb_("s_rb", [EXP, 1]); s_bg = sb_("s_bg", [96, 1])
        s_kb1 = sb_("s_kb1", [P, 16]); s_vb1 = sb_("s_vb1", [P, 16])
        s_kb2 = sb_("s_kb2", [DH, 1]); s_vb2 = sb_("s_vb2", [DH, 1])
        s_kpos = sb_("s_kpos", [DH, H * CB]); s_vpos = sb_("s_vpos", [DH, H * CB])
        s_cos = sb_("s_cos", [P, TOK]); s_sin = sb_("s_sin", [P, TOK])
        s_eps = sb_("s_eps", [1, 1])

        ps_ = lambda n: ctx.enter_context(nc.psum_tensor(n, [P, 512], F32))
        pb = [ps_(f"pb{i}") for i in range(8)]

        pr = Prog(nc, ctx)
        block = ctx.enter_context(nc.Block())

        for dst, src, key in [
            (s_rb[:], rbias, "rb"), (s_bg[:], bgate, "bg"),
            (s_kb1[:], kb1, "kb1"), (s_vb1[:], vb1, "vb1"),
            (s_kb2[:], kb2, "kb2"), (s_vb2[:], vb2, "vb2"),
            (s_kpos[:], kposT, "kpos"), (s_vpos[:], vposT, "vpos"),
            (s_cos[0:64, :], cosT, "cos"), (s_sin[0:64, :], sinT, "sin"),
            (s_cos[64:128, :], cosT, "cos"), (s_sin[64:128, :], sinT, "sin"),
            (s_id[:], ident, "id"), (s_idb[:], identb, "idb"),
        ]:
            pr.dma("sync", dst, src, writes=(key,))
        pr.dma("gp", s_rw[:].rearrange("p (k e) -> p k e", e=EXP),
               rwT.rearrange("(k p) e -> p k e", p=P), writes=("rw",))
        pr.dma("gp", s_eb[:], ebias, writes=("eb",))
        pr.dma("gp", s_wg[:].rearrange("p (k e) -> p k e", e=96),
               wgateT.rearrange("(k p) e -> p k e", p=P), writes=("wg",))
        pr.op("dve", lambda e: e.memset(s_ones1[:], 1.0), writes=("ones1",))
        pr.op("dve", lambda e: e.memset(s_eps[:], EPS), writes=("eps",))
        pr.op("dve", lambda e: e.memset(s_onesP[:], 1.0), writes=("onesP",))

        xTr = xT.rearrange("(k p) t -> p k t", p=P)
        x1r = o_x1.rearrange("(k p) t -> p k t", p=P)

        with ExitStack() as c1:
            sb1 = lambda n, sh, dt=F32: c1.enter_context(nc.sbuf_tensor(n, sh, dt))
            s_x = sb1("s_x", [P, D])
            s_sq = sb1("s_sq", [P, D])
            s_y32 = sb1("s_y32", [P, D])
            s_y16r = sb1("s_y16r", [P, D], BF16)
            s_x1 = sb1("s_x1", [P, D])
            s_stat = sb1("s_stat", [1, 512])
            s_rep = sb1("s_rep", [P, 1, 2 * P])
            s_lg = sb1("s_lg", [EXP, P])
            s_e = sb1("s_e", [P, EXP])
            s_col = sb1("s_col", [P, 8])
            s_m8 = sb1("s_m8", [P, 8])
            s_zap = sb1("s_zap", [P, EXP])
            s_pm = sb1("s_pm", [P, EXP])
            s_pms = sb1("s_pms", [P, EXP], BF16)
            s_ws = sb1("s_ws", [P, 1])
            s_pmT = sb1("s_pmT", [EXP, P], BF16)
            s_wsrow = sb1("s_wsrow", [1, P])
            s_wsr = sb1("s_wsr", [P, 1, P])

            def layer_norm(src_key, src_sb, out16, out16_key):
                pr.op("act", lambda e: e.activation(s_sq[:], src_sb[:], AF.Square),
                      reads=(src_key,), writes=("sq",))
                def sums(e):
                    for k in range(NT):
                        i = e.matmul(pb[0][0:1, 0:P], s_onesP[:],
                                     src_sb[:, k * P:(k + 1) * P],
                                     start=(k == 0), stop=(k == NT - 1))
                    return i
                pr.op("pe", sums, reads=(src_key, "onesP"), writes=("pb0",))
                def sqsums(e):
                    for k in range(NT):
                        i = e.matmul(pb[0][0:1, P:2 * P], s_onesP[:],
                                     s_sq[:, k * P:(k + 1) * P],
                                     start=(k == 0), stop=(k == NT - 1))
                    return i
                pr.op("pe", sqsums, reads=("sq", "onesP"), writes=("pb0",))
                pr.op("dve", lambda e: e.tensor_scalar(
                    s_stat[0:1, 0:P], pb[0][0:1, 0:P], 1.0 / D, None, ALU.mult),
                    reads=("pb0",), writes=("stat",))
                pr.op("dve", lambda e: e.tensor_tensor(
                    s_stat[0:1, P:2 * P], s_stat[0:1, 0:P], s_stat[0:1, 0:P],
                    ALU.mult), reads=("stat",), writes=("stat",))
                pr.op("dve", lambda e: e.scalar_tensor_tensor(
                    s_stat[0:1, 2 * P:3 * P], pb[0][0:1, P:2 * P], 1.0 / D,
                    s_stat[0:1, P:2 * P], ALU.mult, ALU.subtract),
                    reads=("pb0", "stat"), writes=("stat",))
                pr.op("act", lambda e: e.activation(
                    s_stat[0:1, 2 * P:3 * P], s_stat[0:1, 2 * P:3 * P], AF.Sqrt,
                    bias=s_eps[0:1, 0:1], scale=1.0),
                    reads=("stat", "eps"), writes=("stat",))
                pr.op("dve", lambda e: e.reciprocal(
                    s_stat[0:1, 3 * P:4 * P], s_stat[0:1, 2 * P:3 * P]),
                    reads=("stat",), writes=("stat",))
                def bcast(e):
                    e.matmul(pb[0][:, 0:P], s_ones1[:], s_stat[0:1, 0:P],
                             start=True, stop=True)
                    return e.matmul(pb[0][:, P:2 * P], s_ones1[:],
                                    s_stat[0:1, 3 * P:4 * P], start=True, stop=True)
                pr.op("pe", bcast, reads=("stat", "ones1"), writes=("pb0",))
                pr.op("dve", lambda e: e.tensor_copy(s_rep[:, 0, :], pb[0][:, 0:2 * P]),
                      reads=("pb0",), writes=("rep",))
                pr.op("dve", lambda e: e.tensor_tensor(
                    s_y32[:].rearrange("p (k t) -> p k t", k=NT),
                    src_sb[:].rearrange("p (k t) -> p k t", k=NT),
                    s_rep[:, :, 0:P].to_broadcast([P, NT, P]), ALU.subtract),
                    reads=(src_key, "rep"), writes=("y32",))
                pr.op("dve", lambda e: e.tensor_tensor(
                    s_y32[:].rearrange("p (k t) -> p k t", k=NT),
                    s_y32[:].rearrange("p (k t) -> p k t", k=NT),
                    s_rep[:, :, P:2 * P].to_broadcast([P, NT, P]), ALU.mult),
                    reads=("rep",), writes=("y32",))
                src3 = s_y32[:].rearrange("p (k t) -> p k t", t=P) \
                    if len(out16.shape) == 3 else s_y32[:]
                pr.op("dve", lambda e, src3=src3: e.tensor_copy(out16, src3),
                      reads=("y32",), writes=(out16_key,))

            for j in range(NJ):
                pr.dma("sync", s_x[:].rearrange("p (k t) -> p k t", t=P),
                       xTr[:, :, j * P:(j + 1) * P], writes=("x",))
                layer_norm("x", s_x, s_y16r[:], "y16r")
                def logits(e):
                    for k in range(NT):
                        i = e.matmul(pb[0][0:EXP, 2 * P:3 * P],
                                     s_rw[:, k * EXP:(k + 1) * EXP],
                                     s_y16r[:, k * P:(k + 1) * P],
                                     start=(k == 0), stop=(k == NT - 1))
                    return i
                pr.op("pe", logits, reads=("y16r", "rw"), writes=("pb0",))
                pr.op("act", lambda e: e.activation(
                    s_lg[:], pb[0][0:EXP, 2 * P:3 * P], AF.Identity,
                    bias=s_rb[:, 0:1], scale=1.0),
                    reads=("pb0", "rb"), writes=("lg",))
                pr.op("pe", lambda e: e.matmul(
                    pb[0][0:P, 3 * P:3 * P + EXP], s_lg[:], s_id[0:EXP, 0:EXP],
                    start=True, stop=True), reads=("lg", "id"), writes=("pb0",))
                pr.op("dve", lambda e: e.tensor_reduce(
                    s_col[:, 0:1], pb[0][0:P, 3 * P:3 * P + EXP], AX, ALU.max,
                    negate=True), reads=("pb0",), writes=("col",))
                pr.op("act", lambda e: e.activation(
                    s_e[:], pb[0][0:P, 3 * P:3 * P + EXP], AF.Exp,
                    bias=s_col[:, 0:1], scale=1.0),
                    reads=("pb0", "col"), writes=("e",))
                pr.op("dve", lambda e: e.tensor_reduce(
                    s_col[:, 1:2], s_e[:], AX, ALU.add), reads=("e",), writes=("col",))
                pr.op("dve", lambda e: e.max(out=s_m8[:], in_=s_e[:]),
                      reads=("e",), writes=("m8",))
                pr.op("dve", lambda e: e.match_replace(
                    out=s_zap[:], in_to_replace=s_m8[:], in_values=s_e[:],
                    imm_value=0.0), reads=("m8", "e"), writes=("zap", "m8"))
                pr.op("dve", lambda e: e.tensor_tensor(
                    s_pm[:], s_e[:], s_zap[:], ALU.subtract),
                    reads=("e", "zap"), writes=("pm",))
                pr.op("dve", lambda e: e.tensor_reduce(
                    s_col[:, 2:3], s_pm[:], AX, ALU.add), reads=("pm",), writes=("col",))
                pr.op("dve", lambda e: e.reciprocal(s_col[:, 3:4], s_col[:, 1:2]),
                      reads=("col",), writes=("col",))
                pr.op("dve", lambda e: e.scalar_tensor_tensor(
                    s_ws[:], s_col[:, 2:3], 0.5, s_col[:, 3:4], ALU.mult, ALU.mult),
                    reads=("col",), writes=("ws",))
                pr.op("dve", lambda e: e.tensor_scalar(
                    s_col[:, 4:5], s_col[:, 3:4], 0.5, None, ALU.mult),
                    reads=("col",), writes=("col",))
                pr.op("dve", lambda e: e.tensor_scalar(
                    s_pms[:], s_pm[:], s_col[:, 4:5], None, ALU.mult),
                    reads=("pm", "col"), writes=("pms",))
                pr.op("pe", lambda e: e.matmul(
                    pb[7][0:EXP, 0:P], s_pms[:], s_idb[:], start=True, stop=True),
                    reads=("pms", "idb"), writes=("pb7",))
                pr.op("dve", lambda e: e.tensor_copy(s_pmT[:], pb[7][0:EXP, 0:P]),
                      reads=("pb7",), writes=("pmT",))
                pr.op("pe", lambda e: e.matmul(
                    pb[7][0:1, 0:P], s_ws[:], s_id[:], start=True, stop=True),
                    reads=("ws", "id"), writes=("pb7",))
                pr.op("dve", lambda e: e.tensor_copy(s_wsrow[:], pb[7][0:1, 0:P]),
                      reads=("pb7",), writes=("wsrow",))
                pr.op("pe", lambda e: e.matmul(
                    pb[0][:, 0:P], s_ones1[:], s_wsrow[:], start=True, stop=True),
                    reads=("wsrow", "ones1"), writes=("pb0",))
                pr.op("dve", lambda e: e.tensor_copy(s_wsr[:, 0, :], pb[0][:, 0:P]),
                      reads=("pb0",), writes=("wsr",))
                pr.op("dve", lambda e: e.tensor_tensor(
                    s_sq[:].rearrange("p (k t) -> p k t", k=NT),
                    s_y32[:].rearrange("p (k t) -> p k t", k=NT),
                    s_wsr[:].to_broadcast([P, NT, P]), ALU.mult),
                    reads=("y32", "wsr"), writes=("sq",))
                pr.op("dve", lambda e: e.tensor_tensor(
                    s_x1[:], s_x[:], s_sq[:], ALU.add),
                    reads=("x", "sq"), writes=("x1",))
                for k in range(NT):
                    bank = 1 + (k % 2)
                    pr.op("pe", lambda e, k=k, bank=bank: e.matmul(
                        pb[bank][0:P, 0:P], s_eb[:, k * P:(k + 1) * P], s_pmT[:],
                        start=True, stop=True),
                        reads=("eb", "pmT"), writes=(f"pb{bank}",))
                    pr.op("dve", lambda e, k=k, bank=bank: e.tensor_tensor(
                        s_x1[:, k * P:(k + 1) * P], s_x1[:, k * P:(k + 1) * P],
                        pb[bank][0:P, 0:P], ALU.add),
                        reads=(f"pb{bank}", "x1"), writes=("x1",))
                pr.dma("sync", x1r[:, :, j * P:(j + 1) * P],
                       s_x1[:].rearrange("p (k t) -> p k t", t=P), reads=("x1",))
                y16j = s_y16[:].rearrange("p (k j t) -> p k j t", j=NJ, t=P)[:, :, j, :]
                layer_norm("x1", s_x1, y16j, ("y16", j))

            pr.barrier()

        y16keys = tuple(("y16", j) for j in range(NJ))

        with ExitStack() as c2:
            sb2 = lambda n, sh, dt=F32: c2.enter_context(nc.sbuf_tensor(n, sh, dt))
            s_w = [sb2(f"s_w{i}", [P, NT * P], BF16) for i in range(2)]
            s_qkv = sb2("s_qkv", [P, TOK], BF16)
            s_rope = sb2("s_rope", [P, TOK], BF16)
            s_ksc = sb2("s_ksc", [P, TOK], BF16)
            s_qs1 = sb2("s_qs1", [P, TOK], BF16)
            s_qs2 = sb2("s_qs2", [P, TOK], BF16)
            s_tr = sb2("s_tr", [P, TOK], BF16)
            s_cin = sb2("s_cin", [P, 2 * H * TOK], BF16)
            s_h1 = sb2("s_h1", [P, 16 * 512], BF16)
            s_co = sb2("s_co", [P, 512], BF16)
            s_gt = sb2("s_gt", [96, TOK])

            wqr = wqkvT.rearrange("(k p) c -> p k c", p=P)
            ktokr = o_ktok.rearrange("(j p) c -> p j c", p=P)
            vtokr = o_vtok.rearrange("(j p) c -> p j c", p=P)

            def qkv_coltile(ct):
                wbuf = s_w[ct % 2]
                pr.dma("gp", wbuf[:].rearrange("p (k c) -> p k c", c=P),
                       wqr[:, :, ct * P:(ct + 1) * P], writes=(f"wbuf{ct % 2}",))
                b0, b1 = (3, 4) if ct % 2 == 0 else (5, 6)
                def mms(e, b0=b0, b1=b1, wbuf=wbuf):
                    for k in range(NT):
                        e.matmul(pb[b0][:, 0:512], wbuf[:, k * P:(k + 1) * P],
                                 s_y16[:, k * TOK:k * TOK + 512],
                                 start=(k == 0), stop=(k == NT - 1))
                    for k in range(NT):
                        i = e.matmul(pb[b1][:, 0:512], wbuf[:, k * P:(k + 1) * P],
                                     s_y16[:, k * TOK + 512:(k + 1) * TOK],
                                     start=(k == 0), stop=(k == NT - 1))
                    return i
                pr.op("pe", mms, reads=(f"wbuf{ct % 2}",) + y16keys,
                      writes=(f"pb{b0}", f"pb{b1}"))
                return b0, b1

            def rope(src, src_key, dst, dst_key):
                x0 = src[0:64, :]; x1 = src[64:128, :]
                pr.op("dve", lambda e: e.tensor_tensor(
                    dst[0:64, :], x0, s_cos[0:64, :], ALU.mult),
                    reads=(src_key, "cos"), writes=(dst_key,))
                pr.op("dve", lambda e: e.tensor_tensor(
                    s_tr[0:64, :], x1, s_sin[64:128, :], ALU.mult),
                    reads=(src_key, "sin"), writes=("trs",))
                pr.op("dve", lambda e: e.tensor_tensor(
                    dst[0:64, :], dst[0:64, :], s_tr[0:64, :], ALU.subtract),
                    reads=(dst_key, "trs"), writes=(dst_key,))
                pr.op("dve", lambda e: e.tensor_tensor(
                    dst[64:128, :], x1, s_cos[64:128, :], ALU.mult),
                    reads=(src_key, "cos"), writes=(dst_key,))
                pr.op("dve", lambda e: e.tensor_tensor(
                    s_tr[64:128, :], x0, s_sin[0:64, :], ALU.mult),
                    reads=(src_key, "sin"), writes=("trs",))
                pr.op("dve", lambda e: e.tensor_tensor(
                    dst[64:128, :], dst[64:128, :], s_tr[64:128, :], ALU.add),
                    reads=(dst_key, "trs"), writes=(dst_key,))

            for ct in range(QH):
                h, g = ct // 4, ct % 4
                b0, b1 = qkv_coltile(ct)
                def ev(e, b0=b0, b1=b1):
                    e.activation(s_qkv[:, 0:512], pb[b0][:, 0:512], AF.Identity,
                                 scale=SCALE)
                    return e.activation(s_qkv[:, 512:1024], pb[b1][:, 0:512],
                                        AF.Identity, scale=SCALE)
                pr.op("act", ev, reads=(f"pb{b0}", f"pb{b1}"), writes=("qkv",))
                pr.dma("sync", o_q[h, :, g * TOK:(g + 1) * TOK], s_qkv[:],
                       reads=("qkv",))
                rope(s_qkv, "qkv", s_rope, "rope")
                pr.dma("sync", o_rq[h, :, g * TOK:(g + 1) * TOK], s_rope[:],
                       reads=("rope",))
                if g == 0:
                    pr.op("dve", lambda e: e.tensor_copy(s_qs1[:], s_qkv[:]),
                          reads=("qkv",), writes=("qs1",))
                elif g == 1:
                    pr.op("dve", lambda e: e.tensor_tensor(
                        s_qs1[:], s_qs1[:], s_qkv[:], ALU.add),
                        reads=("qkv", "qs1"), writes=("qs1",))
                elif g == 2:
                    pr.op("dve", lambda e: e.tensor_copy(s_qs2[:], s_qkv[:]),
                          reads=("qkv",), writes=("qs2",))
                else:
                    pr.op("dve", lambda e: e.tensor_tensor(
                        s_qs2[:], s_qs2[:], s_qkv[:], ALU.add),
                        reads=("qkv", "qs2"), writes=("qs2",))
                    pr.op("dve", lambda e: e.tensor_tensor(
                        s_qs1[:], s_qs1[:], s_qs2[:], ALU.add),
                        reads=("qs1", "qs2"), writes=("qs1",))
                    pr.op("dve", lambda e: e.tensor_scalar(
                        s_qs1[:], s_qs1[:], 0.25, None, ALU.mult),
                        reads=("qs1",), writes=("qs1",))
                    pr.dma("sync", o_qsum[h, :, :], s_qs1[:], reads=("qs1",))

            for ct in range(QH, QH + H):
                h = ct - QH
                b0, b1 = qkv_coltile(ct)
                def evk(e, b0=b0, b1=b1):
                    e.activation(s_ksc[:, 0:512], pb[b0][:, 0:512], AF.Identity)
                    return e.activation(s_ksc[:, 512:1024], pb[b1][:, 0:512],
                                        AF.Identity)
                pr.op("act", evk, reads=(f"pb{b0}", f"pb{b1}"), writes=("ksc",))
                rope(s_ksc, "ksc", s_rope, "rope")
                pr.dma("sync", o_rk[h, :, :], s_rope[:], reads=("rope",))
                for jt in range(NJ):
                    pr.op("pe", lambda e, jt=jt: e.matmul(
                        pb[7][:, 0:P], s_rope[:, jt * P:(jt + 1) * P], s_idb[:],
                        start=True, stop=True),
                        reads=("rope", "idb"), writes=("pb7",))
                    pr.op("dve", lambda e, jt=jt: e.tensor_copy(
                        s_tr[:, jt * P:(jt + 1) * P], pb[7][:, 0:P]),
                        reads=("pb7",), writes=("trs",))
                pr.dma("sync", ktokr[:, :, h * DH:(h + 1) * DH],
                       s_tr[:].rearrange("p (j c) -> p j c", j=NJ), reads=("trs",))
                for cb in range(CB):
                    off = h * TOK + cb * LBLK
                    pr.op("dve", lambda e, cb=cb, off=off, h=h: e.tensor_scalar(
                        s_cin[:, off:off + LBLK],
                        s_ksc[:].rearrange("p (blk c) -> p blk c", c=CB)[:, :, cb:cb + 1]
                        .rearrange("p b c -> p (b c)"),
                        s_kpos[:, h * CB + cb:h * CB + cb + 1], None, ALU.add),
                        reads=("ksc", "kpos"), writes=(("cin", 0, h),))

            for ct in range(QH + H, QH + 2 * H):
                h = ct - QH - H
                b0, b1 = qkv_coltile(ct)
                def evv(e, b0=b0, b1=b1):
                    e.activation(s_qkv[:, 0:512], pb[b0][:, 0:512], AF.Identity)
                    return e.activation(s_qkv[:, 512:1024], pb[b1][:, 0:512],
                                        AF.Identity)
                pr.op("act", evv, reads=(f"pb{b0}", f"pb{b1}"), writes=("qkv",))
                for jt in range(NJ):
                    pr.op("pe", lambda e, jt=jt: e.matmul(
                        pb[7][:, 0:P], s_qkv[:, jt * P:(jt + 1) * P], s_idb[:],
                        start=True, stop=True),
                        reads=("qkv", "idb"), writes=("pb7",))
                    pr.op("dve", lambda e, jt=jt: e.tensor_copy(
                        s_tr[:, jt * P:(jt + 1) * P], pb[7][:, 0:P]),
                        reads=("pb7",), writes=("trs",))
                pr.dma("sync", vtokr[:, :, h * DH:(h + 1) * DH],
                       s_tr[:].rearrange("p (j c) -> p j c", j=NJ), reads=("trs",))
                for cb in range(CB):
                    off = (H + h) * TOK + cb * LBLK
                    pr.op("dve", lambda e, cb=cb, off=off, h=h: e.tensor_scalar(
                        s_cin[:, off:off + LBLK],
                        s_qkv[:].rearrange("p (blk c) -> p blk c", c=CB)[:, :, cb:cb + 1]
                        .rearrange("p b c -> p (b c)"),
                        s_vpos[:, h * CB + cb:h * CB + cb + 1], None, ALU.add),
                        reads=("qkv", "vpos"), writes=(("cin", 1, h),))

            ckr = o_ck.rearrange("h b d -> b h d")
            cvr = o_cv.rearrange("h b d -> b h d")
            for side, w1, w2, bias1, bias2, outr in (
                    (0, kw1T, kw2T, s_kb1, s_kb2, ckr),
                    (1, vw1T, vw2T, s_vb1, s_vb2, cvr)):
                cink = tuple(("cin", side, h) for h in range(H))
                w1r = w1.rearrange("(c p) o -> p c o", p=P)
                base = side * H * TOK
                rhs_ap = lambda c, side=side: s_cin[:].rearrange(
                    "p (s h cb blk) -> p s h cb blk", s=2, h=H, cb=CB) \
                    [:, side, :, c, :]
                for o in range(16):
                    wb = s_w[o % 2]
                    pr.dma("gp", wb[:, 0:CDIM].rearrange("p (c o) -> p c o", o=P),
                           w1r[:, :, o * P:(o + 1) * P], writes=(f"wbuf{o % 2}",))
                    bank = 3 + (o % 2)
                    def m1(e, o=o, wb=wb, bank=bank, rhs_ap=rhs_ap):
                        for c in range(16):
                            i = e.matmul(pb[bank][:, 0:512],
                                         wb[:, c * P:(c + 1) * P], rhs_ap(c),
                                         start=(c == 0), stop=(c == 15))
                        return i
                    pr.op("pe", m1, reads=(f"wbuf{o % 2}",) + cink,
                          writes=(f"pb{bank}",))
                    pr.op("act", lambda e, o=o, bank=bank, bias1=bias1: e.activation(
                        s_h1[:, o * 512:(o + 1) * 512], pb[bank][:, 0:512],
                        AF.Relu, bias=bias1[:, o:o + 1], scale=1.0),
                        reads=(f"pb{bank}", "kb1" if side == 0 else "vb1"),
                        writes=("h1",))
                w2r = w2.rearrange("(c p) o -> p c o", p=P)
                pr.dma("gp", s_w[0][:, 0:16 * DH].rearrange("p (c o) -> p c o", o=DH),
                       w2r, writes=("wbuf0",))
                def m2(e):
                    for c in range(16):
                        i = e.matmul(pb[5][:, 0:512],
                                     s_w[0][:, c * DH:(c + 1) * DH],
                                     s_h1[:, c * 512:(c + 1) * 512],
                                     start=(c == 0), stop=(c == 15))
                    return i
                pr.op("pe", m2, reads=("wbuf0", "h1"), writes=("pb5",))
                pr.op("act", lambda e, bias2=bias2: e.activation(
                    s_co[:], pb[5][:, 0:512], AF.Identity,
                    bias=bias2[:, 0:1], scale=1.0),
                    reads=("pb5", "kb2" if side == 0 else "vb2"), writes=("co",))
                for h in range(H):
                    pr.op("pe", lambda e, h=h: e.matmul(
                        pb[7][0:64, 0:P], s_co[:, h * 64:(h + 1) * 64], s_idb[:],
                        start=True, stop=True),
                        reads=("co", "idb"), writes=("pb7",))
                    pr.op("dve", lambda e, h=h: e.tensor_copy(
                        s_tr[0:64, h * DH:(h + 1) * DH], pb[7][0:64, 0:P]),
                        reads=("pb7",), writes=("trs",))
                pr.dma("sync", outr,
                       s_tr[0:64, 0:H * DH].rearrange("p (h d) -> p h d", h=H),
                       reads=("trs",))

            def gmm(e):
                for k in range(NT):
                    e.matmul(pb[1][0:96, 0:512], s_wg[:, k * 96:(k + 1) * 96],
                             s_y16[:, k * TOK:k * TOK + 512],
                             start=(k == 0), stop=(k == NT - 1))
                for k in range(NT):
                    i = e.matmul(pb[2][0:96, 0:512], s_wg[:, k * 96:(k + 1) * 96],
                                 s_y16[:, k * TOK + 512:(k + 1) * TOK],
                                 start=(k == 0), stop=(k == NT - 1))
                return i
            pr.op("pe", gmm, reads=("wg",) + y16keys, writes=("pb1", "pb2"))
            def gev(e):
                e.activation(s_gt[:, 0:512], pb[1][0:96, 0:512], AF.Sigmoid,
                             bias=s_bg[:, 0:1], scale=1.0)
                return e.activation(s_gt[:, 512:1024], pb[2][0:96, 0:512],
                                    AF.Sigmoid, bias=s_bg[:, 0:1], scale=1.0)
            pr.op("act", gev, reads=("pb1", "pb2", "bg"), writes=("gt",))
            pr.dma("sync", o_gates, s_gt[:], reads=("gt",))

            pr.finalize(block)
    return nc


P12_OUTS = ["o_x1", "o_q", "o_rq", "o_qsum", "o_gates", "o_rk", "o_ktok",
            "o_vtok", "o_ck", "o_cv"]




def _prep_p34(inputs, res1, cores=tuple(range(8))):
    mem_kv = np.asarray(inputs["L0_attn_mem_kv"])
    w_o = np.asarray(inputs["L0_attn_w_o"])
    rw = np.asarray(inputs["L0_ffn_router_weight"]); rb = np.asarray(inputs["L0_ffn_router_bias"])
    eb = np.asarray(inputs["L0_ffn_expert_bias"])
    bf = ml_dtypes.bfloat16

    onehot = np.full((512, NBLK), NEG, dtype=bf)
    for jb in range(NBLK):
        onehot[jb + 1, jb] = 0.0
    E9 = np.zeros((9, P), dtype=bf)
    for kk in range(P):
        E9[kk // CB, kk] = 1.0
    E9[8, :] = 1.0
    e3 = np.zeros((P, 9), dtype=bf)
    for bb in range(3):
        e3[:, 3 * bb + bb] = 1.0
    e3w = np.zeros((3, 3 * P), dtype=bf)
    for bb in range(3):
        e3w[bb, bb * P:(bb + 1) * P] = 1.0
    gsel = np.zeros((96, 96), dtype=np.float32)
    for qh in range(QH):
        for bb in range(3):
            gsel[3 * qh + bb, qh * 3 + bb] = 1.0
    qloc = np.arange(P)
    wloc = np.arange(P)
    od = np.where((wloc[:, None] >= (qloc[None, :] // CB) * CB)
                  & (wloc[:, None] <= qloc[None, :]), 0.0, NEG).astype(np.float32)
    owndiag = np.tile(od, (1, 4)).reshape(P, 4 * P)
    owndiag = np.concatenate([od] * 4, axis=1).astype(bf)

    shared = dict(
        onehot=onehot, E9=E9, e3=e3, e3w=e3w, gsel=gsel, owndiag=owndiag,
        woT=np.ascontiguousarray(w_o.T),
        rwT=np.ascontiguousarray(rw.T), rbias=rb.reshape(EXP, 1).copy(), ebias=eb,
        ident=np.eye(P, dtype=np.float32), identb=np.eye(P, dtype=bf),
        memkP=np.ascontiguousarray(mem_kv[0, :, 0, :][:, _PERM].T),
        memv=np.ascontiguousarray(mem_kv[1, :, 0, :].reshape(1, H * DH)),
    )

    maps = {}
    for c in cores:
        b, r = c // 4, c % 4
        rkTf = np.zeros((H * DH, S), dtype=bf)
        vtok = np.zeros((S, H * DH), dtype=bf)
        ktok = np.zeros((S, H * DH), dtype=bf)
        ck_full = np.zeros((H, NBLK, DH), dtype=bf)
        cv_full = np.zeros((H, NBLK, DH), dtype=bf)
        for rr in range(4):
            rsrc = res1[4 * b + rr]
            pos = _own_positions(rr)
            rkTf[:, pos] = rsrc["o_rk"].reshape(H * DH, TOK)
            vtok[pos] = rsrc["o_vtok"]
            ktok[pos] = rsrc["o_ktok"]
            oblk = ((np.arange(NJ)[:, None] * 4 + rr) * NJ + np.arange(NJ)[None, :]).reshape(-1)
            ck_full[:, oblk] = rsrc["o_ck"]
            cv_full[:, oblk] = rsrc["o_cv"]
        ckTin = np.zeros((H * DH, 384), dtype=bf)
        for h in range(H):
            ckTin[h * DH:(h + 1) * DH, 1:NBLK + 1] = ck_full[h].T
        cvin = np.zeros((384, H * DH), dtype=bf)
        for h in range(H):
            cvin[1:NBLK + 1, h * DH:(h + 1) * DH] = cv_full[h]
        wk = np.zeros((NJ, 2 * P, H * DH), dtype=bf)
        wv = np.zeros((NJ, 2 * P, H * DH), dtype=bf)
        for j in range(NJ):
            T = 4 * j + r
            lo = T * P - P
            for w in range(2 * P):
                gw = lo + w
                if gw >= 0:
                    wk[j, w] = ktok[gw]
                    wv[j, w] = vtok[gw]
        cmask = np.zeros((NJ, 2, P, 4 * P), dtype=bf)
        impmask = np.zeros((NJ, P, 384), dtype=bf)
        swmask = np.zeros((NJ, 2, P, 4 * P), dtype=bf)
        for j in range(NJ):
            T = 4 * j + r
            qabs = T * P + np.arange(P)
            col = np.arange(384)
            ck_seq = np.where(col == 0, -1, 16 * col - 1)
            allow = (ck_seq[:, None] < qabs[None, :]) & (col[:, None] <= NBLK)
            cm = np.where(allow, 0.0, NEG).astype(np.float32)
            cmask[j] = np.tile(cm.reshape(3, P, 1, P), (1, 1, 4, 1)).reshape(3, P, 4 * P)[:2]
            col = np.arange(384)
            iallow = (col[None, :] >= 1) & (col[None, :] <= NBLK) \
                & (16 * col[None, :] - 1 < qabs[:, None])
            im = np.where(iallow, 0.0, NEG).astype(np.float32)
            im[:, NBLK + 1] = -1000.0
            impmask[j] = im
            kw = T * P - P + np.arange(2 * P)
            sallow = ((kw[:, None] >= qabs[None, :] - 64)
                      & (kw[:, None] <= qabs[None, :]) & (kw[:, None] >= 0))
            sm = np.where(sallow, 0.0, NEG).astype(np.float32)
            swmask[j] = np.tile(sm.reshape(2, P, 1, P), (1, 1, 4, 1)).reshape(2, P, 4 * P)

        rsrc = res1[c]
        m = dict(shared)
        m.update(
            x1T=rsrc["o_x1"], qT=rsrc["o_q"], rqT=rsrc["o_rq"],
            qsumT=rsrc["o_qsum"], gatesT=rsrc["o_gates"],
            rkTf=rkTf, vtokf=vtok, ckTin=ckTin, cvin=cvin,
            wk=np.ascontiguousarray(wk.reshape(NJ * 2 * P, H * DH)),
            wv=np.ascontiguousarray(wv.reshape(NJ * 2 * P, H * DH)),
            cmask=cmask, impmask=impmask, swmask=swmask,
        )
        maps[c] = m
    return maps


def _build_p34():
    nc = bass.Bass("TRN2", target_bir_lowering=False)
    d_ = lambda n, sh, dt=F32, k="ExternalInput": nc.dram_tensor(n, sh, dt, kind=k).ap()
    x1T = d_("x1T", [D, TOK])
    qT = d_("qT", [H, DH, G * TOK], BF16)
    rqT = d_("rqT", [H, DH, G * TOK], BF16)
    qsumT = d_("qsumT", [H, DH, TOK], BF16)
    gatesT = d_("gatesT", [96, TOK])
    rkTf = d_("rkTf", [H * DH, S], BF16)
    vtokf = d_("vtokf", [S, H * DH], BF16)
    ckTin = d_("ckTin", [H * DH, 384], BF16)
    cvin = d_("cvin", [384, H * DH], BF16)
    wk = d_("wk", [NJ * 2 * P, H * DH], BF16)
    wv = d_("wv", [NJ * 2 * P, H * DH], BF16)
    cmask = d_("cmask", [NJ, 2, P, 4 * P], BF16)
    impmask = d_("impmask", [NJ, P, 384], BF16)
    swmask = d_("swmask", [NJ, 2, P, 4 * P], BF16)
    owndiag = d_("owndiag", [P, 4 * P], BF16)
    onehot = d_("onehot", [512, NBLK], BF16)
    E9in = d_("E9", [9, P], BF16)
    e3in = d_("e3", [P, 9], BF16)
    e3w = d_("e3w", [3, 3 * P], BF16)
    gselin = d_("gsel", [96, 96])
    woT = d_("woT", [D, D])
    rwT = d_("rwT", [D, EXP]); rbias = d_("rbias", [EXP, 1]); ebias = d_("ebias", [EXP, D])
    ident = d_("ident", [P, P]); identb = d_("identb", [P, P], BF16)
    memkP = d_("memkP", [DH, H]); memv = d_("memv", [1, H * DH])

    attd = nc.dram_tensor("attd", [QH, DH, TOK], BF16, kind="Internal").ap()
    m01d = nc.dram_tensor("m01d", [2 * P, P], BF16, kind="Internal").ap()
    x2d = nc.dram_tensor("x2d", [D, TOK], F32, kind="Internal").ap()
    o_out = d_("o_out", [D, TOK], F32, "ExternalOutput")

    with ExitStack() as ctx:
        sb_ = lambda n, sh, dt=F32: ctx.enter_context(nc.sbuf_tensor(n, sh, dt))
        s_id = sb_("s_id", [P, P]); s_idb = sb_("s_idb", [P, P], BF16)
        s_ones1 = sb_("s_ones1", [1, P]); s_onesP = sb_("s_onesP", [P, 1])
        s_onesPb = sb_("s_onesPb", [P, 1], BF16)
        s_eps = sb_("s_eps", [1, 1])
        s_E9 = sb_("s_E9", [9, P], BF16)
        s_e3 = sb_("s_e3", [P, 9], BF16)
        s_e3w = sb_("s_e3w", [35, 3 * P], BF16)
        s_gsel = sb_("s_gsel", [96, 96])
        s_od = sb_("s_od", [P, 4 * P], BF16)
        s_gates = sb_("s_gates", [96, TOK])
        s_maxk = sb_("s_maxk", [1, 16])
        s_ones1b = sb_("s_ones1b", [1, P], BF16)
        s_rb = sb_("s_rb", [EXP, 1])

        pA = ctx.enter_context(nc.psum_tensor("pA", [P, 1024], F32))
        pB = ctx.enter_context(nc.psum_tensor("pB", [P, 1024], F32))
        pc = [ctx.enter_context(nc.psum_tensor(f"pc{i}", [P, 512], F32))
              for i in range(4)]
        pSc = lambda t: pA[:, (t % 2) * 512:(t % 2) * 512 + 512]
        pM = pB[:, 0:512]
        pL = pB[:, 512:1024]
        pLg = pB[32:35, 512:1024]
        pC, pF, pW, pI = pc[0], pc[1], pc[2], pc[3]

        pr = Prog(nc, ctx)
        block = ctx.enter_context(nc.Block())

        for dst, src, key in [
            (s_id[:], ident, "id"), (s_idb[:], identb, "idb"),
            (s_E9[:], E9in, "E9"), (s_e3[:], e3in, "e3"),
            (s_e3w[32:35, :], e3w, "e3w"), (s_gsel[:], gselin, "gsel"),
            (s_od[:], owndiag, "od"), (s_gates[:], gatesT, "gates"),
            (s_rb[:], rbias, "rb"),
        ]:
            pr.dma("sync", dst, src, writes=(key,))
        pr.op("dve", lambda e: e.memset(s_ones1[:], 1.0), writes=("ones1",))
        pr.op("dve", lambda e: e.memset(s_onesP[:], 1.0), writes=("onesP",))
        pr.op("dve", lambda e: e.memset(s_onesPb[:], 1.0), writes=("onesPb",))
        pr.op("dve", lambda e: e.memset(s_ones1b[:], 1.0), writes=("ones1b",))
        pr.op("dve", lambda e: e.memset(s_eps[:], EPS), writes=("eps",))

        with ExitStack() as c1:
            sb1 = lambda n, sh, dt=F32: c1.enter_context(nc.sbuf_tensor(n, sh, dt))
            s_rk = sb1("s_rk", [P, H * S], BF16)
            s_vt = sb1("s_vt", [P, NT * H * DH], BF16)
            s_ck = sb1("s_ck", [P, H * 384], BF16)
            s_cv = sb1("s_cv", [P, 3 * H * DH], BF16)
            s_sq = sb1("s_sq", [P, 512], BF16)
            s_qg = [sb1(f"s_qg{i}", [P, QCOLS], BF16) for i in range(2)]
            s_rqg = [sb1(f"s_rqg{i}", [P, QCOLS], BF16) for i in range(2)]
            s_qsj = sb1("s_qsj", [P, P], BF16)
            s_cm = sb1("s_cm", [P, 2 * 512], BF16)
            s_im = [sb1(f"s_im{i}", [P, 384], BF16) for i in range(2)]
            s_sm = sb1("s_sm", [P, 2 * 512], BF16)
            s_wk = sb1("s_wk", [P, H * DH], BF16)
            s_wv = sb1("s_wv", [P, 2 * H * DH], BF16)
            s_wkT = sb1("s_wkT", [P, H * 2 * DH], BF16)
            s_pc16 = [sb1(f"s_pc16_{i}", [P, 512], BF16) for i in range(2)]
            s_pf16 = [sb1(f"s_pf16_{i}", [P, 512], BF16) for i in range(2)]
            s_ps16 = sb1("s_ps16", [P, 2 * 512], BF16)
            s_po16 = sb1("s_po16", [P, 512], BF16)
            s_ei = sb1("s_ei", [P, 384])
            s_col = sb1("s_col", [P, 8])
            s_m8v = sb1("s_m8v", [P, 8])
            s_m8i = sb1("s_m8i", [P, 8], U32)
            s_sel = sb1("s_sel", [P, 4], U32)
            s_selv = sb1("s_selv", [P, 4], U32)
            s_mg = [sb1(f"s_mg{i}", [P, NBLK], BF16) for i in range(4)]
            s_mS = s_mg[0]
            s_m01T = s_mg[1]
            s_m01r = sb1("s_m01r", [8, NT * P], BF16)
            s_mb = [sb1(f"s_mb{i}", [9, 512], BF16) for i in range(2)]
            s_nrow = sb1("s_nrow", [1, 512])
            s_nmf = sb1("s_nmf", [1, 512])
            s_nmf16 = [sb1(f"s_nmf16_{i}", [1, 512], BF16) for i in range(2)]
            s_nmc16 = [sb1(f"s_nmc16_{i}", [1, 512], BF16) for i in range(2)]
            s_rl = sb1("s_rl", [35, 512])
            s_Wb = sb1("s_Wb", [35, 512], BF16)
            s_wrep = sb1("s_wrep", [P, 512], BF16)
            s_evc = sb1("s_evc", [P, 3 * 512], BF16)
            s_att = sb1("s_att", [P, 512], BF16)
            s_att16 = sb1("s_att16", [P, 512], BF16)
            s_row = sb1("s_row", [1, 8])

            for h in range(H):
                pr.dma("sync", s_rk[:, h * S:(h + 1) * S],
                       rkTf[h * DH:(h + 1) * DH, :], writes=(("rk", h),))
            pr.dma("sync", s_vt[:].rearrange("p (t c) -> p t c", c=H * DH),
                   vtokf.rearrange("(t p) c -> p t c", p=P), writes=("vt",))
            pr.dma("sync", s_ck[:].rearrange("p (h c) -> p h c", c=384),
                   ckTin.rearrange("(h p) c -> p h c", p=P), writes=("ckt",))
            pr.dma("sync", s_cv[:].rearrange("p (u c) -> p u c", c=H * DH),
                   cvin.rearrange("(u p) c -> p u c", p=P), writes=("cvt",))
            pr.dma("gp", s_ck[:].rearrange("p (h c) -> p h c", c=384)[:, :, 0:1],
                   memkP.rearrange("p (h o) -> p h o", o=1),
                   reads=("ckt",), writes=("ckt",),
                   allow_slow_non_contiguous=True)
            pr.dma("gp", s_cv[0:1, 0:H * DH], memv, reads=("cvt",), writes=("cvt",))

            pr.op("dve", lambda e: e.memset(s_maxk[:], 0.0), writes=("maxk",))
            def norm_pass(src_ap, width, colidx, key):
                for base in range(0, width, 512):
                    w = min(512, width - base)
                    pr.op("act", lambda e, base=base, w=w: e.activation(
                        s_sq[:, 0:w], src_ap[:, base:base + w], AF.Square),
                        reads=(key,), writes=("sq",))
                    for cb2 in range(0, w, 512):
                        cw = min(512, w - cb2)
                        pr.op("pe", lambda e, cb2=cb2, cw=cw: e.matmul(
                            pI[0:1, 0:cw], s_onesPb[:],
                            s_sq[:, cb2:cb2 + cw], start=True, stop=True),
                            reads=("sq", "onesPb"), writes=("pI",))
                        pr.op("dve", lambda e, cw=cw: e.tensor_reduce(
                            s_row[0:1, 0:1], pI[0:1, 0:cw], AX, ALU.max),
                            reads=("pI",), writes=("rowx",))
                        pr.op("dve", lambda e, colidx=colidx: e.tensor_tensor(
                            s_maxk[0:1, colidx:colidx + 1],
                            s_maxk[0:1, colidx:colidx + 1],
                            s_row[0:1, 0:1], ALU.max),
                            reads=("rowx", "maxk"), writes=("maxk",))
            for h in range(H):
                norm_pass(s_rk[:, h * S:(h + 1) * S], S, h, ("rk", h))
                norm_pass(s_ck[:, h * 384:(h + 1) * 384], 384, 8 + h, "ckt")

            units = [(j, h) for j in range(NJ) for h in range(H)]

            def j_boundary(j):
                pr.dma("sync", s_cm[:].rearrange("p (t c) -> p t c", c=512),
                       cmask[j].rearrange("t p c -> p t c"), writes=("cmj",))
                pr.dma("sync", s_sm[:].rearrange("p (t c) -> p t c", c=512),
                       swmask[j].rearrange("u p c -> p u c"), writes=("smj",))
                pr.dma("sync", s_wv[:].rearrange("p (u c) -> p u c", c=H * DH),
                       wv.rearrange("(j u p) c -> j p u c", j=NJ, p=P)[j],
                       writes=("wvbuf",))
                for u in range(2):
                    pr.dma("sync", s_wk[:],
                           wk.rearrange("(j u p) c -> j u p c", j=NJ, p=P)[j, u],
                           writes=("wkbuf",))
                    for h in range(H):
                        pr.op("pe", lambda e, h=h: e.matmul(
                            pM[:, 0:P], s_wk[:, h * DH:(h + 1) * DH],
                            s_idb[:], start=True, stop=True),
                            reads=("wkbuf", "idb"), writes=("pM",))
                        pr.op("dve", lambda e, h=h, u=u: e.tensor_copy(
                            s_wkT[:, (h * 2 + u) * DH:(h * 2 + u + 1) * DH],
                            pM[:, 0:P]), reads=("pM",), writes=("wkT",))

            def sel_stage(j, h):
                if h == 0:
                    pr.dma("sync", s_im[j % 2][:], impmask[j],
                           writes=(f"imj{j % 2}",))
                pr.dma("sync", s_qsj[:], qsumT[h, :, j * P:(j + 1) * P],
                       writes=("qsj",))
                pr.op("pe", lambda e, h=h: e.matmul(
                    pI[:, 0:384], s_qsj[:], s_ck[:, h * 384:(h + 1) * 384],
                    start=True, stop=True),
                    reads=("qsj", "ckt"), writes=("pI",))
                pr.op("dve", lambda e, j=j: e.tensor_tensor(
                    pI[:, 0:384], pI[:, 0:384], s_im[j % 2][:], ALU.add),
                    reads=(f"imj{j % 2}", "pI"), writes=("pI",))
                pr.op("dve", lambda e: e.tensor_reduce(
                    s_col[:, 0:1], pI[:, 0:384], AX, ALU.max, negate=True),
                    reads=("pI",), writes=("col",))
                pr.op("act", lambda e: e.activation(
                    s_ei[:], pI[:, 0:384], AF.Exp, bias=s_col[:, 0:1],
                    scale=1.0), reads=("pI", "col"), writes=("ei",))
                pr.op("dve", lambda e: e.tensor_reduce(
                    s_col[:, 1:2], s_ei[:], AX, ALU.add),
                    reads=("ei",), writes=("col",))
                pr.op("dve", lambda e: e.max(out=s_m8v[:], in_=s_ei[:]),
                      reads=("ei",), writes=("m8v",))
                pr.op("dve", lambda e: e.max_index(
                    out=s_m8i[:], in_max=s_m8v[:], in_values=s_ei[:]),
                    reads=("m8v", "ei"), writes=("m8i",))
                pr.op("dve", lambda e: e.tensor_scalar(
                    s_col[:, 2:3], s_col[:, 1:2], 1e-10, None, ALU.mult),
                    reads=("col",), writes=("col",))
                pr.op("dve", lambda e: e.tensor_scalar(
                    s_selv[:], s_m8v[:, 0:4], s_col[:, 2:3], None, ALU.is_gt),
                    reads=("m8v", "col"), writes=("selv",))
                pr.op("dve", lambda e: e.memset(s_sel[:], 0), writes=("sel",))
                pr.op("dve", lambda e: e.copy_predicated(
                    s_sel[:], s_selv[:], s_m8i[:, 0:4]),
                    reads=("selv", "m8i", "sel"), writes=("sel",))
                for ss in range(4):
                    pr.qop("gp", lambda e, ss=ss: e.indirect_dma_start(
                        s_mg[ss][:], None, onehot,
                        IndirectOffsetOnAxis(s_sel[:, ss:ss + 1], 0)),
                        reads=("sel",), writes=(f"mg{ss}",))

            def sel_stage_b():
                pr.op("dve", lambda e: e.tensor_tensor(
                    s_mS[:], s_mg[0][:], s_mg[1][:], ALU.max),
                    reads=("mg0", "mg1"), writes=("mg0",))
                pr.op("dve", lambda e: e.tensor_tensor(
                    s_mS[:], s_mS[:], s_mg[2][:], ALU.max),
                    reads=("mg0", "mg2"), writes=("mg0",))
                pr.op("dve", lambda e: e.tensor_tensor(
                    s_mS[:], s_mS[:], s_mg[3][:], ALU.max),
                    reads=("mg0", "mg3"), writes=("mg0",))
                for u in range(2):
                    pr.op("pe", lambda e, u=u: e.matmul(
                        pI[:, 0:P], s_mS[:, u * P:(u + 1) * P], s_idb[:],
                        start=True, stop=True),
                        reads=("mg0", "idb"), writes=("pI",))
                    pr.op("dve", lambda e, u=u: e.tensor_copy(
                        s_m01T[:, u * P:(u + 1) * P], pI[:, 0:P]),
                        reads=("pI",), writes=("mg1",))
                pr.dma("sync", m01d.rearrange("(u p) q -> p u q", p=P),
                       s_m01T[:].rearrange("p (u q) -> p u q", q=P),
                       reads=("mg1",), writes=("m01d",))

            def m01r_load():
                pr.dma("sync",
                       s_m01r[0:8, :].rearrange("r (t q) -> r t q", q=P),
                       m01d.rearrange("(t r) q -> r t q", r=8),
                       reads=("m01d",), writes=("m01r",))

            def load_negm(ui):
                jn, hn = units[ui]
                sl = ui % 2
                pr.dma("sync", s_qg[sl][:].rearrange("p (g q) -> p g q", q=P),
                       qT[hn].rearrange("p (g t) -> p g t", t=TOK)
                       [:, :, jn * P:(jn + 1) * P], writes=(f"qg{sl}",))
                pr.dma("sync", s_rqg[sl][:].rearrange("p (g q) -> p g q", q=P),
                       rqT[hn].rearrange("p (g t) -> p g t", t=TOK)
                       [:, :, jn * P:(jn + 1) * P], writes=(f"rqg{sl}",))

                def negm_row(src, srckey, col, dst16, dstkey):
                    pr.op("dve", lambda e, src=src: e.tensor_tensor(
                        s_sq[:, 0:512], src, src, ALU.mult),
                        reads=(srckey,), writes=("sq",))
                    pr.op("pe", lambda e: e.matmul(
                        pM[0:1, 0:512], s_onesPb[:], s_sq[:, 0:512],
                        start=True, stop=True, skip_group_check=True),
                        reads=("sq", "onesPb"), writes=("pM",))
                    pr.op("dve", lambda e, col=col: e.tensor_scalar(
                        s_nrow[:], pM[0:1, 0:512],
                        s_maxk[0:1, col:col + 1], None, ALU.mult),
                        reads=("pM", "maxk"), writes=("nrow",))
                    pr.op("act", lambda e: e.activation(
                        s_nrow[:], s_nrow[:], AF.Sqrt),
                        reads=("nrow",), writes=("nrow",))
                    pr.op("dve", lambda e: e.tensor_scalar(
                        s_nmf[:], s_nrow[:], -1.0001, None, ALU.mult),
                        reads=("nrow",), writes=("nmf",))
                    pr.op("dve", lambda e, dst16=dst16: e.tensor_copy(
                        dst16, s_nmf[:]), reads=("nmf",), writes=(dstkey,))
                negm_row(s_rqg[sl][:], f"rqg{sl}", hn,
                         s_nmf16[sl][:], f"nmf16_{sl}")
                negm_row(s_qg[sl][:], f"qg{sl}", 8 + hn,
                         s_nmc16[sl][:], f"nmc16_{sl}")

            def main_stage(idx, j, h):
                NTD = 4 * j + 4
                sl = idx % 2
                for i in range(2):
                    pr.dma("sync", s_mb[i][8:9, :], s_nmf16[sl][:],
                           reads=(f"nmf16_{sl}",), writes=(f"mb{i}",))

                lfirst = [True]
                pend = []

                def flush_pend():
                    while pend:
                        pend.pop(0)()

                def lpv(probkey, prob_ap, e3c, pvb, pvkey, vkey, v_ap,
                        lstop=False, pvstart=False, pvstop=False):
                    def emit(probkey=probkey, prob_ap=prob_ap, e3c=e3c,
                             pvb=pvb, pvkey=pvkey, vkey=vkey, v_ap=v_ap,
                             lstop=lstop, pvstart=pvstart, pvstop=pvstop):
                        st = lfirst[0]
                        lfirst[0] = False
                        def mm(e, st=st):
                            e.matmul(pL[0:3, :], s_e3[:, e3c:e3c + 3],
                                     prob_ap, start=st, stop=lstop,
                                     skip_group_check=True)
                            return e.matmul(pvb, v_ap, prob_ap,
                                            start=pvstart, stop=pvstop)
                        pr.op("pe", mm, reads=(probkey, "e3", vkey),
                              writes=("pL", pvkey))
                    return emit

                CT = 1 if j <= 3 else 2
                for t in range(CT):
                    def cmm(e, t=t, h=h):
                        e.matmul(pSc(t),
                                 s_ck[:, h * 384 + t * P:h * 384 + (t + 1) * P],
                                 s_qg[sl][:], start=True, stop=False)
                        return e.matmul(pSc(t), s_ones1b[:], s_nmc16[sl][:],
                                        start=False, stop=True)
                    pr.op("pe", cmm,
                          reads=("ckt", f"qg{sl}", "ones1b", f"nmc16_{sl}"),
                          writes=(f"pS{t % 2}",))
                    pr.op("dve", lambda e, t=t: e.tensor_tensor(
                        pSc(t), pSc(t), s_cm[:, t * 512:(t + 1) * 512], ALU.add),
                        reads=("cmj", f"pS{t % 2}"), writes=(f"pS{t % 2}",))
                    pr.op("act", lambda e, t=t: e.activation(
                        s_pc16[t % 2][:], pSc(t), AF.Exp),
                        reads=(f"pS{t % 2}",), writes=(f"pc{t % 2}",))
                    flush_pend()
                    pend.append(lpv(
                        f"pc{t % 2}", s_pc16[t % 2][:], 0, pC[:, :], "pC",
                        "cvt", s_cv[:, t * H * DH + h * DH:
                                    t * H * DH + (h + 1) * DH],
                        pvstart=(t == 0), pvstop=(t == CT - 1)))

                if idx + 1 < len(units):
                    sel_stage(*units[idx + 1])

                for t in range(NTD):
                    pr.op("dve", lambda e, t=t: e.tensor_copy(
                        s_mb[t % 2][0:8, :].rearrange("r (g q) -> r g q", q=P),
                        s_m01r[0:8, :].rearrange("r (t q) -> r t q", q=P)
                        [:, t:t + 1, :].to_broadcast([8, 4, P])),
                        reads=("m01r",), writes=(f"mb{t % 2}",))
                    def fmm(e, t=t, h=h):
                        e.matmul(pSc(t),
                                 s_rk[:, h * S + t * P:h * S + (t + 1) * P],
                                 s_rqg[sl][:], start=True, stop=False)
                        return e.matmul(pSc(t), s_E9[:], s_mb[t % 2][:],
                                        start=False, stop=True)
                    pr.op("pe", fmm,
                          reads=(("rk", h), f"rqg{sl}", f"mb{t % 2}", "E9"),
                          writes=(f"pS{t % 2}",))
                    pr.op("act", lambda e, t=t: e.activation(
                        s_pf16[t % 2][:], pSc(t), AF.Exp),
                        reads=(f"pS{t % 2}",), writes=(f"pf{t % 2}",))
                    flush_pend()
                    pend.append(lpv(
                        f"pf{t % 2}", s_pf16[t % 2][:], 3, pF[:, :], "pF",
                        "vt", s_vt[:, t * H * DH + h * DH:
                                   t * H * DH + (h + 1) * DH],
                        pvstart=(t == 0)))
                if epi1_pend:
                    epi1_pend.pop(0)()
                if idx + 1 < len(units):
                    load_negm(idx + 1)
                def omm(e, h=h):
                    e.matmul(pSc(0),
                             s_wkT[:, (h * 2 + 1) * DH:(h * 2 + 2) * DH],
                             s_rqg[sl][:], start=True, stop=False)
                    return e.matmul(pSc(0), s_ones1b[:], s_nmf16[sl][:],
                                    start=False, stop=True)
                pr.op("pe", omm,
                      reads=("wkT", f"rqg{sl}", "ones1b", f"nmf16_{sl}"),
                      writes=("pS0",))
                pr.op("dve", lambda e: e.tensor_tensor(
                    pSc(0), pSc(0), s_od[:], ALU.add),
                    reads=("od", "pS0"), writes=("pS0",))
                pr.op("act", lambda e: e.activation(
                    s_po16[:], pSc(0), AF.Exp),
                    reads=("pS0",), writes=("po16",))
                flush_pend()
                pend.append(lpv(
                    "po16", s_po16[:], 3, pF[:, :], "pF",
                    "wvbuf", s_wv[:, H * DH + h * DH:H * DH + (h + 1) * DH],
                    pvstop=True))
                for u in range(2):
                    def smm(e, u=u, h=h):
                        e.matmul(pSc(u),
                                 s_wkT[:, (h * 2 + u) * DH:(h * 2 + u + 1) * DH],
                                 s_rqg[sl][:], start=True, stop=False)
                        return e.matmul(pSc(u), s_ones1b[:], s_nmf16[sl][:],
                                        start=False, stop=True)
                    pr.op("pe", smm,
                          reads=("wkT", f"rqg{sl}", "ones1b", f"nmf16_{sl}"),
                          writes=(f"pS{u}",))
                    pr.op("dve", lambda e, u=u: e.tensor_tensor(
                        pSc(u), pSc(u), s_sm[:, u * 512:(u + 1) * 512], ALU.add),
                        reads=("smj", f"pS{u}"), writes=(f"pS{u}",))
                    pr.op("act", lambda e, u=u: e.activation(
                        s_ps16[:, u * 512:(u + 1) * 512], pSc(u), AF.Exp),
                        reads=(f"pS{u}",), writes=("ps16",))
                    flush_pend()
                    pend.append(lpv(
                        "ps16", s_ps16[:, u * 512:(u + 1) * 512], 6,
                        pW[:, :], "pW",
                        "wvbuf", s_wv[:, u * H * DH + h * DH:
                                      u * H * DH + (h + 1) * DH],
                        lstop=(u == 1), pvstart=(u == 0), pvstop=(u == 1)))
                if idx + 1 < len(units):
                    sel_stage_b()
                    m01r_load()
                if epi2_pend:
                    epi2_pend.pop(0)()
                for g in range(G):
                    pr.op("pe", lambda e, g=g, h=h, j=j: e.matmul(
                        pLg[:, g * P:(g + 1) * P],
                        s_gsel[:, (h * G + g) * 3:(h * G + g) * 3 + 3],
                        s_gates[:, j * P:(j + 1) * P],
                        start=(g == 0), stop=(g == 3),
                        skip_group_check=True),
                        reads=("gsel", "gates"), writes=("pLg",))
                flush_pend()
                pr.op("dve", lambda e: e.tensor_scalar(
                    s_rl[32:35, :], pL[0:3, :], 1e-37, None, ALU.add),
                    reads=("pL",), writes=("rl",))
                pr.op("act", lambda e: e.activation(
                    s_evc[:, 0:512], pC[:, :], AF.Identity),
                    reads=("pC",), writes=("evc",))
                pr.op("act", lambda e: e.activation(
                    s_evc[:, 512:1024], pF[:, :], AF.Identity),
                    reads=("pF",), writes=("evc",))
                pr.op("act", lambda e: e.activation(
                    s_evc[:, 1024:1536], pW[:, :], AF.Identity),
                    reads=("pW",), writes=("evc",))

                def epi1():
                    pr.op("dve", lambda e: e.reciprocal(
                        s_rl[32:35, :], s_rl[32:35, :]),
                        reads=("rl",), writes=("rl",))
                    pr.op("dve", lambda e: e.tensor_tensor(
                        s_Wb[32:35, :], pLg[:, 0:512], s_rl[32:35, :],
                        ALU.mult),
                        reads=("pLg", "rl"), writes=("Wb",))
                epi1_pend.append(epi1)

                def epi2(j=j, h=h):
                    for bb, bkey, src_ap in (
                            (0, "evc", s_evc[:, 0:512]),
                            (1, "evc", s_evc[:, 512:1024]),
                            (2, "evc", s_evc[:, 1024:1536])):
                        pr.op("pe", lambda e, bb=bb: e.matmul(
                            pI[:, 0:512], s_e3w[32:35, bb * P:(bb + 1) * P],
                            s_Wb[32:35, :],
                            start=True, stop=True, skip_group_check=True),
                            reads=("Wb", "e3w"), writes=("pI",))
                        pr.op("act", lambda e: e.activation(
                            s_wrep[:], pI[:, 0:512], AF.Identity),
                            reads=("pI",), writes=("wrep",))
                        if bb == 0:
                            pr.op("dve", lambda e, src_ap=src_ap:
                                  e.tensor_tensor(
                                      s_att[:], src_ap, s_wrep[:], ALU.mult),
                                  reads=(bkey, "wrep"), writes=("att",))
                        else:
                            pr.op("dve", lambda e, src_ap=src_ap:
                                  e.scalar_tensor_tensor(
                                      s_sq[:, 0:512], src_ap, 1.0, s_wrep[:],
                                      ALU.mult, ALU.mult),
                                  reads=(bkey, "wrep"), writes=("sq",))
                            pr.op("dve", lambda e: e.tensor_tensor(
                                s_att[:], s_att[:], s_sq[:, 0:512], ALU.add),
                                reads=("att", "sq"), writes=("att",))
                    pr.op("act", lambda e: e.activation(
                        s_att16[:], s_att[:], AF.Identity),
                        reads=("att",), writes=("att16",))
                    pr.dma("sync",
                           attd[4 * h:4 * h + 4, :, j * P:(j + 1) * P]
                           .rearrange("g p q -> p g q"),
                           s_att16[:].rearrange("p (g q) -> p g q", q=P),
                           reads=("att16",), writes=(("attd", h, j),))
                epi2_pend.append(epi2)

            epi1_pend = []
            epi2_pend = []
            load_negm(0)
            sel_stage(0, 0)
            sel_stage_b()
            m01r_load()
            for idx, (j, h) in enumerate(units):
                if h == 0:
                    j_boundary(j)
                main_stage(idx, j, h)
            while epi1_pend:
                epi1_pend.pop(0)()
            while epi2_pend:
                epi2_pend.pop(0)()
            pr.barrier()

        with ExitStack() as c2:
            sb2 = lambda n, sh, dt=F32: c2.enter_context(nc.sbuf_tensor(n, sh, dt))
            s_attf = sb2("s_attf", [P, QH * TOK], BF16)
            s_wo = [sb2(f"s_wo{i}", [P, NT * P], BF16) for i in range(2)]
            s_x1c = [sb2(f"s_x1c{i}", [P, TOK]) for i in range(2)]
            s_x2c = sb2("s_x2c", [P, TOK])
            pr.dma("sync", s_attf[:].rearrange("p (qh t) -> p qh t", t=TOK),
                   attd.rearrange("qh p t -> p qh t"), writes=("attf",),
                   reads=tuple(("attd", h, j) for h in range(H) for j in range(NJ)))
            wor = woT.rearrange("(k p) c -> p k c", p=P)
            x1r2 = x1T.rearrange("(k p) t -> p k t", p=P)
            x2r = x2d.rearrange("(k p) t -> p k t", p=P)
            for cc in range(NT):
                wb = s_wo[cc % 2]
                pr.dma("gp", wb[:].rearrange("p (k c) -> p k c", c=P),
                       wor[:, :, cc * P:(cc + 1) * P], writes=(f"wo{cc % 2}",))
                pr.dma("sync", s_x1c[cc % 2][:], x1r2[:, cc, :],
                       writes=(f"x1c{cc % 2}",))
                pbank = pA if cc % 2 == 0 else pB
                def womm(e, wb=wb, pbank=pbank):
                    for k in range(NT):
                        e.matmul(pbank[:, 0:512], wb[:, k * P:(k + 1) * P],
                                 s_attf[:, k * TOK:k * TOK + 512],
                                 start=(k == 0), stop=(k == NT - 1))
                    for k in range(NT):
                        i = e.matmul(pbank[:, 512:1024], wb[:, k * P:(k + 1) * P],
                                     s_attf[:, k * TOK + 512:(k + 1) * TOK],
                                     start=(k == 0), stop=(k == NT - 1))
                    return i
                pr.op("pe", womm, reads=(f"wo{cc % 2}", "attf"),
                      writes=(f"pAB{cc % 2}",))
                pr.op("dve", lambda e, cc=cc, pbank=pbank: e.tensor_tensor(
                    s_x2c[:], s_x1c[cc % 2][:], pbank[:, :], ALU.add),
                    reads=(f"pAB{cc % 2}", f"x1c{cc % 2}"), writes=("x2c",))
                pr.dma("sync", x2r[:, cc, :], s_x2c[:], reads=("x2c",), writes=(("x2d", cc),))
            pr.barrier()

        with ExitStack() as c3:
            sb3 = lambda n, sh, dt=F32: c3.enter_context(nc.sbuf_tensor(n, sh, dt))
            s_x = sb3("s_x", [P, D])
            s_sq3 = sb3("s_sq3", [P, D])
            s_y32 = sb3("s_y32", [P, D])
            s_y16r = sb3("s_y16r", [P, D], BF16)
            s_x3 = sb3("s_x3", [P, D])
            s_stat = sb3("s_stat", [1, 512])
            s_rep = sb3("s_rep", [P, 1, 2 * P])
            s_lg = sb3("s_lg", [EXP, P])
            s_e = sb3("s_e", [P, EXP])
            s_col3 = sb3("s_col3", [P, 8])
            s_m8 = sb3("s_m8", [P, 8])
            s_zap = sb3("s_zap", [P, EXP])
            s_pm = sb3("s_pm", [P, EXP])
            s_pms = sb3("s_pms", [P, EXP], BF16)
            s_ws = sb3("s_ws", [P, 1])
            s_pmT = sb3("s_pmT", [EXP, P], BF16)
            s_wsrow = sb3("s_wsrow", [1, P])
            s_wsr = sb3("s_wsr", [P, 1, P])
            s_eb = sb3("s_eb", [EXP, D], BF16)
            s_rw = sb3("s_rw", [P, NT * EXP], BF16)
            pr.dma("gp", s_eb[:], ebias, writes=("eb",))
            pr.dma("gp", s_rw[:].rearrange("p (k e) -> p k e", e=EXP),
                   rwT.rearrange("(k p) e -> p k e", p=P), writes=("rw",))
            pb0, pb1, pb2, pb7 = pc[0], pc[1], pc[2], pc[3]
            x2rr = x2d.rearrange("(k p) t -> p k t", p=P)
            outr = o_out.rearrange("(k p) t -> p k t", p=P)

            for j in range(NJ):
                pr.dma("sync", s_x[:].rearrange("p (k t) -> p k t", t=P),
                       x2rr[:, :, j * P:(j + 1) * P], writes=("x",),
                       reads=tuple(("x2d", cc) for cc in range(NT)))
                pr.op("act", lambda e: e.activation(s_sq3[:], s_x[:], AF.Square),
                      reads=("x",), writes=("sq3",))
                def sums(e):
                    for k in range(NT):
                        i = e.matmul(pb0[0:1, 0:P], s_onesP[:],
                                     s_x[:, k * P:(k + 1) * P],
                                     start=(k == 0), stop=(k == NT - 1))
                    return i
                pr.op("pe", sums, reads=("x", "onesP"), writes=("pb0",))
                def sqsums(e):
                    for k in range(NT):
                        i = e.matmul(pb0[0:1, P:2 * P], s_onesP[:],
                                     s_sq3[:, k * P:(k + 1) * P],
                                     start=(k == 0), stop=(k == NT - 1))
                    return i
                pr.op("pe", sqsums, reads=("sq3", "onesP"), writes=("pb0",))
                pr.op("dve", lambda e: e.tensor_scalar(
                    s_stat[0:1, 0:P], pb0[0:1, 0:P], 1.0 / D, None, ALU.mult),
                    reads=("pb0",), writes=("stat",))
                pr.op("dve", lambda e: e.tensor_tensor(
                    s_stat[0:1, P:2 * P], s_stat[0:1, 0:P], s_stat[0:1, 0:P],
                    ALU.mult), reads=("stat",), writes=("stat",))
                pr.op("dve", lambda e: e.scalar_tensor_tensor(
                    s_stat[0:1, 2 * P:3 * P], pb0[0:1, P:2 * P], 1.0 / D,
                    s_stat[0:1, P:2 * P], ALU.mult, ALU.subtract),
                    reads=("pb0", "stat"), writes=("stat",))
                pr.op("act", lambda e: e.activation(
                    s_stat[0:1, 2 * P:3 * P], s_stat[0:1, 2 * P:3 * P], AF.Sqrt,
                    bias=s_eps[0:1, 0:1], scale=1.0),
                    reads=("stat", "eps"), writes=("stat",))
                pr.op("dve", lambda e: e.reciprocal(
                    s_stat[0:1, 3 * P:4 * P], s_stat[0:1, 2 * P:3 * P]),
                    reads=("stat",), writes=("stat",))
                def bcast(e):
                    e.matmul(pb0[:, 0:P], s_ones1[:], s_stat[0:1, 0:P],
                             start=True, stop=True)
                    return e.matmul(pb0[:, P:2 * P], s_ones1[:],
                                    s_stat[0:1, 3 * P:4 * P], start=True, stop=True)
                pr.op("pe", bcast, reads=("stat", "ones1"), writes=("pb0",))
                pr.op("dve", lambda e: e.tensor_copy(s_rep[:, 0, :], pb0[:, 0:2 * P]),
                      reads=("pb0",), writes=("rep",))
                pr.op("dve", lambda e: e.tensor_tensor(
                    s_y32[:].rearrange("p (k t) -> p k t", k=NT),
                    s_x[:].rearrange("p (k t) -> p k t", k=NT),
                    s_rep[:, :, 0:P].to_broadcast([P, NT, P]), ALU.subtract),
                    reads=("x", "rep"), writes=("y32",))
                pr.op("dve", lambda e: e.tensor_tensor(
                    s_y32[:].rearrange("p (k t) -> p k t", k=NT),
                    s_y32[:].rearrange("p (k t) -> p k t", k=NT),
                    s_rep[:, :, P:2 * P].to_broadcast([P, NT, P]), ALU.mult),
                    reads=("rep",), writes=("y32",))
                pr.op("dve", lambda e: e.tensor_copy(s_y16r[:], s_y32[:]),
                      reads=("y32",), writes=("y16r",))
                def logits(e):
                    for k in range(NT):
                        i = e.matmul(pb0[0:EXP, 2 * P:3 * P],
                                     s_rw[:, k * EXP:(k + 1) * EXP],
                                     s_y16r[:, k * P:(k + 1) * P],
                                     start=(k == 0), stop=(k == NT - 1))
                    return i
                pr.op("pe", logits, reads=("y16r", "rw"), writes=("pb0",))
                pr.op("act", lambda e: e.activation(
                    s_lg[:], pb0[0:EXP, 2 * P:3 * P], AF.Identity,
                    bias=s_rb[:, 0:1], scale=1.0),
                    reads=("pb0", "rb"), writes=("lg",))
                pr.op("pe", lambda e: e.matmul(
                    pb0[0:P, 3 * P:3 * P + EXP], s_lg[:], s_id[0:EXP, 0:EXP],
                    start=True, stop=True), reads=("lg", "id"), writes=("pb0",))
                pr.op("dve", lambda e: e.tensor_reduce(
                    s_col3[:, 0:1], pb0[0:P, 3 * P:3 * P + EXP], AX, ALU.max,
                    negate=True), reads=("pb0",), writes=("col3",))
                pr.op("act", lambda e: e.activation(
                    s_e[:], pb0[0:P, 3 * P:3 * P + EXP], AF.Exp,
                    bias=s_col3[:, 0:1], scale=1.0),
                    reads=("pb0", "col3"), writes=("e",))
                pr.op("dve", lambda e: e.tensor_reduce(
                    s_col3[:, 1:2], s_e[:], AX, ALU.add), reads=("e",), writes=("col3",))
                pr.op("dve", lambda e: e.max(out=s_m8[:], in_=s_e[:]),
                      reads=("e",), writes=("m8",))
                pr.op("dve", lambda e: e.match_replace(
                    out=s_zap[:], in_to_replace=s_m8[:], in_values=s_e[:],
                    imm_value=0.0), reads=("m8", "e"), writes=("zap", "m8"))
                pr.op("dve", lambda e: e.tensor_tensor(
                    s_pm[:], s_e[:], s_zap[:], ALU.subtract),
                    reads=("e", "zap"), writes=("pm",))
                pr.op("dve", lambda e: e.tensor_reduce(
                    s_col3[:, 2:3], s_pm[:], AX, ALU.add), reads=("pm",), writes=("col3",))
                pr.op("dve", lambda e: e.reciprocal(s_col3[:, 3:4], s_col3[:, 1:2]),
                      reads=("col3",), writes=("col3",))
                pr.op("dve", lambda e: e.scalar_tensor_tensor(
                    s_ws[:], s_col3[:, 2:3], 0.5, s_col3[:, 3:4], ALU.mult, ALU.mult),
                    reads=("col3",), writes=("ws",))
                pr.op("dve", lambda e: e.tensor_scalar(
                    s_col3[:, 4:5], s_col3[:, 3:4], 0.5, None, ALU.mult),
                    reads=("col3",), writes=("col3",))
                pr.op("dve", lambda e: e.tensor_scalar(
                    s_pms[:], s_pm[:], s_col3[:, 4:5], None, ALU.mult),
                    reads=("pm", "col3"), writes=("pms",))
                pr.op("pe", lambda e: e.matmul(
                    pb7[0:EXP, 0:P], s_pms[:], s_idb[:], start=True, stop=True),
                    reads=("pms", "idb"), writes=("pb7",))
                pr.op("dve", lambda e: e.tensor_copy(s_pmT[:], pb7[0:EXP, 0:P]),
                      reads=("pb7",), writes=("pmT",))
                pr.op("pe", lambda e: e.matmul(
                    pb7[0:1, 0:P], s_ws[:], s_id[:], start=True, stop=True),
                    reads=("ws", "id"), writes=("pb7",))
                pr.op("dve", lambda e: e.tensor_copy(s_wsrow[:], pb7[0:1, 0:P]),
                      reads=("pb7",), writes=("wsrow",))
                pr.op("pe", lambda e: e.matmul(
                    pb0[:, 0:P], s_ones1[:], s_wsrow[:], start=True, stop=True),
                    reads=("wsrow", "ones1"), writes=("pb0",))
                pr.op("dve", lambda e: e.tensor_copy(s_wsr[:, 0, :], pb0[:, 0:P]),
                      reads=("pb0",), writes=("wsr",))
                pr.op("dve", lambda e: e.tensor_tensor(
                    s_sq3[:].rearrange("p (k t) -> p k t", k=NT),
                    s_y32[:].rearrange("p (k t) -> p k t", k=NT),
                    s_wsr[:].to_broadcast([P, NT, P]), ALU.mult),
                    reads=("y32", "wsr"), writes=("sq3",))
                pr.op("dve", lambda e: e.tensor_tensor(
                    s_x3[:], s_x[:], s_sq3[:], ALU.add),
                    reads=("x", "sq3"), writes=("x3",))
                for k in range(NT):
                    bank = pb1 if k % 2 == 0 else pb2
                    bkey = "pb1" if k % 2 == 0 else "pb2"
                    pr.op("pe", lambda e, k=k, bank=bank: e.matmul(
                        bank[0:P, 0:P], s_eb[:, k * P:(k + 1) * P], s_pmT[:],
                        start=True, stop=True),
                        reads=("eb", "pmT"), writes=(bkey,))
                    pr.op("dve", lambda e, k=k, bank=bank: e.tensor_tensor(
                        s_x3[:, k * P:(k + 1) * P], s_x3[:, k * P:(k + 1) * P],
                        bank[0:P, 0:P], ALU.add),
                        reads=(bkey, "x3"), writes=("x3",))
                pr.dma("sync", outr[:, :, j * P:(j + 1) * P],
                       s_x3[:].rearrange("p (k t) -> p k t", t=P), reads=("x3",))
            pr.finalize(block)
    return nc


P34_OUTS = ["o_out"]


def kernel(**inputs):
    maps1 = _prep_p12(inputs)
    nc1 = _build_p12()
    res1 = run_bass_kernel_spmd(nc1, maps1, core_ids=list(range(8))).results
    maps2 = _prep_p34(inputs, res1)
    nc2 = _build_p34()
    res2 = run_bass_kernel_spmd(nc2, [maps2[c] for c in range(8)],
                                core_ids=list(range(8))).results
    out = np.zeros((2, S, D), dtype=np.float32)
    for c in range(8):
        b, r = c // 4, c % 4
        out[b, _own_positions(r)] = res2[c]["o_out"].T
    return out


# version 13 (final)
# speedup vs baseline: 1.1776x; 1.0093x over previous
import numpy as np
import ml_dtypes
from contextlib import ExitStack

import concourse.bass as bass
import concourse.mybir as mybir
from concourse.bass_utils import run_bass_kernel_spmd
from concourse.bass import IndirectOffsetOnAxis

F32 = mybir.dt.float32
BF16 = mybir.dt.bfloat16
U32 = mybir.dt.uint32
AF = mybir.ActivationFunctionType
ALU = mybir.AluOpType
AX = mybir.AxisListType.X

P = 128
D = 4096
S = 4096
NT = 32
NJ = 8
TOK = 1024
H = 8
G = 4
QH = 32
DH = 128
CB = 16
NBLK = 256
LBLK = 64
CDIM = 2048
EXP = 64
SCALE = float(DH) ** -0.5
EPS = 1e-5
NEG = -3.0e38
QCOLS = 512

_PERM = np.concatenate([np.arange(0, DH, 2), np.arange(1, DH, 2)])


class Prog:
    QUEUES = ("sync", "gp")
    ENGINES = ("sync", "gp", "pe", "act", "dve")
    NQSEM = 8

    def __init__(self, nc, ctx):
        self.nc = nc
        self.sems = {e: ctx.enter_context(nc.semaphore("sem_" + e))
                     for e in ("pe", "act", "dve")}
        self.qsems = {q: [ctx.enter_context(nc.semaphore(f"sem_{q}{i}"))
                          for i in range(self.NQSEM)] for q in self.QUEUES}
        self.quse = {q: [0] * self.NQSEM for q in self.QUEUES}
        self.qseq = {q: 0 for q in self.QUEUES}
        self.count = {e: 0 for e in ("pe", "act", "dve")}
        self.stream = {e: [] for e in self.ENGINES}
        self.lastw = {}
        self.readers = {}
        self.waited = {e: {} for e in self.ENGINES}

    def _need(self, eng, token):
        dep_eng, sem, val = token
        if eng == dep_eng == "pe":
            return
        if eng == dep_eng and eng not in self.QUEUES:
            pass
        sid = sem.num
        if val <= self.waited[eng].get(sid, 0):
            return
        self.waited[eng][sid] = val
        self.stream[eng].append(lambda e, s=sem, v=val: e.wait_ge(s, v))

    def _deps(self, eng, reads, writes):
        for k in reads:
            if k in self.lastw:
                self._need(eng, self.lastw[k])
        for k in writes:
            if k in self.lastw:
                self._need(eng, self.lastw[k])
            for tok in self.readers.get(k, {}).values():
                self._need(eng, tok)

    def _record(self, token, reads, writes):
        for k in writes:
            self.lastw[k] = token
            self.readers[k] = {}
        for k in reads:
            self.readers.setdefault(k, {})[token[1].num] = token

    def op(self, eng, fn, reads=(), writes=()):
        assert eng not in self.QUEUES
        self._deps(eng, reads, writes)
        self.count[eng] += 1
        cnt = self.count[eng]
        sem = self.sems[eng]
        self.stream[eng].append(lambda e, f=fn, s=sem: f(e).then_inc(s, 1))
        self._record((eng, sem, cnt), reads, writes)
        return cnt

    def qop(self, q, fn, reads=(), writes=()):
        self._deps(q, reads, writes)
        i = self.qseq[q] % self.NQSEM
        self.qseq[q] += 1
        sem = self.qsems[q][i]
        if self.quse[q][i] > 0:
            self._need(q, (q, sem, 16 * self.quse[q][i]))
        self.quse[q][i] += 1
        val = 16 * self.quse[q][i]
        self.stream[q].append(
            lambda e, f=fn, sm=sem: f(e).then_inc(sm, 16))
        self._record((q, sem, val), reads, writes)
        return val

    def dma(self, q, out, in_, reads=(), writes=(), **kw):
        return self.qop(q, lambda e, o=out, s=in_, kw=kw:
                        e.dma_start(out=o, in_=s, **kw), reads, writes)

    def _alltokens(self):
        toks = [(e, self.sems[e], self.count[e]) for e in ("pe", "act", "dve")
                if self.count[e] > 0]
        for q in self.QUEUES:
            for i in range(self.NQSEM):
                if self.quse[q][i] > 0:
                    toks.append((q, self.qsems[q][i], 16 * self.quse[q][i]))
        return toks

    def barrier(self):
        toks = self._alltokens()
        for e in self.ENGINES:
            for tok in toks:
                if tok[0] != e or e in self.QUEUES:
                    self._need(e, tok)

    def finalize(self, block):
        self.barrier()
        for i in range(self.NQSEM):
            for q in self.QUEUES:
                if self.quse[q][i] > 0:
                    self._need("sync", (("x",), self.qsems[q][i],
                                        16 * self.quse[q][i])[0:3]
                               if False else
                               ("_", self.qsems[q][i], 16 * self.quse[q][i]))

        def runner(items):
            def go(e):
                for f in items:
                    f(e)
            return go

        block.sync(runner(self.stream["sync"]))
        block.gpsimd(runner(self.stream["gp"]))
        block.tensor(runner(self.stream["pe"]))
        block.scalar(runner(self.stream["act"]))
        block.vector(runner(self.stream["dve"]))


def _rope_tables():
    inv = 1.0 / (10000.0 ** (np.arange(0, DH, 2, dtype=np.float64) / DH))
    ang = np.arange(S, dtype=np.float64)[:, None] * inv[None, :]
    return (np.cos(ang).T.astype(np.float32), np.sin(ang).T.astype(np.float32))


def _own_positions(r):
    tiles = np.arange(NJ) * 4 + r
    return (tiles[:, None] * P + np.arange(P)[None, :]).reshape(-1)


def _prep_p12(inputs):
    x = np.asarray(inputs["L0_attn_hidden_states"])
    w_qkv = np.asarray(inputs["L0_attn_w_qkv"])
    k_w1 = np.asarray(inputs["L0_attn_k_w1"]); k_b1 = np.asarray(inputs["L0_attn_k_b1"])
    k_w2 = np.asarray(inputs["L0_attn_k_w2"]); k_b2 = np.asarray(inputs["L0_attn_k_b2"])
    v_w1 = np.asarray(inputs["L0_attn_v_w1"]); v_b1 = np.asarray(inputs["L0_attn_v_b1"])
    v_w2 = np.asarray(inputs["L0_attn_v_w2"]); v_b2 = np.asarray(inputs["L0_attn_v_b2"])
    k_pos = np.asarray(inputs["L0_attn_k_pos"]); v_pos = np.asarray(inputs["L0_attn_v_pos"])
    w_gate = np.asarray(inputs["L0_attn_w_gate"]); b_gate = np.asarray(inputs["L0_attn_b_gate"])
    rw = np.asarray(inputs["L0_ffn_router_weight"]); rb = np.asarray(inputs["L0_ffn_router_bias"])
    eb = np.asarray(inputs["L0_ffn_expert_bias"])

    qk_perm = np.arange(6144)
    for qh in range(QH):
        qk_perm[qh * DH:(qh + 1) * DH] = qh * DH + _PERM
    for kh in range(H):
        base = QH * DH + kh * DH
        qk_perm[base:base + DH] = base + _PERM
    wqkvT = np.ascontiguousarray(w_qkv[qk_perm].T)

    in_perm_k = (np.arange(CB)[:, None] * DH + _PERM[None, :]).reshape(-1)
    cos, sin = _rope_tables()
    shared = dict(
        wqkvT=wqkvT,
        kw1T=np.ascontiguousarray(k_w1[:, in_perm_k].T),
        vw1T=np.ascontiguousarray(v_w1.T),
        kw2T=np.ascontiguousarray(k_w2[_PERM].T),
        vw2T=np.ascontiguousarray(v_w2.T),
        kb1=np.ascontiguousarray(k_b1.reshape(16, 128).T),
        vb1=np.ascontiguousarray(v_b1.reshape(16, 128).T),
        kb2=k_b2[_PERM].reshape(DH, 1).copy(), vb2=v_b2.reshape(DH, 1).copy(),
        kposT=np.ascontiguousarray(k_pos[:, :, _PERM].transpose(2, 0, 1).reshape(DH, H * CB)),
        vposT=np.ascontiguousarray(v_pos.transpose(2, 0, 1).reshape(DH, H * CB)),
        wgateT=np.ascontiguousarray(w_gate.T), bgate=b_gate.reshape(96, 1).copy(),
        rwT=np.ascontiguousarray(rw.T), rbias=rb.reshape(EXP, 1).copy(), ebias=eb,
        ident=np.eye(P, dtype=np.float32), identb=np.eye(P, dtype=ml_dtypes.bfloat16),
    )
    maps = []
    for c in range(8):
        b, r = c // 4, c % 4
        pos = _own_positions(r)
        m = dict(shared)
        m.update(xT=np.ascontiguousarray(x[b][pos].T),
                 cosT=np.ascontiguousarray(cos[:, pos]),
                 sinT=np.ascontiguousarray(sin[:, pos]))
        maps.append(m)
    return maps


def _build_p12():
    nc = bass.Bass("TRN2", target_bir_lowering=False)
    d_ = lambda n, sh, dt=F32, k="ExternalInput": nc.dram_tensor(n, sh, dt, kind=k).ap()
    xT = d_("xT", [D, TOK])
    wqkvT = d_("wqkvT", [D, 6144])
    kw1T = d_("kw1T", [CDIM, CDIM]); vw1T = d_("vw1T", [CDIM, CDIM])
    kw2T = d_("kw2T", [CDIM, DH]); vw2T = d_("vw2T", [CDIM, DH])
    kb1 = d_("kb1", [P, 16]); vb1 = d_("vb1", [P, 16])
    kb2 = d_("kb2", [DH, 1]); vb2 = d_("vb2", [DH, 1])
    kposT = d_("kposT", [DH, H * CB]); vposT = d_("vposT", [DH, H * CB])
    wgateT = d_("wgateT", [D, 96]); bgate = d_("bgate", [96, 1])
    rwT = d_("rwT", [D, EXP]); rbias = d_("rbias", [EXP, 1]); ebias = d_("ebias", [EXP, D])
    ident = d_("ident", [P, P]); identb = d_("identb", [P, P], BF16)
    cosT = d_("cosT", [64, TOK]); sinT = d_("sinT", [64, TOK])

    o_x1 = d_("o_x1", [D, TOK], F32, "ExternalOutput")
    o_q = d_("o_q", [H, DH, G * TOK], BF16, "ExternalOutput")
    o_rq = d_("o_rq", [H, DH, G * TOK], BF16, "ExternalOutput")
    o_qsum = d_("o_qsum", [H, DH, TOK], BF16, "ExternalOutput")
    o_gates = d_("o_gates", [96, TOK], F32, "ExternalOutput")
    o_rk = d_("o_rk", [H, DH, TOK], BF16, "ExternalOutput")
    o_ktok = d_("o_ktok", [TOK, H * DH], BF16, "ExternalOutput")
    o_vtok = d_("o_vtok", [TOK, H * DH], BF16, "ExternalOutput")
    o_ck = d_("o_ck", [H, LBLK, DH], BF16, "ExternalOutput")
    o_cv = d_("o_cv", [H, LBLK, DH], BF16, "ExternalOutput")

    with ExitStack() as ctx:
        sb_ = lambda n, sh, dt=F32: ctx.enter_context(nc.sbuf_tensor(n, sh, dt))
        s_y16 = sb_("s_y16", [P, NT * TOK], BF16)
        s_rw = sb_("s_rw", [P, NT * EXP], BF16)
        s_eb = sb_("s_eb", [EXP, D], BF16)
        s_wg = sb_("s_wg", [P, NT * 96], BF16)
        s_id = sb_("s_id", [P, P]); s_idb = sb_("s_idb", [P, P], BF16)
        s_ones1 = sb_("s_ones1", [1, P]); s_onesP = sb_("s_onesP", [P, 1])
        s_rb = sb_("s_rb", [EXP, 1]); s_bg = sb_("s_bg", [96, 1])
        s_kb1 = sb_("s_kb1", [P, 16]); s_vb1 = sb_("s_vb1", [P, 16])
        s_kb2 = sb_("s_kb2", [DH, 1]); s_vb2 = sb_("s_vb2", [DH, 1])
        s_kpos = sb_("s_kpos", [DH, H * CB]); s_vpos = sb_("s_vpos", [DH, H * CB])
        s_cos = sb_("s_cos", [P, TOK]); s_sin = sb_("s_sin", [P, TOK])
        s_eps = sb_("s_eps", [1, 1])

        ps_ = lambda n: ctx.enter_context(nc.psum_tensor(n, [P, 512], F32))
        pb = [ps_(f"pb{i}") for i in range(8)]

        pr = Prog(nc, ctx)
        block = ctx.enter_context(nc.Block())

        for dst, src, key in [
            (s_rb[:], rbias, "rb"), (s_bg[:], bgate, "bg"),
            (s_kb1[:], kb1, "kb1"), (s_vb1[:], vb1, "vb1"),
            (s_kb2[:], kb2, "kb2"), (s_vb2[:], vb2, "vb2"),
            (s_kpos[:], kposT, "kpos"), (s_vpos[:], vposT, "vpos"),
            (s_cos[0:64, :], cosT, "cos"), (s_sin[0:64, :], sinT, "sin"),
            (s_cos[64:128, :], cosT, "cos"), (s_sin[64:128, :], sinT, "sin"),
            (s_id[:], ident, "id"), (s_idb[:], identb, "idb"),
        ]:
            pr.dma("sync", dst, src, writes=(key,))
        pr.dma("gp", s_rw[:].rearrange("p (k e) -> p k e", e=EXP),
               rwT.rearrange("(k p) e -> p k e", p=P), writes=("rw",))
        pr.dma("gp", s_eb[:], ebias, writes=("eb",))
        pr.dma("gp", s_wg[:].rearrange("p (k e) -> p k e", e=96),
               wgateT.rearrange("(k p) e -> p k e", p=P), writes=("wg",))
        pr.op("dve", lambda e: e.memset(s_ones1[:], 1.0), writes=("ones1",))
        pr.op("dve", lambda e: e.memset(s_eps[:], EPS), writes=("eps",))
        pr.op("dve", lambda e: e.memset(s_onesP[:], 1.0), writes=("onesP",))

        xTr = xT.rearrange("(k p) t -> p k t", p=P)
        x1r = o_x1.rearrange("(k p) t -> p k t", p=P)

        with ExitStack() as c1:
            sb1 = lambda n, sh, dt=F32: c1.enter_context(nc.sbuf_tensor(n, sh, dt))
            s_x = sb1("s_x", [P, D])
            s_sq = sb1("s_sq", [P, D])
            s_y32 = sb1("s_y32", [P, D])
            s_y16r = sb1("s_y16r", [P, D], BF16)
            s_x1 = sb1("s_x1", [P, D])
            s_stat = sb1("s_stat", [1, 512])
            s_rep = sb1("s_rep", [P, 1, 2 * P])
            s_lg = sb1("s_lg", [EXP, P])
            s_e = sb1("s_e", [P, EXP])
            s_col = sb1("s_col", [P, 8])
            s_m8 = sb1("s_m8", [P, 8])
            s_zap = sb1("s_zap", [P, EXP])
            s_pm = sb1("s_pm", [P, EXP])
            s_pms = sb1("s_pms", [P, EXP], BF16)
            s_ws = sb1("s_ws", [P, 1])
            s_pmT = sb1("s_pmT", [EXP, P], BF16)
            s_wsrow = sb1("s_wsrow", [1, P])
            s_wsr = sb1("s_wsr", [P, 1, P])

            def layer_norm(src_key, src_sb, out16, out16_key):
                pr.op("act", lambda e: e.activation(s_sq[:], src_sb[:], AF.Square),
                      reads=(src_key,), writes=("sq",))
                def sums(e):
                    for k in range(NT):
                        i = e.matmul(pb[0][0:1, 0:P], s_onesP[:],
                                     src_sb[:, k * P:(k + 1) * P],
                                     start=(k == 0), stop=(k == NT - 1))
                    return i
                pr.op("pe", sums, reads=(src_key, "onesP"), writes=("pb0",))
                def sqsums(e):
                    for k in range(NT):
                        i = e.matmul(pb[0][0:1, P:2 * P], s_onesP[:],
                                     s_sq[:, k * P:(k + 1) * P],
                                     start=(k == 0), stop=(k == NT - 1))
                    return i
                pr.op("pe", sqsums, reads=("sq", "onesP"), writes=("pb0",))
                pr.op("dve", lambda e: e.tensor_scalar(
                    s_stat[0:1, 0:P], pb[0][0:1, 0:P], 1.0 / D, None, ALU.mult),
                    reads=("pb0",), writes=("stat",))
                pr.op("dve", lambda e: e.tensor_tensor(
                    s_stat[0:1, P:2 * P], s_stat[0:1, 0:P], s_stat[0:1, 0:P],
                    ALU.mult), reads=("stat",), writes=("stat",))
                pr.op("dve", lambda e: e.scalar_tensor_tensor(
                    s_stat[0:1, 2 * P:3 * P], pb[0][0:1, P:2 * P], 1.0 / D,
                    s_stat[0:1, P:2 * P], ALU.mult, ALU.subtract),
                    reads=("pb0", "stat"), writes=("stat",))
                pr.op("act", lambda e: e.activation(
                    s_stat[0:1, 2 * P:3 * P], s_stat[0:1, 2 * P:3 * P], AF.Sqrt,
                    bias=s_eps[0:1, 0:1], scale=1.0),
                    reads=("stat", "eps"), writes=("stat",))
                pr.op("dve", lambda e: e.reciprocal(
                    s_stat[0:1, 3 * P:4 * P], s_stat[0:1, 2 * P:3 * P]),
                    reads=("stat",), writes=("stat",))
                def bcast(e):
                    e.matmul(pb[0][:, 0:P], s_ones1[:], s_stat[0:1, 0:P],
                             start=True, stop=True)
                    return e.matmul(pb[0][:, P:2 * P], s_ones1[:],
                                    s_stat[0:1, 3 * P:4 * P], start=True, stop=True)
                pr.op("pe", bcast, reads=("stat", "ones1"), writes=("pb0",))
                pr.op("dve", lambda e: e.tensor_copy(s_rep[:, 0, :], pb[0][:, 0:2 * P]),
                      reads=("pb0",), writes=("rep",))
                pr.op("dve", lambda e: e.tensor_tensor(
                    s_y32[:].rearrange("p (k t) -> p k t", k=NT),
                    src_sb[:].rearrange("p (k t) -> p k t", k=NT),
                    s_rep[:, :, 0:P].to_broadcast([P, NT, P]), ALU.subtract),
                    reads=(src_key, "rep"), writes=("y32",))
                pr.op("dve", lambda e: e.tensor_tensor(
                    s_y32[:].rearrange("p (k t) -> p k t", k=NT),
                    s_y32[:].rearrange("p (k t) -> p k t", k=NT),
                    s_rep[:, :, P:2 * P].to_broadcast([P, NT, P]), ALU.mult),
                    reads=("rep",), writes=("y32",))
                src3 = s_y32[:].rearrange("p (k t) -> p k t", t=P) \
                    if len(out16.shape) == 3 else s_y32[:]
                pr.op("dve", lambda e, src3=src3: e.tensor_copy(out16, src3),
                      reads=("y32",), writes=(out16_key,))

            for j in range(NJ):
                pr.dma("sync", s_x[:].rearrange("p (k t) -> p k t", t=P),
                       xTr[:, :, j * P:(j + 1) * P], writes=("x",))
                layer_norm("x", s_x, s_y16r[:], "y16r")
                def logits(e):
                    for k in range(NT):
                        i = e.matmul(pb[0][0:EXP, 2 * P:3 * P],
                                     s_rw[:, k * EXP:(k + 1) * EXP],
                                     s_y16r[:, k * P:(k + 1) * P],
                                     start=(k == 0), stop=(k == NT - 1))
                    return i
                pr.op("pe", logits, reads=("y16r", "rw"), writes=("pb0",))
                pr.op("act", lambda e: e.activation(
                    s_lg[:], pb[0][0:EXP, 2 * P:3 * P], AF.Identity,
                    bias=s_rb[:, 0:1], scale=1.0),
                    reads=("pb0", "rb"), writes=("lg",))
                pr.op("pe", lambda e: e.matmul(
                    pb[0][0:P, 3 * P:3 * P + EXP], s_lg[:], s_id[0:EXP, 0:EXP],
                    start=True, stop=True), reads=("lg", "id"), writes=("pb0",))
                pr.op("dve", lambda e: e.tensor_reduce(
                    s_col[:, 0:1], pb[0][0:P, 3 * P:3 * P + EXP], AX, ALU.max,
                    negate=True), reads=("pb0",), writes=("col",))
                pr.op("act", lambda e: e.activation(
                    s_e[:], pb[0][0:P, 3 * P:3 * P + EXP], AF.Exp,
                    bias=s_col[:, 0:1], scale=1.0),
                    reads=("pb0", "col"), writes=("e",))
                pr.op("dve", lambda e: e.tensor_reduce(
                    s_col[:, 1:2], s_e[:], AX, ALU.add), reads=("e",), writes=("col",))
                pr.op("dve", lambda e: e.max(out=s_m8[:], in_=s_e[:]),
                      reads=("e",), writes=("m8",))
                pr.op("dve", lambda e: e.match_replace(
                    out=s_zap[:], in_to_replace=s_m8[:], in_values=s_e[:],
                    imm_value=0.0), reads=("m8", "e"), writes=("zap", "m8"))
                pr.op("dve", lambda e: e.tensor_tensor(
                    s_pm[:], s_e[:], s_zap[:], ALU.subtract),
                    reads=("e", "zap"), writes=("pm",))
                pr.op("dve", lambda e: e.tensor_reduce(
                    s_col[:, 2:3], s_pm[:], AX, ALU.add), reads=("pm",), writes=("col",))
                pr.op("dve", lambda e: e.reciprocal(s_col[:, 3:4], s_col[:, 1:2]),
                      reads=("col",), writes=("col",))
                pr.op("dve", lambda e: e.scalar_tensor_tensor(
                    s_ws[:], s_col[:, 2:3], 0.5, s_col[:, 3:4], ALU.mult, ALU.mult),
                    reads=("col",), writes=("ws",))
                pr.op("dve", lambda e: e.tensor_scalar(
                    s_col[:, 4:5], s_col[:, 3:4], 0.5, None, ALU.mult),
                    reads=("col",), writes=("col",))
                pr.op("dve", lambda e: e.tensor_scalar(
                    s_pms[:], s_pm[:], s_col[:, 4:5], None, ALU.mult),
                    reads=("pm", "col"), writes=("pms",))
                pr.op("pe", lambda e: e.matmul(
                    pb[7][0:EXP, 0:P], s_pms[:], s_idb[:], start=True, stop=True),
                    reads=("pms", "idb"), writes=("pb7",))
                pr.op("dve", lambda e: e.tensor_copy(s_pmT[:], pb[7][0:EXP, 0:P]),
                      reads=("pb7",), writes=("pmT",))
                pr.op("pe", lambda e: e.matmul(
                    pb[7][0:1, 0:P], s_ws[:], s_id[:], start=True, stop=True),
                    reads=("ws", "id"), writes=("pb7",))
                pr.op("dve", lambda e: e.tensor_copy(s_wsrow[:], pb[7][0:1, 0:P]),
                      reads=("pb7",), writes=("wsrow",))
                pr.op("pe", lambda e: e.matmul(
                    pb[0][:, 0:P], s_ones1[:], s_wsrow[:], start=True, stop=True),
                    reads=("wsrow", "ones1"), writes=("pb0",))
                pr.op("dve", lambda e: e.tensor_copy(s_wsr[:, 0, :], pb[0][:, 0:P]),
                      reads=("pb0",), writes=("wsr",))
                pr.op("dve", lambda e: e.tensor_tensor(
                    s_sq[:].rearrange("p (k t) -> p k t", k=NT),
                    s_y32[:].rearrange("p (k t) -> p k t", k=NT),
                    s_wsr[:].to_broadcast([P, NT, P]), ALU.mult),
                    reads=("y32", "wsr"), writes=("sq",))
                pr.op("dve", lambda e: e.tensor_tensor(
                    s_x1[:], s_x[:], s_sq[:], ALU.add),
                    reads=("x", "sq"), writes=("x1",))
                for k in range(NT):
                    bank = 1 + (k % 2)
                    pr.op("pe", lambda e, k=k, bank=bank: e.matmul(
                        pb[bank][0:P, 0:P], s_eb[:, k * P:(k + 1) * P], s_pmT[:],
                        start=True, stop=True),
                        reads=("eb", "pmT"), writes=(f"pb{bank}",))
                    pr.op("dve", lambda e, k=k, bank=bank: e.tensor_tensor(
                        s_x1[:, k * P:(k + 1) * P], s_x1[:, k * P:(k + 1) * P],
                        pb[bank][0:P, 0:P], ALU.add),
                        reads=(f"pb{bank}", "x1"), writes=("x1",))
                pr.dma("sync", x1r[:, :, j * P:(j + 1) * P],
                       s_x1[:].rearrange("p (k t) -> p k t", t=P), reads=("x1",))
                y16j = s_y16[:].rearrange("p (k j t) -> p k j t", j=NJ, t=P)[:, :, j, :]
                layer_norm("x1", s_x1, y16j, ("y16", j))

            pr.barrier()

        y16keys = tuple(("y16", j) for j in range(NJ))

        with ExitStack() as c2:
            sb2 = lambda n, sh, dt=F32: c2.enter_context(nc.sbuf_tensor(n, sh, dt))
            s_w = [sb2(f"s_w{i}", [P, NT * P], BF16) for i in range(2)]
            s_qkv = sb2("s_qkv", [P, TOK], BF16)
            s_rope = sb2("s_rope", [P, TOK], BF16)
            s_ksc = sb2("s_ksc", [P, TOK], BF16)
            s_qs1 = sb2("s_qs1", [P, TOK], BF16)
            s_qs2 = sb2("s_qs2", [P, TOK], BF16)
            s_tr = sb2("s_tr", [P, TOK], BF16)
            s_cin = sb2("s_cin", [P, 2 * H * TOK], BF16)
            s_h1 = sb2("s_h1", [P, 16 * 512], BF16)
            s_co = sb2("s_co", [P, 512], BF16)
            s_gt = sb2("s_gt", [96, TOK])

            wqr = wqkvT.rearrange("(k p) c -> p k c", p=P)
            ktokr = o_ktok.rearrange("(j p) c -> p j c", p=P)
            vtokr = o_vtok.rearrange("(j p) c -> p j c", p=P)

            def qkv_coltile(ct):
                wbuf = s_w[ct % 2]
                pr.dma("gp", wbuf[:].rearrange("p (k c) -> p k c", c=P),
                       wqr[:, :, ct * P:(ct + 1) * P], writes=(f"wbuf{ct % 2}",))
                b0, b1 = (3, 4) if ct % 2 == 0 else (5, 6)
                def mms(e, b0=b0, b1=b1, wbuf=wbuf):
                    for k in range(NT):
                        e.matmul(pb[b0][:, 0:512], wbuf[:, k * P:(k + 1) * P],
                                 s_y16[:, k * TOK:k * TOK + 512],
                                 start=(k == 0), stop=(k == NT - 1))
                    for k in range(NT):
                        i = e.matmul(pb[b1][:, 0:512], wbuf[:, k * P:(k + 1) * P],
                                     s_y16[:, k * TOK + 512:(k + 1) * TOK],
                                     start=(k == 0), stop=(k == NT - 1))
                    return i
                pr.op("pe", mms, reads=(f"wbuf{ct % 2}",) + y16keys,
                      writes=(f"pb{b0}", f"pb{b1}"))
                return b0, b1

            def rope(src, src_key, dst, dst_key):
                x0 = src[0:64, :]; x1 = src[64:128, :]
                pr.op("dve", lambda e: e.tensor_tensor(
                    dst[0:64, :], x0, s_cos[0:64, :], ALU.mult),
                    reads=(src_key, "cos"), writes=(dst_key,))
                pr.op("dve", lambda e: e.tensor_tensor(
                    s_tr[0:64, :], x1, s_sin[64:128, :], ALU.mult),
                    reads=(src_key, "sin"), writes=("trs",))
                pr.op("dve", lambda e: e.tensor_tensor(
                    dst[0:64, :], dst[0:64, :], s_tr[0:64, :], ALU.subtract),
                    reads=(dst_key, "trs"), writes=(dst_key,))
                pr.op("dve", lambda e: e.tensor_tensor(
                    dst[64:128, :], x1, s_cos[64:128, :], ALU.mult),
                    reads=(src_key, "cos"), writes=(dst_key,))
                pr.op("dve", lambda e: e.tensor_tensor(
                    s_tr[64:128, :], x0, s_sin[0:64, :], ALU.mult),
                    reads=(src_key, "sin"), writes=("trs",))
                pr.op("dve", lambda e: e.tensor_tensor(
                    dst[64:128, :], dst[64:128, :], s_tr[64:128, :], ALU.add),
                    reads=(dst_key, "trs"), writes=(dst_key,))

            for ct in range(QH):
                h, g = ct // 4, ct % 4
                b0, b1 = qkv_coltile(ct)
                def ev(e, b0=b0, b1=b1):
                    e.activation(s_qkv[:, 0:512], pb[b0][:, 0:512], AF.Identity,
                                 scale=SCALE)
                    return e.activation(s_qkv[:, 512:1024], pb[b1][:, 0:512],
                                        AF.Identity, scale=SCALE)
                pr.op("act", ev, reads=(f"pb{b0}", f"pb{b1}"), writes=("qkv",))
                pr.dma("sync", o_q[h, :, g * TOK:(g + 1) * TOK], s_qkv[:],
                       reads=("qkv",))
                rope(s_qkv, "qkv", s_rope, "rope")
                pr.dma("sync", o_rq[h, :, g * TOK:(g + 1) * TOK], s_rope[:],
                       reads=("rope",))
                if g == 0:
                    pr.op("dve", lambda e: e.tensor_copy(s_qs1[:], s_qkv[:]),
                          reads=("qkv",), writes=("qs1",))
                elif g == 1:
                    pr.op("dve", lambda e: e.tensor_tensor(
                        s_qs1[:], s_qs1[:], s_qkv[:], ALU.add),
                        reads=("qkv", "qs1"), writes=("qs1",))
                elif g == 2:
                    pr.op("dve", lambda e: e.tensor_copy(s_qs2[:], s_qkv[:]),
                          reads=("qkv",), writes=("qs2",))
                else:
                    pr.op("dve", lambda e: e.tensor_tensor(
                        s_qs2[:], s_qs2[:], s_qkv[:], ALU.add),
                        reads=("qkv", "qs2"), writes=("qs2",))
                    pr.op("dve", lambda e: e.tensor_tensor(
                        s_qs1[:], s_qs1[:], s_qs2[:], ALU.add),
                        reads=("qs1", "qs2"), writes=("qs1",))
                    pr.op("dve", lambda e: e.tensor_scalar(
                        s_qs1[:], s_qs1[:], 0.25, None, ALU.mult),
                        reads=("qs1",), writes=("qs1",))
                    pr.dma("sync", o_qsum[h, :, :], s_qs1[:], reads=("qs1",))

            for ct in range(QH, QH + H):
                h = ct - QH
                b0, b1 = qkv_coltile(ct)
                def evk(e, b0=b0, b1=b1):
                    e.activation(s_ksc[:, 0:512], pb[b0][:, 0:512], AF.Identity)
                    return e.activation(s_ksc[:, 512:1024], pb[b1][:, 0:512],
                                        AF.Identity)
                pr.op("act", evk, reads=(f"pb{b0}", f"pb{b1}"), writes=("ksc",))
                rope(s_ksc, "ksc", s_rope, "rope")
                pr.dma("sync", o_rk[h, :, :], s_rope[:], reads=("rope",))
                for jt in range(NJ):
                    pr.op("pe", lambda e, jt=jt: e.matmul(
                        pb[7][:, 0:P], s_rope[:, jt * P:(jt + 1) * P], s_idb[:],
                        start=True, stop=True),
                        reads=("rope", "idb"), writes=("pb7",))
                    pr.op("dve", lambda e, jt=jt: e.tensor_copy(
                        s_tr[:, jt * P:(jt + 1) * P], pb[7][:, 0:P]),
                        reads=("pb7",), writes=("trs",))
                pr.dma("sync", ktokr[:, :, h * DH:(h + 1) * DH],
                       s_tr[:].rearrange("p (j c) -> p j c", j=NJ), reads=("trs",))
                for cb in range(CB):
                    off = h * TOK + cb * LBLK
                    pr.op("dve", lambda e, cb=cb, off=off, h=h: e.tensor_scalar(
                        s_cin[:, off:off + LBLK],
                        s_ksc[:].rearrange("p (blk c) -> p blk c", c=CB)[:, :, cb:cb + 1]
                        .rearrange("p b c -> p (b c)"),
                        s_kpos[:, h * CB + cb:h * CB + cb + 1], None, ALU.add),
                        reads=("ksc", "kpos"), writes=(("cin", 0, h),))

            for ct in range(QH + H, QH + 2 * H):
                h = ct - QH - H
                b0, b1 = qkv_coltile(ct)
                def evv(e, b0=b0, b1=b1):
                    e.activation(s_qkv[:, 0:512], pb[b0][:, 0:512], AF.Identity)
                    return e.activation(s_qkv[:, 512:1024], pb[b1][:, 0:512],
                                        AF.Identity)
                pr.op("act", evv, reads=(f"pb{b0}", f"pb{b1}"), writes=("qkv",))
                for jt in range(NJ):
                    pr.op("pe", lambda e, jt=jt: e.matmul(
                        pb[7][:, 0:P], s_qkv[:, jt * P:(jt + 1) * P], s_idb[:],
                        start=True, stop=True),
                        reads=("qkv", "idb"), writes=("pb7",))
                    pr.op("dve", lambda e, jt=jt: e.tensor_copy(
                        s_tr[:, jt * P:(jt + 1) * P], pb[7][:, 0:P]),
                        reads=("pb7",), writes=("trs",))
                pr.dma("sync", vtokr[:, :, h * DH:(h + 1) * DH],
                       s_tr[:].rearrange("p (j c) -> p j c", j=NJ), reads=("trs",))
                for cb in range(CB):
                    off = (H + h) * TOK + cb * LBLK
                    pr.op("dve", lambda e, cb=cb, off=off, h=h: e.tensor_scalar(
                        s_cin[:, off:off + LBLK],
                        s_qkv[:].rearrange("p (blk c) -> p blk c", c=CB)[:, :, cb:cb + 1]
                        .rearrange("p b c -> p (b c)"),
                        s_vpos[:, h * CB + cb:h * CB + cb + 1], None, ALU.add),
                        reads=("qkv", "vpos"), writes=(("cin", 1, h),))

            ckr = o_ck.rearrange("h b d -> b h d")
            cvr = o_cv.rearrange("h b d -> b h d")
            for side, w1, w2, bias1, bias2, outr in (
                    (0, kw1T, kw2T, s_kb1, s_kb2, ckr),
                    (1, vw1T, vw2T, s_vb1, s_vb2, cvr)):
                cink = tuple(("cin", side, h) for h in range(H))
                w1r = w1.rearrange("(c p) o -> p c o", p=P)
                base = side * H * TOK
                rhs_ap = lambda c, side=side: s_cin[:].rearrange(
                    "p (s h cb blk) -> p s h cb blk", s=2, h=H, cb=CB) \
                    [:, side, :, c, :]
                for o in range(16):
                    wb = s_w[o % 2]
                    pr.dma("gp", wb[:, 0:CDIM].rearrange("p (c o) -> p c o", o=P),
                           w1r[:, :, o * P:(o + 1) * P], writes=(f"wbuf{o % 2}",))
                    bank = 3 + (o % 2)
                    def m1(e, o=o, wb=wb, bank=bank, rhs_ap=rhs_ap):
                        for c in range(16):
                            i = e.matmul(pb[bank][:, 0:512],
                                         wb[:, c * P:(c + 1) * P], rhs_ap(c),
                                         start=(c == 0), stop=(c == 15))
                        return i
                    pr.op("pe", m1, reads=(f"wbuf{o % 2}",) + cink,
                          writes=(f"pb{bank}",))
                    pr.op("act", lambda e, o=o, bank=bank, bias1=bias1: e.activation(
                        s_h1[:, o * 512:(o + 1) * 512], pb[bank][:, 0:512],
                        AF.Relu, bias=bias1[:, o:o + 1], scale=1.0),
                        reads=(f"pb{bank}", "kb1" if side == 0 else "vb1"),
                        writes=("h1",))
                w2r = w2.rearrange("(c p) o -> p c o", p=P)
                pr.dma("gp", s_w[0][:, 0:16 * DH].rearrange("p (c o) -> p c o", o=DH),
                       w2r, writes=("wbuf0",))
                def m2(e):
                    for c in range(16):
                        i = e.matmul(pb[5][:, 0:512],
                                     s_w[0][:, c * DH:(c + 1) * DH],
                                     s_h1[:, c * 512:(c + 1) * 512],
                                     start=(c == 0), stop=(c == 15))
                    return i
                pr.op("pe", m2, reads=("wbuf0", "h1"), writes=("pb5",))
                pr.op("act", lambda e, bias2=bias2: e.activation(
                    s_co[:], pb[5][:, 0:512], AF.Identity,
                    bias=bias2[:, 0:1], scale=1.0),
                    reads=("pb5", "kb2" if side == 0 else "vb2"), writes=("co",))
                for h in range(H):
                    pr.op("pe", lambda e, h=h: e.matmul(
                        pb[7][0:64, 0:P], s_co[:, h * 64:(h + 1) * 64], s_idb[:],
                        start=True, stop=True),
                        reads=("co", "idb"), writes=("pb7",))
                    pr.op("dve", lambda e, h=h: e.tensor_copy(
                        s_tr[0:64, h * DH:(h + 1) * DH], pb[7][0:64, 0:P]),
                        reads=("pb7",), writes=("trs",))
                pr.dma("sync", outr,
                       s_tr[0:64, 0:H * DH].rearrange("p (h d) -> p h d", h=H),
                       reads=("trs",))

            def gmm(e):
                for k in range(NT):
                    e.matmul(pb[1][0:96, 0:512], s_wg[:, k * 96:(k + 1) * 96],
                             s_y16[:, k * TOK:k * TOK + 512],
                             start=(k == 0), stop=(k == NT - 1))
                for k in range(NT):
                    i = e.matmul(pb[2][0:96, 0:512], s_wg[:, k * 96:(k + 1) * 96],
                                 s_y16[:, k * TOK + 512:(k + 1) * TOK],
                                 start=(k == 0), stop=(k == NT - 1))
                return i
            pr.op("pe", gmm, reads=("wg",) + y16keys, writes=("pb1", "pb2"))
            def gev(e):
                e.activation(s_gt[:, 0:512], pb[1][0:96, 0:512], AF.Sigmoid,
                             bias=s_bg[:, 0:1], scale=1.0)
                return e.activation(s_gt[:, 512:1024], pb[2][0:96, 0:512],
                                    AF.Sigmoid, bias=s_bg[:, 0:1], scale=1.0)
            pr.op("act", gev, reads=("pb1", "pb2", "bg"), writes=("gt",))
            pr.dma("sync", o_gates, s_gt[:], reads=("gt",))

            pr.finalize(block)
    return nc


P12_OUTS = ["o_x1", "o_q", "o_rq", "o_qsum", "o_gates", "o_rk", "o_ktok",
            "o_vtok", "o_ck", "o_cv"]




def _prep_p34(inputs, res1, cores=tuple(range(8))):
    mem_kv = np.asarray(inputs["L0_attn_mem_kv"])
    w_o = np.asarray(inputs["L0_attn_w_o"])
    rw = np.asarray(inputs["L0_ffn_router_weight"]); rb = np.asarray(inputs["L0_ffn_router_bias"])
    eb = np.asarray(inputs["L0_ffn_expert_bias"])
    bf = ml_dtypes.bfloat16

    onehot = np.full((512, NBLK), NEG, dtype=bf)
    for jb in range(NBLK):
        onehot[jb + 1, jb] = 0.0
    E9 = np.zeros((9, P), dtype=bf)
    for kk in range(P):
        E9[kk // CB, kk] = 1.0
    E9[8, :] = 1.0
    e3 = np.zeros((P, 9), dtype=bf)
    for bb in range(3):
        e3[:, 3 * bb + bb] = 1.0
    e3w = np.zeros((3, 3 * P), dtype=bf)
    for bb in range(3):
        e3w[bb, bb * P:(bb + 1) * P] = 1.0
    gsel = np.zeros((96, 96), dtype=np.float32)
    for qh in range(QH):
        for bb in range(3):
            gsel[3 * qh + bb, qh * 3 + bb] = 1.0
    qloc = np.arange(P)
    wloc = np.arange(P)
    od = np.where((wloc[:, None] >= (qloc[None, :] // CB) * CB)
                  & (wloc[:, None] <= qloc[None, :]), 0.0, NEG).astype(np.float32)
    owndiag = np.tile(od, (1, 4)).reshape(P, 4 * P)
    owndiag = np.concatenate([od] * 4, axis=1).astype(bf)

    shared = dict(
        onehot=onehot, E9=E9, e3=e3, e3w=e3w, gsel=gsel, owndiag=owndiag,
        woT=np.ascontiguousarray(w_o.T),
        rwT=np.ascontiguousarray(rw.T), rbias=rb.reshape(EXP, 1).copy(), ebias=eb,
        ident=np.eye(P, dtype=np.float32), identb=np.eye(P, dtype=bf),
        memkP=np.ascontiguousarray(mem_kv[0, :, 0, :][:, _PERM].T),
        memv=np.ascontiguousarray(mem_kv[1, :, 0, :].reshape(1, H * DH)),
    )

    maps = {}
    for c in cores:
        b, r = c // 4, c % 4
        rkTf = np.zeros((H * DH, S), dtype=bf)
        vtok = np.zeros((S, H * DH), dtype=bf)
        ktok = np.zeros((S, H * DH), dtype=bf)
        ck_full = np.zeros((H, NBLK, DH), dtype=bf)
        cv_full = np.zeros((H, NBLK, DH), dtype=bf)
        for rr in range(4):
            rsrc = res1[4 * b + rr]
            pos = _own_positions(rr)
            rkTf[:, pos] = rsrc["o_rk"].reshape(H * DH, TOK)
            vtok[pos] = rsrc["o_vtok"]
            ktok[pos] = rsrc["o_ktok"]
            oblk = ((np.arange(NJ)[:, None] * 4 + rr) * NJ + np.arange(NJ)[None, :]).reshape(-1)
            ck_full[:, oblk] = rsrc["o_ck"]
            cv_full[:, oblk] = rsrc["o_cv"]
        ckTin = np.zeros((H * DH, 384), dtype=bf)
        for h in range(H):
            ckTin[h * DH:(h + 1) * DH, 1:NBLK + 1] = ck_full[h].T
        cvin = np.zeros((384, H * DH), dtype=bf)
        for h in range(H):
            cvin[1:NBLK + 1, h * DH:(h + 1) * DH] = cv_full[h]
        wk = np.zeros((NJ, 2 * P, H * DH), dtype=bf)
        wv = np.zeros((NJ, 2 * P, H * DH), dtype=bf)
        for j in range(NJ):
            T = 4 * j + r
            lo = T * P - P
            for w in range(2 * P):
                gw = lo + w
                if gw >= 0:
                    wk[j, w] = ktok[gw]
                    wv[j, w] = vtok[gw]
        cmask = np.zeros((NJ, 2, P, 4 * P), dtype=bf)
        impmask = np.zeros((NJ, P, 384), dtype=bf)
        swmask = np.zeros((NJ, 2, P, 4 * P), dtype=bf)
        for j in range(NJ):
            T = 4 * j + r
            qabs = T * P + np.arange(P)
            col = np.arange(384)
            ck_seq = np.where(col == 0, -1, 16 * col - 1)
            allow = (ck_seq[:, None] < qabs[None, :]) & (col[:, None] <= NBLK)
            cm = np.where(allow, 0.0, NEG).astype(np.float32)
            cmask[j] = np.tile(cm.reshape(3, P, 1, P), (1, 1, 4, 1)).reshape(3, P, 4 * P)[:2]
            col = np.arange(384)
            iallow = (col[None, :] >= 1) & (col[None, :] <= NBLK) \
                & (16 * col[None, :] - 1 < qabs[:, None])
            im = np.where(iallow, 0.0, NEG).astype(np.float32)
            im[:, NBLK + 1] = -1000.0
            impmask[j] = im
            kw = T * P - P + np.arange(2 * P)
            sallow = ((kw[:, None] >= qabs[None, :] - 64)
                      & (kw[:, None] <= qabs[None, :]) & (kw[:, None] >= 0))
            sm = np.where(sallow, 0.0, NEG).astype(np.float32)
            swmask[j] = np.tile(sm.reshape(2, P, 1, P), (1, 1, 4, 1)).reshape(2, P, 4 * P)

        rsrc = res1[c]
        m = dict(shared)
        m.update(
            x1T=rsrc["o_x1"], qT=rsrc["o_q"], rqT=rsrc["o_rq"],
            qsumT=rsrc["o_qsum"], gatesT=rsrc["o_gates"],
            rkTf=rkTf, vtokf=vtok, ckTin=ckTin, cvin=cvin,
            wk=np.ascontiguousarray(wk.reshape(NJ * 2 * P, H * DH)),
            wv=np.ascontiguousarray(wv.reshape(NJ * 2 * P, H * DH)),
            cmask=cmask, impmask=impmask, swmask=swmask,
        )
        maps[c] = m
    return maps


def _build_p34():
    nc = bass.Bass("TRN2", target_bir_lowering=False)
    d_ = lambda n, sh, dt=F32, k="ExternalInput": nc.dram_tensor(n, sh, dt, kind=k).ap()
    x1T = d_("x1T", [D, TOK])
    qT = d_("qT", [H, DH, G * TOK], BF16)
    rqT = d_("rqT", [H, DH, G * TOK], BF16)
    qsumT = d_("qsumT", [H, DH, TOK], BF16)
    gatesT = d_("gatesT", [96, TOK])
    rkTf = d_("rkTf", [H * DH, S], BF16)
    vtokf = d_("vtokf", [S, H * DH], BF16)
    ckTin = d_("ckTin", [H * DH, 384], BF16)
    cvin = d_("cvin", [384, H * DH], BF16)
    wk = d_("wk", [NJ * 2 * P, H * DH], BF16)
    wv = d_("wv", [NJ * 2 * P, H * DH], BF16)
    cmask = d_("cmask", [NJ, 2, P, 4 * P], BF16)
    impmask = d_("impmask", [NJ, P, 384], BF16)
    swmask = d_("swmask", [NJ, 2, P, 4 * P], BF16)
    owndiag = d_("owndiag", [P, 4 * P], BF16)
    onehot = d_("onehot", [512, NBLK], BF16)
    E9in = d_("E9", [9, P], BF16)
    e3in = d_("e3", [P, 9], BF16)
    e3w = d_("e3w", [3, 3 * P], BF16)
    gselin = d_("gsel", [96, 96])
    woT = d_("woT", [D, D])
    rwT = d_("rwT", [D, EXP]); rbias = d_("rbias", [EXP, 1]); ebias = d_("ebias", [EXP, D])
    ident = d_("ident", [P, P]); identb = d_("identb", [P, P], BF16)
    memkP = d_("memkP", [DH, H]); memv = d_("memv", [1, H * DH])

    attd = nc.dram_tensor("attd", [QH, DH, TOK], BF16, kind="Internal").ap()
    m01d = nc.dram_tensor("m01d", [2 * P, P], BF16, kind="Internal").ap()
    x2d = nc.dram_tensor("x2d", [D, TOK], F32, kind="Internal").ap()
    o_out = d_("o_out", [D, TOK], F32, "ExternalOutput")

    with ExitStack() as ctx:
        sb_ = lambda n, sh, dt=F32: ctx.enter_context(nc.sbuf_tensor(n, sh, dt))
        s_id = sb_("s_id", [P, P]); s_idb = sb_("s_idb", [P, P], BF16)
        s_ones1 = sb_("s_ones1", [1, P]); s_onesP = sb_("s_onesP", [P, 1])
        s_onesPb = sb_("s_onesPb", [P, 1], BF16)
        s_eps = sb_("s_eps", [1, 1])
        s_E9 = sb_("s_E9", [9, P], BF16)
        s_e3 = sb_("s_e3", [P, 9], BF16)
        s_e3w = sb_("s_e3w", [35, 3 * P], BF16)
        s_gsel = sb_("s_gsel", [96, 96])
        s_od = sb_("s_od", [P, 4 * P], BF16)
        s_gates = sb_("s_gates", [96, TOK])
        s_maxk = sb_("s_maxk", [1, 16])
        s_ones1b = sb_("s_ones1b", [1, P], BF16)
        s_rb = sb_("s_rb", [EXP, 1])

        pA = ctx.enter_context(nc.psum_tensor("pA", [P, 1024], F32))
        pB = ctx.enter_context(nc.psum_tensor("pB", [P, 1024], F32))
        pc = [ctx.enter_context(nc.psum_tensor(f"pc{i}", [P, 512], F32))
              for i in range(4)]
        pSc = lambda t: pA[:, (t % 2) * 512:(t % 2) * 512 + 512]
        pM = pB[:, 0:512]
        pL = pB[:, 512:1024]
        pLg = pB[32:35, 512:1024]
        pC, pF, pW, pI = pc[0], pc[1], pc[2], pc[3]

        pr = Prog(nc, ctx)
        block = ctx.enter_context(nc.Block())

        for dst, src, key in [
            (s_id[:], ident, "id"), (s_idb[:], identb, "idb"),
            (s_E9[:], E9in, "E9"), (s_e3[:], e3in, "e3"),
            (s_e3w[32:35, :], e3w, "e3w"), (s_gsel[:], gselin, "gsel"),
            (s_od[:], owndiag, "od"), (s_gates[:], gatesT, "gates"),
            (s_rb[:], rbias, "rb"),
        ]:
            pr.dma("sync", dst, src, writes=(key,))
        pr.op("dve", lambda e: e.memset(s_ones1[:], 1.0), writes=("ones1",))
        pr.op("dve", lambda e: e.memset(s_onesP[:], 1.0), writes=("onesP",))
        pr.op("dve", lambda e: e.memset(s_onesPb[:], 1.0), writes=("onesPb",))
        pr.op("dve", lambda e: e.memset(s_ones1b[:], 1.0), writes=("ones1b",))
        pr.op("dve", lambda e: e.memset(s_eps[:], EPS), writes=("eps",))

        with ExitStack() as c1:
            sb1 = lambda n, sh, dt=F32: c1.enter_context(nc.sbuf_tensor(n, sh, dt))
            s_rk = sb1("s_rk", [P, H * S], BF16)
            s_vt = sb1("s_vt", [P, NT * H * DH], BF16)
            s_ck = sb1("s_ck", [P, H * 384], BF16)
            s_cv = sb1("s_cv", [P, 3 * H * DH], BF16)
            s_sq = sb1("s_sq", [P, 512], BF16)
            s_qg = [sb1(f"s_qg{i}", [P, QCOLS], BF16) for i in range(2)]
            s_rqg = [sb1(f"s_rqg{i}", [P, QCOLS], BF16) for i in range(2)]
            s_qsj = sb1("s_qsj", [P, P], BF16)
            s_cm = sb1("s_cm", [P, 2 * 512], BF16)
            s_im = [sb1(f"s_im{i}", [P, 384], BF16) for i in range(2)]
            s_sm = sb1("s_sm", [P, 2 * 512], BF16)
            s_wk = sb1("s_wk", [P, H * DH], BF16)
            s_wv = sb1("s_wv", [P, 2 * H * DH], BF16)
            s_wkT = sb1("s_wkT", [P, H * 2 * DH], BF16)
            s_pc16 = [sb1(f"s_pc16_{i}", [P, 512], BF16) for i in range(2)]
            s_pf16 = [sb1(f"s_pf16_{i}", [P, 512], BF16) for i in range(2)]
            s_ps16 = sb1("s_ps16", [P, 2 * 512], BF16)
            s_po16 = sb1("s_po16", [P, 512], BF16)
            s_ei = sb1("s_ei", [P, 384])
            s_col = sb1("s_col", [P, 8])
            s_m8v = sb1("s_m8v", [P, 8])
            s_m8i = sb1("s_m8i", [P, 8], U32)
            s_sel = sb1("s_sel", [P, 4], U32)
            s_selv = sb1("s_selv", [P, 4], U32)
            s_mg = [sb1(f"s_mg{i}", [P, NBLK], BF16) for i in range(4)]
            s_mS = s_mg[0]
            s_m01T = s_mg[1]
            s_m01r = sb1("s_m01r", [8, NT * P], BF16)
            s_mb = [sb1(f"s_mb{i}", [9, 512], BF16) for i in range(2)]
            s_nrow = sb1("s_nrow", [1, 512])
            s_nmf = sb1("s_nmf", [1, 512])
            s_nmf16 = [sb1(f"s_nmf16_{i}", [1, 512], BF16) for i in range(2)]
            s_nmc16 = [sb1(f"s_nmc16_{i}", [1, 512], BF16) for i in range(2)]
            s_rl = sb1("s_rl", [35, 512])
            s_Wb = sb1("s_Wb", [35, 512], BF16)
            s_wrep = sb1("s_wrep", [P, 512], BF16)
            s_evc = sb1("s_evc", [P, 3 * 512], BF16)
            s_att = sb1("s_att", [P, 512], BF16)
            s_att16 = sb1("s_att16", [P, 512], BF16)
            s_row = sb1("s_row", [1, 8])

            for h in range(H):
                pr.dma("sync", s_rk[:, h * S:(h + 1) * S],
                       rkTf[h * DH:(h + 1) * DH, :], writes=(("rk", h),))
            pr.dma("sync", s_vt[:].rearrange("p (t c) -> p t c", c=H * DH),
                   vtokf.rearrange("(t p) c -> p t c", p=P), writes=("vt",))
            pr.dma("sync", s_ck[:].rearrange("p (h c) -> p h c", c=384),
                   ckTin.rearrange("(h p) c -> p h c", p=P), writes=("ckt",))
            pr.dma("sync", s_cv[:].rearrange("p (u c) -> p u c", c=H * DH),
                   cvin.rearrange("(u p) c -> p u c", p=P), writes=("cvt",))
            pr.dma("gp", s_ck[:].rearrange("p (h c) -> p h c", c=384)[:, :, 0:1],
                   memkP.rearrange("p (h o) -> p h o", o=1),
                   reads=("ckt",), writes=("ckt",),
                   allow_slow_non_contiguous=True)
            pr.dma("gp", s_cv[0:1, 0:H * DH], memv, reads=("cvt",), writes=("cvt",))

            pr.op("dve", lambda e: e.memset(s_maxk[:], 0.0), writes=("maxk",))
            def norm_pass(src_ap, width, colidx, key):
                for base in range(0, width, 512):
                    w = min(512, width - base)
                    pr.op("act", lambda e, base=base, w=w: e.activation(
                        s_sq[:, 0:w], src_ap[:, base:base + w], AF.Square),
                        reads=(key,), writes=("sq",))
                    for cb2 in range(0, w, 512):
                        cw = min(512, w - cb2)
                        pr.op("pe", lambda e, cb2=cb2, cw=cw: e.matmul(
                            pI[0:1, 0:cw], s_onesPb[:],
                            s_sq[:, cb2:cb2 + cw], start=True, stop=True),
                            reads=("sq", "onesPb"), writes=("pI",))
                        pr.op("dve", lambda e, cw=cw: e.tensor_reduce(
                            s_row[0:1, 0:1], pI[0:1, 0:cw], AX, ALU.max),
                            reads=("pI",), writes=("rowx",))
                        pr.op("dve", lambda e, colidx=colidx: e.tensor_tensor(
                            s_maxk[0:1, colidx:colidx + 1],
                            s_maxk[0:1, colidx:colidx + 1],
                            s_row[0:1, 0:1], ALU.max),
                            reads=("rowx", "maxk"), writes=("maxk",))
            for h in range(H):
                norm_pass(s_rk[:, h * S:(h + 1) * S], S, h, ("rk", h))
                norm_pass(s_ck[:, h * 384:(h + 1) * 384], 384, 8 + h, "ckt")

            units = [(j, h) for j in range(NJ) for h in range(H)]

            def j_boundary(j):
                pr.dma("gp", s_cm[:].rearrange("p (t c) -> p t c", c=512),
                       cmask[j].rearrange("t p c -> p t c"), writes=("cmj",))
                pr.dma("sync", s_sm[:].rearrange("p (t c) -> p t c", c=512),
                       swmask[j].rearrange("u p c -> p u c"), writes=("smj",))
                pr.dma("gp", s_wv[:].rearrange("p (u c) -> p u c", c=H * DH),
                       wv.rearrange("(j u p) c -> j p u c", j=NJ, p=P)[j],
                       writes=("wvbuf",))
                for u in range(2):
                    pr.dma("gp", s_wk[:],
                           wk.rearrange("(j u p) c -> j u p c", j=NJ, p=P)[j, u],
                           writes=("wkbuf",))
                    for h in range(H):
                        pr.op("pe", lambda e, h=h: e.matmul(
                            pM[:, 0:P], s_wk[:, h * DH:(h + 1) * DH],
                            s_idb[:], start=True, stop=True),
                            reads=("wkbuf", "idb"), writes=("pM",))
                        pr.op("dve", lambda e, h=h, u=u: e.tensor_copy(
                            s_wkT[:, (h * 2 + u) * DH:(h * 2 + u + 1) * DH],
                            pM[:, 0:P]), reads=("pM",), writes=("wkT",))

            def sel_stage(j, h):
                if h == 0:
                    pr.dma("sync", s_im[j % 2][:], impmask[j],
                           writes=(f"imj{j % 2}",))
                pr.dma("sync", s_qsj[:], qsumT[h, :, j * P:(j + 1) * P],
                       writes=("qsj",))
                pr.op("pe", lambda e, h=h: e.matmul(
                    pI[:, 0:384], s_qsj[:], s_ck[:, h * 384:(h + 1) * 384],
                    start=True, stop=True),
                    reads=("qsj", "ckt"), writes=("pI",))
                pr.op("dve", lambda e, j=j: e.tensor_tensor(
                    pI[:, 0:384], pI[:, 0:384], s_im[j % 2][:], ALU.add),
                    reads=(f"imj{j % 2}", "pI"), writes=("pI",))
                pr.op("dve", lambda e: e.tensor_reduce(
                    s_col[:, 0:1], pI[:, 0:384], AX, ALU.max, negate=True),
                    reads=("pI",), writes=("col",))
                pr.op("act", lambda e: e.activation(
                    s_ei[:], pI[:, 0:384], AF.Exp, bias=s_col[:, 0:1],
                    scale=1.0), reads=("pI", "col"), writes=("ei",))
                pr.op("dve", lambda e: e.tensor_reduce(
                    s_col[:, 1:2], s_ei[:], AX, ALU.add),
                    reads=("ei",), writes=("col",))
                pr.op("dve", lambda e: e.max(out=s_m8v[:], in_=s_ei[:]),
                      reads=("ei",), writes=("m8v",))
                pr.op("dve", lambda e: e.max_index(
                    out=s_m8i[:], in_max=s_m8v[:], in_values=s_ei[:]),
                    reads=("m8v", "ei"), writes=("m8i",))
                pr.op("dve", lambda e: e.tensor_scalar(
                    s_col[:, 2:3], s_col[:, 1:2], 1e-10, None, ALU.mult),
                    reads=("col",), writes=("col",))
                pr.op("dve", lambda e: e.tensor_scalar(
                    s_selv[:], s_m8v[:, 0:4], s_col[:, 2:3], None, ALU.is_gt),
                    reads=("m8v", "col"), writes=("selv",))
                pr.op("dve", lambda e: e.memset(s_sel[:], 0), writes=("sel",))
                pr.op("dve", lambda e: e.copy_predicated(
                    s_sel[:], s_selv[:], s_m8i[:, 0:4]),
                    reads=("selv", "m8i", "sel"), writes=("sel",))
                for ss in range(4):
                    pr.qop("gp", lambda e, ss=ss: e.indirect_dma_start(
                        s_mg[ss][:], None, onehot,
                        IndirectOffsetOnAxis(s_sel[:, ss:ss + 1], 0)),
                        reads=("sel",), writes=(f"mg{ss}",))

            def sel_stage_b():
                pr.op("dve", lambda e: e.tensor_tensor(
                    s_mS[:], s_mg[0][:], s_mg[1][:], ALU.max),
                    reads=("mg0", "mg1"), writes=("mg0",))
                pr.op("dve", lambda e: e.tensor_tensor(
                    s_mS[:], s_mS[:], s_mg[2][:], ALU.max),
                    reads=("mg0", "mg2"), writes=("mg0",))
                pr.op("dve", lambda e: e.tensor_tensor(
                    s_mS[:], s_mS[:], s_mg[3][:], ALU.max),
                    reads=("mg0", "mg3"), writes=("mg0",))
                for u in range(2):
                    pr.op("pe", lambda e, u=u: e.matmul(
                        pI[:, 0:P], s_mS[:, u * P:(u + 1) * P], s_idb[:],
                        start=True, stop=True),
                        reads=("mg0", "idb"), writes=("pI",))
                    pr.op("dve", lambda e, u=u: e.tensor_copy(
                        s_m01T[:, u * P:(u + 1) * P], pI[:, 0:P]),
                        reads=("pI",), writes=("mg1",))
                pr.dma("sync", m01d.rearrange("(u p) q -> p u q", p=P),
                       s_m01T[:].rearrange("p (u q) -> p u q", q=P),
                       reads=("mg1",), writes=("m01d",))

            def m01r_load():
                pr.dma("sync",
                       s_m01r[0:8, :].rearrange("r (t q) -> r t q", q=P),
                       m01d.rearrange("(t r) q -> r t q", r=8),
                       reads=("m01d",), writes=("m01r",))

            def load_negm(ui):
                jn, hn = units[ui]
                sl = ui % 2
                pr.dma("sync", s_qg[sl][:].rearrange("p (g q) -> p g q", q=P),
                       qT[hn].rearrange("p (g t) -> p g t", t=TOK)
                       [:, :, jn * P:(jn + 1) * P], writes=(f"qg{sl}",))
                pr.dma("sync", s_rqg[sl][:].rearrange("p (g q) -> p g q", q=P),
                       rqT[hn].rearrange("p (g t) -> p g t", t=TOK)
                       [:, :, jn * P:(jn + 1) * P], writes=(f"rqg{sl}",))

                def negm_row(src, srckey, col, dst16, dstkey):
                    pr.op("dve", lambda e, src=src: e.tensor_tensor(
                        s_sq[:, 0:512], src, src, ALU.mult),
                        reads=(srckey,), writes=("sq",))
                    pr.op("pe", lambda e: e.matmul(
                        pM[0:1, 0:512], s_onesPb[:], s_sq[:, 0:512],
                        start=True, stop=True, skip_group_check=True),
                        reads=("sq", "onesPb"), writes=("pM",))
                    pr.op("dve", lambda e, col=col: e.tensor_scalar(
                        s_nrow[:], pM[0:1, 0:512],
                        s_maxk[0:1, col:col + 1], None, ALU.mult),
                        reads=("pM", "maxk"), writes=("nrow",))
                    pr.op("act", lambda e: e.activation(
                        s_nrow[:], s_nrow[:], AF.Sqrt),
                        reads=("nrow",), writes=("nrow",))
                    pr.op("dve", lambda e: e.tensor_scalar(
                        s_nmf[:], s_nrow[:], -1.0001, None, ALU.mult),
                        reads=("nrow",), writes=("nmf",))
                    pr.op("dve", lambda e, dst16=dst16: e.tensor_copy(
                        dst16, s_nmf[:]), reads=("nmf",), writes=(dstkey,))
                negm_row(s_rqg[sl][:], f"rqg{sl}", hn,
                         s_nmf16[sl][:], f"nmf16_{sl}")
                negm_row(s_qg[sl][:], f"qg{sl}", 8 + hn,
                         s_nmc16[sl][:], f"nmc16_{sl}")

            def main_stage(idx, j, h):
                NTD = 4 * j + 4
                sl = idx % 2
                for i in range(2):
                    pr.dma("sync", s_mb[i][8:9, :], s_nmf16[sl][:],
                           reads=(f"nmf16_{sl}",), writes=(f"mb{i}",))

                lfirst = [True]
                pend = []

                def flush_pend():
                    while pend:
                        pend.pop(0)()

                def lpv(probkey, prob_ap, e3c, pvb, pvkey, vkey, v_ap,
                        lstop=False, pvstart=False, pvstop=False):
                    def emit(probkey=probkey, prob_ap=prob_ap, e3c=e3c,
                             pvb=pvb, pvkey=pvkey, vkey=vkey, v_ap=v_ap,
                             lstop=lstop, pvstart=pvstart, pvstop=pvstop):
                        st = lfirst[0]
                        lfirst[0] = False
                        def mm(e, st=st):
                            e.matmul(pL[0:3, :], s_e3[:, e3c:e3c + 3],
                                     prob_ap, start=st, stop=lstop,
                                     skip_group_check=True)
                            return e.matmul(pvb, v_ap, prob_ap,
                                            start=pvstart, stop=pvstop)
                        pr.op("pe", mm, reads=(probkey, "e3", vkey),
                              writes=("pL", pvkey))
                    return emit

                CT = 1 if j <= 3 else 2
                for t in range(CT):
                    def cmm(e, t=t, h=h):
                        e.matmul(pSc(t),
                                 s_ck[:, h * 384 + t * P:h * 384 + (t + 1) * P],
                                 s_qg[sl][:], start=True, stop=False)
                        return e.matmul(pSc(t), s_ones1b[:], s_nmc16[sl][:],
                                        start=False, stop=True)
                    pr.op("pe", cmm,
                          reads=("ckt", f"qg{sl}", "ones1b", f"nmc16_{sl}"),
                          writes=(f"pS{t % 2}",))
                    pr.op("dve", lambda e, t=t: e.tensor_tensor(
                        pSc(t), pSc(t), s_cm[:, t * 512:(t + 1) * 512], ALU.add),
                        reads=("cmj", f"pS{t % 2}"), writes=(f"pS{t % 2}",))
                    pr.op("act", lambda e, t=t: e.activation(
                        s_pc16[t % 2][:], pSc(t), AF.Exp),
                        reads=(f"pS{t % 2}",), writes=(f"pc{t % 2}",))
                    flush_pend()
                    pend.append(lpv(
                        f"pc{t % 2}", s_pc16[t % 2][:], 0, pC[:, :], "pC",
                        "cvt", s_cv[:, t * H * DH + h * DH:
                                    t * H * DH + (h + 1) * DH],
                        pvstart=(t == 0), pvstop=(t == CT - 1)))

                if idx + 1 < len(units):
                    sel_stage(*units[idx + 1])

                for t in range(NTD):
                    pr.op("pe", lambda e, t=t, h=h: e.matmul(
                        pSc(t), s_rk[:, h * S + t * P:h * S + (t + 1) * P],
                        s_rqg[sl][:], start=True, stop=False),
                        reads=(("rk", h), f"rqg{sl}"), writes=(f"pS{t % 2}",))
                    pr.op("dve", lambda e, t=t: e.tensor_copy(
                        s_mb[t % 2][0:8, :].rearrange("r (g q) -> r g q", q=P),
                        s_m01r[0:8, :].rearrange("r (t q) -> r t q", q=P)
                        [:, t:t + 1, :].to_broadcast([8, 4, P])),
                        reads=("m01r",), writes=(f"mb{t % 2}",))
                    pr.op("pe", lambda e, t=t: e.matmul(
                        pSc(t), s_E9[:], s_mb[t % 2][:],
                        start=False, stop=True),
                        reads=(f"mb{t % 2}", "E9"), writes=(f"pS{t % 2}",))
                    pr.op("act", lambda e, t=t: e.activation(
                        s_pf16[t % 2][:], pSc(t), AF.Exp),
                        reads=(f"pS{t % 2}",), writes=(f"pf{t % 2}",))
                    flush_pend()
                    pend.append(lpv(
                        f"pf{t % 2}", s_pf16[t % 2][:], 3, pF[:, :], "pF",
                        "vt", s_vt[:, t * H * DH + h * DH:
                                   t * H * DH + (h + 1) * DH],
                        pvstart=(t == 0)))
                if epi1_pend:
                    epi1_pend.pop(0)()
                if idx + 1 < len(units):
                    load_negm(idx + 1)
                def omm(e, h=h):
                    e.matmul(pSc(0),
                             s_wkT[:, (h * 2 + 1) * DH:(h * 2 + 2) * DH],
                             s_rqg[sl][:], start=True, stop=False)
                    return e.matmul(pSc(0), s_ones1b[:], s_nmf16[sl][:],
                                    start=False, stop=True)
                pr.op("pe", omm,
                      reads=("wkT", f"rqg{sl}", "ones1b", f"nmf16_{sl}"),
                      writes=("pS0",))
                pr.op("dve", lambda e: e.tensor_tensor(
                    pSc(0), pSc(0), s_od[:], ALU.add),
                    reads=("od", "pS0"), writes=("pS0",))
                pr.op("act", lambda e: e.activation(
                    s_po16[:], pSc(0), AF.Exp),
                    reads=("pS0",), writes=("po16",))
                flush_pend()
                pend.append(lpv(
                    "po16", s_po16[:], 3, pF[:, :], "pF",
                    "wvbuf", s_wv[:, H * DH + h * DH:H * DH + (h + 1) * DH],
                    pvstop=True))
                for u in range(2):
                    def smm(e, u=u, h=h):
                        e.matmul(pSc(u),
                                 s_wkT[:, (h * 2 + u) * DH:(h * 2 + u + 1) * DH],
                                 s_rqg[sl][:], start=True, stop=False)
                        return e.matmul(pSc(u), s_ones1b[:], s_nmf16[sl][:],
                                        start=False, stop=True)
                    pr.op("pe", smm,
                          reads=("wkT", f"rqg{sl}", "ones1b", f"nmf16_{sl}"),
                          writes=(f"pS{u}",))
                    pr.op("dve", lambda e, u=u: e.tensor_tensor(
                        pSc(u), pSc(u), s_sm[:, u * 512:(u + 1) * 512], ALU.add),
                        reads=("smj", f"pS{u}"), writes=(f"pS{u}",))
                    pr.op("act", lambda e, u=u: e.activation(
                        s_ps16[:, u * 512:(u + 1) * 512], pSc(u), AF.Exp),
                        reads=(f"pS{u}",), writes=("ps16",))
                    flush_pend()
                    pend.append(lpv(
                        "ps16", s_ps16[:, u * 512:(u + 1) * 512], 6,
                        pW[:, :], "pW",
                        "wvbuf", s_wv[:, u * H * DH + h * DH:
                                      u * H * DH + (h + 1) * DH],
                        lstop=(u == 1), pvstart=(u == 0), pvstop=(u == 1)))
                if idx + 1 < len(units):
                    sel_stage_b()
                    m01r_load()
                if epi2_pend:
                    epi2_pend.pop(0)()
                for g in range(G):
                    pr.op("pe", lambda e, g=g, h=h, j=j: e.matmul(
                        pLg[:, g * P:(g + 1) * P],
                        s_gsel[:, (h * G + g) * 3:(h * G + g) * 3 + 3],
                        s_gates[:, j * P:(j + 1) * P],
                        start=(g == 0), stop=(g == 3),
                        skip_group_check=True),
                        reads=("gsel", "gates"), writes=("pLg",))
                flush_pend()
                pr.op("dve", lambda e: e.tensor_scalar(
                    s_rl[32:35, :], pL[0:3, :], 1e-37, None, ALU.add),
                    reads=("pL",), writes=("rl",))
                pr.op("act", lambda e: e.activation(
                    s_evc[:, 0:512], pC[:, :], AF.Identity),
                    reads=("pC",), writes=("evc",))
                pr.op("act", lambda e: e.activation(
                    s_evc[:, 512:1024], pF[:, :], AF.Identity),
                    reads=("pF",), writes=("evc",))
                pr.op("act", lambda e: e.activation(
                    s_evc[:, 1024:1536], pW[:, :], AF.Identity),
                    reads=("pW",), writes=("evc",))

                def epi1():
                    pr.op("dve", lambda e: e.reciprocal(
                        s_rl[32:35, :], s_rl[32:35, :]),
                        reads=("rl",), writes=("rl",))
                    pr.op("dve", lambda e: e.tensor_tensor(
                        s_Wb[32:35, :], pLg[:, 0:512], s_rl[32:35, :],
                        ALU.mult),
                        reads=("pLg", "rl"), writes=("Wb",))
                epi1_pend.append(epi1)

                def epi2(j=j, h=h):
                    for bb, bkey, src_ap in (
                            (0, "evc", s_evc[:, 0:512]),
                            (1, "evc", s_evc[:, 512:1024]),
                            (2, "evc", s_evc[:, 1024:1536])):
                        pr.op("pe", lambda e, bb=bb: e.matmul(
                            pI[:, 0:512], s_e3w[32:35, bb * P:(bb + 1) * P],
                            s_Wb[32:35, :],
                            start=True, stop=True, skip_group_check=True),
                            reads=("Wb", "e3w"), writes=("pI",))
                        pr.op("act", lambda e: e.activation(
                            s_wrep[:], pI[:, 0:512], AF.Identity),
                            reads=("pI",), writes=("wrep",))
                        if bb == 0:
                            pr.op("dve", lambda e, src_ap=src_ap:
                                  e.tensor_tensor(
                                      s_att[:], src_ap, s_wrep[:], ALU.mult),
                                  reads=(bkey, "wrep"), writes=("att",))
                        else:
                            pr.op("dve", lambda e, src_ap=src_ap:
                                  e.scalar_tensor_tensor(
                                      s_sq[:, 0:512], src_ap, 1.0, s_wrep[:],
                                      ALU.mult, ALU.mult),
                                  reads=(bkey, "wrep"), writes=("sq",))
                            pr.op("dve", lambda e: e.tensor_tensor(
                                s_att[:], s_att[:], s_sq[:, 0:512], ALU.add),
                                reads=("att", "sq"), writes=("att",))
                    pr.op("act", lambda e: e.activation(
                        s_att16[:], s_att[:], AF.Identity),
                        reads=("att",), writes=("att16",))
                    pr.dma("sync",
                           attd[4 * h:4 * h + 4, :, j * P:(j + 1) * P]
                           .rearrange("g p q -> p g q"),
                           s_att16[:].rearrange("p (g q) -> p g q", q=P),
                           reads=("att16",), writes=(("attd", h, j),))
                epi2_pend.append(epi2)

            epi1_pend = []
            epi2_pend = []
            load_negm(0)
            sel_stage(0, 0)
            sel_stage_b()
            m01r_load()
            for idx, (j, h) in enumerate(units):
                if h == 0:
                    j_boundary(j)
                main_stage(idx, j, h)
            while epi1_pend:
                epi1_pend.pop(0)()
            while epi2_pend:
                epi2_pend.pop(0)()
            pr.barrier()

        with ExitStack() as c2:
            sb2 = lambda n, sh, dt=F32: c2.enter_context(nc.sbuf_tensor(n, sh, dt))
            s_attf = sb2("s_attf", [P, QH * TOK], BF16)
            s_wo = [sb2(f"s_wo{i}", [P, NT * P], BF16) for i in range(2)]
            s_x1c = [sb2(f"s_x1c{i}", [P, TOK]) for i in range(2)]
            s_x2c = sb2("s_x2c", [P, TOK])
            pr.dma("sync", s_attf[:].rearrange("p (qh t) -> p qh t", t=TOK),
                   attd.rearrange("qh p t -> p qh t"), writes=("attf",),
                   reads=tuple(("attd", h, j) for h in range(H) for j in range(NJ)))
            wor = woT.rearrange("(k p) c -> p k c", p=P)
            x1r2 = x1T.rearrange("(k p) t -> p k t", p=P)
            x2r = x2d.rearrange("(k p) t -> p k t", p=P)
            for cc in range(NT):
                wb = s_wo[cc % 2]
                pr.dma("gp", wb[:].rearrange("p (k c) -> p k c", c=P),
                       wor[:, :, cc * P:(cc + 1) * P], writes=(f"wo{cc % 2}",))
                pr.dma("sync", s_x1c[cc % 2][:], x1r2[:, cc, :],
                       writes=(f"x1c{cc % 2}",))
                pbank = pA if cc % 2 == 0 else pB
                def womm(e, wb=wb, pbank=pbank):
                    for k in range(NT):
                        e.matmul(pbank[:, 0:512], wb[:, k * P:(k + 1) * P],
                                 s_attf[:, k * TOK:k * TOK + 512],
                                 start=(k == 0), stop=(k == NT - 1))
                    for k in range(NT):
                        i = e.matmul(pbank[:, 512:1024], wb[:, k * P:(k + 1) * P],
                                     s_attf[:, k * TOK + 512:(k + 1) * TOK],
                                     start=(k == 0), stop=(k == NT - 1))
                    return i
                pr.op("pe", womm, reads=(f"wo{cc % 2}", "attf"),
                      writes=(f"pAB{cc % 2}",))
                pr.op("dve", lambda e, cc=cc, pbank=pbank: e.tensor_tensor(
                    s_x2c[:], s_x1c[cc % 2][:], pbank[:, :], ALU.add),
                    reads=(f"pAB{cc % 2}", f"x1c{cc % 2}"), writes=("x2c",))
                pr.dma("sync", x2r[:, cc, :], s_x2c[:], reads=("x2c",), writes=(("x2d", cc),))
            pr.barrier()

        with ExitStack() as c3:
            sb3 = lambda n, sh, dt=F32: c3.enter_context(nc.sbuf_tensor(n, sh, dt))
            s_x = sb3("s_x", [P, D])
            s_sq3 = sb3("s_sq3", [P, D])
            s_y32 = sb3("s_y32", [P, D])
            s_y16r = sb3("s_y16r", [P, D], BF16)
            s_x3 = sb3("s_x3", [P, D])
            s_stat = sb3("s_stat", [1, 512])
            s_rep = sb3("s_rep", [P, 1, 2 * P])
            s_lg = sb3("s_lg", [EXP, P])
            s_e = sb3("s_e", [P, EXP])
            s_col3 = sb3("s_col3", [P, 8])
            s_m8 = sb3("s_m8", [P, 8])
            s_zap = sb3("s_zap", [P, EXP])
            s_pm = sb3("s_pm", [P, EXP])
            s_pms = sb3("s_pms", [P, EXP], BF16)
            s_ws = sb3("s_ws", [P, 1])
            s_pmT = sb3("s_pmT", [EXP, P], BF16)
            s_wsrow = sb3("s_wsrow", [1, P])
            s_wsr = sb3("s_wsr", [P, 1, P])
            s_eb = sb3("s_eb", [EXP, D], BF16)
            s_rw = sb3("s_rw", [P, NT * EXP], BF16)
            pr.dma("gp", s_eb[:], ebias, writes=("eb",))
            pr.dma("gp", s_rw[:].rearrange("p (k e) -> p k e", e=EXP),
                   rwT.rearrange("(k p) e -> p k e", p=P), writes=("rw",))
            pb0, pb1, pb2, pb7 = pc[0], pc[1], pc[2], pc[3]
            x2rr = x2d.rearrange("(k p) t -> p k t", p=P)
            outr = o_out.rearrange("(k p) t -> p k t", p=P)

            for j in range(NJ):
                pr.dma("sync", s_x[:].rearrange("p (k t) -> p k t", t=P),
                       x2rr[:, :, j * P:(j + 1) * P], writes=("x",),
                       reads=tuple(("x2d", cc) for cc in range(NT)))
                pr.op("act", lambda e: e.activation(s_sq3[:], s_x[:], AF.Square),
                      reads=("x",), writes=("sq3",))
                def sums(e):
                    for k in range(NT):
                        i = e.matmul(pb0[0:1, 0:P], s_onesP[:],
                                     s_x[:, k * P:(k + 1) * P],
                                     start=(k == 0), stop=(k == NT - 1))
                    return i
                pr.op("pe", sums, reads=("x", "onesP"), writes=("pb0",))
                def sqsums(e):
                    for k in range(NT):
                        i = e.matmul(pb0[0:1, P:2 * P], s_onesP[:],
                                     s_sq3[:, k * P:(k + 1) * P],
                                     start=(k == 0), stop=(k == NT - 1))
                    return i
                pr.op("pe", sqsums, reads=("sq3", "onesP"), writes=("pb0",))
                pr.op("dve", lambda e: e.tensor_scalar(
                    s_stat[0:1, 0:P], pb0[0:1, 0:P], 1.0 / D, None, ALU.mult),
                    reads=("pb0",), writes=("stat",))
                pr.op("dve", lambda e: e.tensor_tensor(
                    s_stat[0:1, P:2 * P], s_stat[0:1, 0:P], s_stat[0:1, 0:P],
                    ALU.mult), reads=("stat",), writes=("stat",))
                pr.op("dve", lambda e: e.scalar_tensor_tensor(
                    s_stat[0:1, 2 * P:3 * P], pb0[0:1, P:2 * P], 1.0 / D,
                    s_stat[0:1, P:2 * P], ALU.mult, ALU.subtract),
                    reads=("pb0", "stat"), writes=("stat",))
                pr.op("act", lambda e: e.activation(
                    s_stat[0:1, 2 * P:3 * P], s_stat[0:1, 2 * P:3 * P], AF.Sqrt,
                    bias=s_eps[0:1, 0:1], scale=1.0),
                    reads=("stat", "eps"), writes=("stat",))
                pr.op("dve", lambda e: e.reciprocal(
                    s_stat[0:1, 3 * P:4 * P], s_stat[0:1, 2 * P:3 * P]),
                    reads=("stat",), writes=("stat",))
                def bcast(e):
                    e.matmul(pb0[:, 0:P], s_ones1[:], s_stat[0:1, 0:P],
                             start=True, stop=True)
                    return e.matmul(pb0[:, P:2 * P], s_ones1[:],
                                    s_stat[0:1, 3 * P:4 * P], start=True, stop=True)
                pr.op("pe", bcast, reads=("stat", "ones1"), writes=("pb0",))
                pr.op("dve", lambda e: e.tensor_copy(s_rep[:, 0, :], pb0[:, 0:2 * P]),
                      reads=("pb0",), writes=("rep",))
                pr.op("dve", lambda e: e.tensor_tensor(
                    s_y32[:].rearrange("p (k t) -> p k t", k=NT),
                    s_x[:].rearrange("p (k t) -> p k t", k=NT),
                    s_rep[:, :, 0:P].to_broadcast([P, NT, P]), ALU.subtract),
                    reads=("x", "rep"), writes=("y32",))
                pr.op("dve", lambda e: e.tensor_tensor(
                    s_y32[:].rearrange("p (k t) -> p k t", k=NT),
                    s_y32[:].rearrange("p (k t) -> p k t", k=NT),
                    s_rep[:, :, P:2 * P].to_broadcast([P, NT, P]), ALU.mult),
                    reads=("rep",), writes=("y32",))
                pr.op("dve", lambda e: e.tensor_copy(s_y16r[:], s_y32[:]),
                      reads=("y32",), writes=("y16r",))
                def logits(e):
                    for k in range(NT):
                        i = e.matmul(pb0[0:EXP, 2 * P:3 * P],
                                     s_rw[:, k * EXP:(k + 1) * EXP],
                                     s_y16r[:, k * P:(k + 1) * P],
                                     start=(k == 0), stop=(k == NT - 1))
                    return i
                pr.op("pe", logits, reads=("y16r", "rw"), writes=("pb0",))
                pr.op("act", lambda e: e.activation(
                    s_lg[:], pb0[0:EXP, 2 * P:3 * P], AF.Identity,
                    bias=s_rb[:, 0:1], scale=1.0),
                    reads=("pb0", "rb"), writes=("lg",))
                pr.op("pe", lambda e: e.matmul(
                    pb0[0:P, 3 * P:3 * P + EXP], s_lg[:], s_id[0:EXP, 0:EXP],
                    start=True, stop=True), reads=("lg", "id"), writes=("pb0",))
                pr.op("dve", lambda e: e.tensor_reduce(
                    s_col3[:, 0:1], pb0[0:P, 3 * P:3 * P + EXP], AX, ALU.max,
                    negate=True), reads=("pb0",), writes=("col3",))
                pr.op("act", lambda e: e.activation(
                    s_e[:], pb0[0:P, 3 * P:3 * P + EXP], AF.Exp,
                    bias=s_col3[:, 0:1], scale=1.0),
                    reads=("pb0", "col3"), writes=("e",))
                pr.op("dve", lambda e: e.tensor_reduce(
                    s_col3[:, 1:2], s_e[:], AX, ALU.add), reads=("e",), writes=("col3",))
                pr.op("dve", lambda e: e.max(out=s_m8[:], in_=s_e[:]),
                      reads=("e",), writes=("m8",))
                pr.op("dve", lambda e: e.match_replace(
                    out=s_zap[:], in_to_replace=s_m8[:], in_values=s_e[:],
                    imm_value=0.0), reads=("m8", "e"), writes=("zap", "m8"))
                pr.op("dve", lambda e: e.tensor_tensor(
                    s_pm[:], s_e[:], s_zap[:], ALU.subtract),
                    reads=("e", "zap"), writes=("pm",))
                pr.op("dve", lambda e: e.tensor_reduce(
                    s_col3[:, 2:3], s_pm[:], AX, ALU.add), reads=("pm",), writes=("col3",))
                pr.op("dve", lambda e: e.reciprocal(s_col3[:, 3:4], s_col3[:, 1:2]),
                      reads=("col3",), writes=("col3",))
                pr.op("dve", lambda e: e.scalar_tensor_tensor(
                    s_ws[:], s_col3[:, 2:3], 0.5, s_col3[:, 3:4], ALU.mult, ALU.mult),
                    reads=("col3",), writes=("ws",))
                pr.op("dve", lambda e: e.tensor_scalar(
                    s_col3[:, 4:5], s_col3[:, 3:4], 0.5, None, ALU.mult),
                    reads=("col3",), writes=("col3",))
                pr.op("dve", lambda e: e.tensor_scalar(
                    s_pms[:], s_pm[:], s_col3[:, 4:5], None, ALU.mult),
                    reads=("pm", "col3"), writes=("pms",))
                pr.op("pe", lambda e: e.matmul(
                    pb7[0:EXP, 0:P], s_pms[:], s_idb[:], start=True, stop=True),
                    reads=("pms", "idb"), writes=("pb7",))
                pr.op("dve", lambda e: e.tensor_copy(s_pmT[:], pb7[0:EXP, 0:P]),
                      reads=("pb7",), writes=("pmT",))
                pr.op("pe", lambda e: e.matmul(
                    pb7[0:1, 0:P], s_ws[:], s_id[:], start=True, stop=True),
                    reads=("ws", "id"), writes=("pb7",))
                pr.op("dve", lambda e: e.tensor_copy(s_wsrow[:], pb7[0:1, 0:P]),
                      reads=("pb7",), writes=("wsrow",))
                pr.op("pe", lambda e: e.matmul(
                    pb0[:, 0:P], s_ones1[:], s_wsrow[:], start=True, stop=True),
                    reads=("wsrow", "ones1"), writes=("pb0",))
                pr.op("dve", lambda e: e.tensor_copy(s_wsr[:, 0, :], pb0[:, 0:P]),
                      reads=("pb0",), writes=("wsr",))
                pr.op("dve", lambda e: e.tensor_tensor(
                    s_sq3[:].rearrange("p (k t) -> p k t", k=NT),
                    s_y32[:].rearrange("p (k t) -> p k t", k=NT),
                    s_wsr[:].to_broadcast([P, NT, P]), ALU.mult),
                    reads=("y32", "wsr"), writes=("sq3",))
                pr.op("dve", lambda e: e.tensor_tensor(
                    s_x3[:], s_x[:], s_sq3[:], ALU.add),
                    reads=("x", "sq3"), writes=("x3",))
                for k in range(NT):
                    bank = pb1 if k % 2 == 0 else pb2
                    bkey = "pb1" if k % 2 == 0 else "pb2"
                    pr.op("pe", lambda e, k=k, bank=bank: e.matmul(
                        bank[0:P, 0:P], s_eb[:, k * P:(k + 1) * P], s_pmT[:],
                        start=True, stop=True),
                        reads=("eb", "pmT"), writes=(bkey,))
                    pr.op("dve", lambda e, k=k, bank=bank: e.tensor_tensor(
                        s_x3[:, k * P:(k + 1) * P], s_x3[:, k * P:(k + 1) * P],
                        bank[0:P, 0:P], ALU.add),
                        reads=(bkey, "x3"), writes=("x3",))
                pr.dma("sync", outr[:, :, j * P:(j + 1) * P],
                       s_x3[:].rearrange("p (k t) -> p k t", t=P), reads=("x3",))
            pr.finalize(block)
    return nc


P34_OUTS = ["o_out"]


def kernel(**inputs):
    maps1 = _prep_p12(inputs)
    nc1 = _build_p12()
    res1 = run_bass_kernel_spmd(nc1, maps1, core_ids=list(range(8))).results
    maps2 = _prep_p34(inputs, res1)
    nc2 = _build_p34()
    res2 = run_bass_kernel_spmd(nc2, [maps2[c] for c in range(8)],
                                core_ids=list(range(8))).results
    out = np.zeros((2, S, D), dtype=np.float32)
    for c in range(8):
        b, r = c // 4, c % 4
        out[b, _own_positions(r)] = res2[c]["o_out"].T
    return out
